# version 01
# baseline (speedup 1.0000x reference)
import numpy as np
from contextlib import ExitStack
import concourse.bass as bass
import concourse.mybir as mybir
from concourse.bass_utils import run_bass_kernel_spmd

F32 = mybir.dt.float32
BF16 = mybir.dt.bfloat16
I32 = mybir.dt.int32
ACT = mybir.ActivationFunctionType
ALU = mybir.AluOpType
AXX = mybir.AxisListType.X

ND = 8
NTB = 4
TKB = 512
NTILE = 16
NER = 16
NT = 2048
CSHIFT = 20.0
ENGINES = ("sync", "gpsimd", "tensor", "vector", "scalar")

C_NCSH, C_SC, C_HALF, C_ONE, C_EPS, C_INV1024, C_NHALF, C_3HALF, C_BIGNEG, C_ZERO, C_NONE = range(11)
CONSTS = [-CSHIFT, 0.125, 0.5, 1.0, 1e-5, 1.0 / 1024.0, -0.5, 1.5, -1e30, 0.0, -1.0]



class Sem:
    __slots__ = ("h", "n", "name")

    def __init__(self, st, nc, name):
        self.h = st.enter_context(nc.semaphore(name))
        self.n = 0
        self.name = name


class P:
    def __init__(self, nc):
        self.nc = nc
        self.ops = {e: [] for e in ENGINES}
        self.waitlog = []

    def check(self):
        bad = []
        for eng, sem, val in self.waitlog:
            if val > sem.n:
                bad.append((eng, sem.name, val, sem.n))
        return bad

    def add(self, eng, fn, sem=None, k=1):
        if sem is None:
            self.ops[eng].append(fn)
        else:
            sem.n += k

            def wrapped(e, fn=fn, sem=sem, k=k):
                fn(e).then_inc(sem.h, k)
            self.ops[eng].append(wrapped)

    def wait(self, eng, sem, val):
        if val <= 0:
            return
        self.waitlog.append((eng, sem, val))
        self.ops[eng].append(lambda e, h=sem.h, v=val: e.wait_ge(h, v))

    def emit(self):
        with self.nc.Block() as block:
            for eng in ENGINES:
                ops = self.ops[eng]
                deco = getattr(block, eng)

                def body(e, ops=ops):
                    for fn in ops:
                        fn(e)
                deco(body)


class Ctx:
    pass


def fence(c, engines=ENGINES):
    p = c.p
    sems = [c.sV, c.sC, c.sMM, c.sT, c.sX, c.sWA, c.sWB, c.sWHL, c.sKV, c.sOUT, c.sCC, c.sG]
    for eng in engines:
        for s in sems:
            p.wait(eng, s, s.n)


def segments(qlo, fd):
    out = []
    off = 0
    while off < fd:
        zcol = qlo + off
        bh = 0 if zcol < 512 else 1
        bo = zcol % 512
        seg = min(512 - bo, fd - off)
        out.append((bh, bo, off, seg))
        off += seg
    return out


def build():
    nc = bass.Bass("TRN2", target_bir_lowering=False, detect_race_conditions=False)
    c = Ctx()
    c.nc = nc
    p = P(nc)
    c.p = p

    c.x_in = nc.dram_tensor("x_in", [NT, 1024], F32, kind="ExternalInput").ap()
    c.wuv_in = nc.dram_tensor("wuv_in", [NER, 2, 1024, 128], F32, kind="ExternalInput").ap()
    c.wo_in = nc.dram_tensor("wo_in", [ND, 2048, 128], F32, kind="ExternalInput").ap()
    c.wkv1_in = nc.dram_tensor("wkv1_in", [1024, 128], F32, kind="ExternalInput").ap()
    c.wq_in = nc.dram_tensor("wq_in", [1024, 64], F32, kind="ExternalInput").ap()
    c.wv2_in = nc.dram_tensor("wv2_in", [64, 1024], F32, kind="ExternalInput").ap()
    c.gw_in = nc.dram_tensor("gw_in", [1024, 8], F32, kind="ExternalInput").ap()
    c.masks_in = nc.dram_tensor("masks_in", [16, 128, 128], BF16, kind="ExternalInput").ap()
    c.out = nc.dram_tensor("out", [NT, 1024], F32, kind="ExternalOutput").ap()
    c.wuv_hl = nc.dram_tensor("wuv_hl", [NER, 2, 1024, 256], BF16).ap()
    c.wo_hl = nc.dram_tensor("wo_hl", [ND, 2048, 256], BF16).ap()
    c.cc_in_t = nc.dram_tensor("cc_in", [NT, 256], BF16)
    c.cc_out_t = nc.dram_tensor("cc_out", [8 * NT, 256], BF16)
    c.cc_in = c.cc_in_t.ap()
    c.cc_out = c.cc_out_t.ap()

    with ExitStack() as st:
        sb = lambda name, shape, dt=F32: st.enter_context(nc.sbuf_tensor(name, shape, dt))
        c.xres = [sb(f"xres{j}", [128, NT]) for j in range(ND)]
        c.xln_h = sb("xln_h", [128, ND * TKB], BF16)
        c.xln_l = sb("xln_l", [128, ND * TKB], BF16)
        c.ident = sb("ident", [128, 128])
        c.onesf = sb("onesf", [1, 128])
        c.onesb = sb("onesb", [1, 128], BF16)
        c.cpool = sb("cpool", [128, 16])
        c.cpool_i = sb("cpool_i", [128, 2], I32)
        c.stat_tm = sb("stat_tm", [128, 2 * NTILE])
        c.rowbuf = sb("rowbuf", [2, TKB])
        c.ln_scr = sb("ln_scr", [128, 512])
        c.ln_t1 = sb("ln_t1", [128, TKB])
        c.ln_t2 = sb("ln_t2", [128, TKB])
        c.gbuf = sb("gbuf", [8, TKB])
        c.g_tm = sb("g_tm", [128, 32])
        c.g_rows = sb("g_rows", [8, TKB])
        c.grh = sb("grh", [8, TKB], BF16)
        c.grl = sb("grl", [8, TKB], BF16)
        c.dbuf = sb("dbuf", [128, 4])
        c.wbuf2 = sb("wbuf2", [128, 4])
        c.gsc = sb("gsc", [128, 16])
        c.gsc2 = sb("gsc2", [128, 16])
        c.gt = [sb(f"gt{i}", [128, 8]) for i in range(4)]
        c.gt2 = [sb(f"gt2_{i}", [128, 8]) for i in range(4)]
        c.selb = sb("selb", [8, 8 * 128], BF16)
        c.self32 = sb("self32", [2, 2 * 128])
        c.wq_h = sb("wq_h", [128, ND * 64], BF16)
        c.wq_l = sb("wq_l", [128, ND * 64], BF16)
        c.wkv_h = sb("wkv_h", [128, ND * 128], BF16)
        c.wkv_l = sb("wkv_l", [128, ND * 128], BF16)
        c.wv2_h = sb("wv2_h", [64, 1024], BF16)
        c.wv2_l = sb("wv2_l", [64, 1024], BF16)
        c.gw_h = sb("gw_h", [128, ND * 8], BF16)
        c.gw_l = sb("gw_l", [128, ND * 8], BF16)
        c.sil = [sb(f"sil{i}", [128, TKB]) for i in range(2)]
        c.s_t1 = sb("s_t1", [128, TKB])
        c.s_t2 = sb("s_t2", [128, TKB])

        c.pb = [st.enter_context(nc.psum_tensor(f"pb{i}", [128, 512], F32)) for i in range(8)]

        S = lambda name: Sem(st, nc, name)
        c.sX = S("sX"); c.sT = S("sT"); c.sV = S("sV"); c.sC = S("sC")
        c.sMM = S("sMM"); c.sWA = S("sWA"); c.sWB = S("sWB"); c.sWHL = S("sWHL")
        c.sCC = S("sCC"); c.sOUT = S("sOUT"); c.sKV = S("sKV"); c.sG = S("sG")

        with ExitStack() as stP0:
            emit_p0(c, stP0)
            fence(c)
        emit_stats(c)
        with ExitStack() as stA:
            emit_ffn(c, stA, call=0)
            fence(c)
        emit_stats(c)
        with ExitStack() as stB:
            emit_p2(c, stB)
            emit_p3(c, stB)
            fence(c)
        emit_stats(c)
        with ExitStack() as stC:
            emit_ffn(c, stC, call=1)
            emit_p5(c, stC)
        p.emit()
    return nc


def emit_p0(c, st):
    nc, p = c.nc, c.p
    sb = lambda name, shape, dt=F32: st.enter_context(nc.sbuf_tensor(name, shape, dt))
    xtm = [sb(f"xtm{t}", [128, 1024]) for t in range(NTILE)]
    c.wsm = sb("wsm", [128, 1024])
    iota_c = sb("iota_c", [128, 128])
    iota_p = sb("iota_p", [128, 1])
    iota_e8 = sb("iota_e8", [8, 8 * 128])
    iota_e2 = sb("iota_e2", [2, 2 * 128])

    def g_consts(e):
        e.iota(iota_c[:], [[1, 128]], channel_multiplier=0, allow_small_or_imprecise_dtypes=True)
        e.iota(iota_p[:], [[0, 1]], channel_multiplier=1, allow_small_or_imprecise_dtypes=True)
        e.memset(c.onesf[:], 1.0)
        e.memset(c.onesb[:], 1.0)
        for i, v in enumerate(CONSTS):
            e.memset(c.cpool[:, i:i + 1], v)
        e.iota(iota_e8[:].rearrange("p (b f) -> p b f", b=8), [[1, 8], [0, 128]],
               channel_multiplier=0, allow_small_or_imprecise_dtypes=True)
        e.iota(iota_e2[:].rearrange("p (b f) -> p b f", b=2), [[1, 2], [0, 128]],
               channel_multiplier=0, allow_small_or_imprecise_dtypes=True)
        return e.memset(c.onesb[:], 1.0)
    p.add("gpsimd", g_consts, c.sV, 1)
    consts_done = c.sV.n

    xld = []
    for t in range(NTILE):
        p.add("sync", lambda e, t=t: e.dma_start(
            out=xtm[t][:], in_=c.x_in[128 * t:128 * (t + 1), :]), c.sX, 16)
        xld.append(c.sX.n)
    smalls = [
        (lambda: c.wsm[:, 0:512].rearrange("p (cc e) -> p cc e", cc=8),
         lambda: c.wq_in.rearrange("(cc p) e -> p cc e", p=128), c.wq_h, c.wq_l, 128, 512),
        (lambda: c.wsm[:, 0:1024].rearrange("p (cc e) -> p cc e", cc=8),
         lambda: c.wkv1_in.rearrange("(cc p) e -> p cc e", p=128), c.wkv_h, c.wkv_l, 128, 1024),
        (lambda: c.wsm[:, 0:64].rearrange("p (cc e) -> p cc e", cc=8),
         lambda: c.gw_in.rearrange("(cc p) e -> p cc e", p=128), c.gw_h, c.gw_l, 128, 64),
        (lambda: c.wsm[:64, 0:1024], lambda: c.wv2_in, c.wv2_h, c.wv2_l, 64, 1024),
    ]
    for i, (dst, src, hi, lo, rows, cols) in enumerate(smalls):
        p.add("sync", lambda e, dst=dst, src=src: e.dma_start(out=dst(), in_=src()), c.sX, 16)
        ld = c.sX.n
        p.wait("vector", c.sX, ld)
        p.add("vector", lambda e, hi=hi, rows=rows, cols=cols: e.tensor_copy(
            hi[:], c.wsm[:rows, 0:cols]))
        p.add("vector", lambda e, lo=lo, hi=hi, rows=rows, cols=cols: e.tensor_sub(
            lo[:], c.wsm[:rows, 0:cols], hi[:]), c.sV, 1)
        if i + 1 < len(smalls):
            p.wait("sync", c.sV, c.sV.n)

    p.wait("vector", c.sV, consts_done)
    p.add("vector", lambda e: e.tensor_scalar(
        c.ident[:], iota_c[:], iota_p[:], None, ALU.is_equal))
    p.add("vector", lambda e: e.tensor_scalar(
        c.selb[:], iota_e8[:], iota_p[0:8, :], None, ALU.is_equal))
    p.add("vector", lambda e: e.tensor_scalar(
        c.self32[:], iota_e2[:], iota_p[0:2, :], None, ALU.is_equal), c.sV, 1)
    ident_done = c.sV.n

    p.wait("tensor", c.sV, ident_done)
    evac = {}
    for g in range(NTB * ND):
        tb, j = divmod(g, ND)
        bank = c.pb[g % 4]
        if g >= 4:
            p.wait("tensor", c.sV, evac[g - 4])
        for t in range(4):
            tile = 4 * tb + t
            p.wait("tensor", c.sX, xld[tile])

            def tr(e, bank=bank, t=t, tile=tile, j=j):
                return e.transpose(bank[:, 128 * t:128 * (t + 1)],
                                   xtm[tile][:, 128 * j:128 * (j + 1)], c.ident[:])
            p.add("tensor", tr, c.sT if t == 3 else None, 1)
        grp = c.sT.n
        p.wait("vector", c.sT, grp)
        p.add("vector", lambda e, tb=tb, j=j, bank=bank: e.tensor_copy(
            c.xres[j][:, TKB * tb:TKB * (tb + 1)], bank[:]), c.sV, 1)
        evac[g] = c.sV.n


def emit_stats(c):
    p = c.p
    p.wait("tensor", c.sV, c.sV.n)
    p.wait("tensor", c.sC, c.sC.n)
    vec_done = {}
    for tp in range(NTILE // 2):
        ta, tb_ = 2 * tp, 2 * tp + 1
        if tp % 2 == 0:
            bA, bB = (c.pb[6], c.pb[7]), (c.pb[4], c.pb[5])
        else:
            bA, bB = (c.pb[0], c.pb[1]), (c.pb[2], c.pb[3])
        if tp >= 2:
            p.wait("tensor", c.sV, vec_done[tp - 2])
            p.wait("tensor", c.sC, c.sC.n - 4)
        for t, banks in ((ta, bA), (tb_, bB)):
            for j in range(ND):
                bnk = banks[0] if j < 4 else banks[1]

                def tr(e, bnk=bnk, j=j, t=t):
                    return e.transpose(bnk[:, 128 * (j % 4):128 * (j % 4 + 1)],
                                       c.xres[j][:, 128 * t:128 * (t + 1)], c.ident[:])
                p.add("tensor", tr, c.sT if j == ND - 1 else None, 1)
        grp = c.sT.n
        p.wait("scalar", c.sT, grp)
        p.add("scalar", lambda e, bA=bA: e.activation(
            c.ln_scr[:, 0:512], bA[0][:], ACT.Square,
            accum_out=c.gsc[:, 8:9]))
        p.add("scalar", lambda e, bA=bA: e.activation(
            c.ln_scr[:, 0:512], bA[1][:], ACT.Square,
            accum_out=c.gsc[:, 9:10]), c.sC, 1)
        p.add("scalar", lambda e, bB=bB: e.activation(
            c.ln_scr[:, 0:512], bB[0][:], ACT.Square,
            accum_out=c.gsc2[:, 8:9]))
        p.add("scalar", lambda e, bB=bB: e.activation(
            c.ln_scr[:, 0:512], bB[1][:], ACT.Square,
            accum_out=c.gsc2[:, 9:10]), c.sC, 1)
        sq_done = c.sC.n
        p.wait("vector", c.sT, grp)
        p.wait("vector", c.sC, sq_done)
        p.add("vector", lambda e, bA=bA: e.reduce_sum(c.gsc[:, 0:1], bA[0][:], axis=AXX))
        p.add("vector", lambda e, bB=bB: e.reduce_sum(c.gsc2[:, 0:1], bB[0][:], axis=AXX))
        p.add("vector", lambda e, bA=bA: e.reduce_sum(c.gsc[:, 1:2], bA[1][:], axis=AXX))
        p.add("vector", lambda e, bB=bB: e.reduce_sum(c.gsc2[:, 1:2], bB[1][:], axis=AXX))

        def stats_ab(e):
            cp = c.cpool
            A, B = c.gsc, c.gsc2
            for g in (A, B):
                e.tensor_scalar(g[:, 2:3], g[:, 0:1], g[:, 1:2], None, ALU.add)
            for g in (A, B):
                e.tensor_scalar(g[:, 3:4], g[:, 8:9], g[:, 9:10], None, ALU.add)
            for g in (A, B):
                e.tensor_scalar(g[:, 4:5], g[:, 2:3], cp[:, C_INV1024:C_INV1024 + 1],
                                None, ALU.mult)
            for g in (A, B):
                e.tensor_scalar(g[:, 5:6], g[:, 3:4], cp[:, C_INV1024:C_INV1024 + 1],
                                cp[:, C_EPS:C_EPS + 1], ALU.mult, ALU.add)
            for g in (A, B):
                e.tensor_scalar(g[:, 6:7], g[:, 4:5], g[:, 4:5], None, ALU.mult)
            for g in (A, B):
                return_ins = e.tensor_scalar(g[:, 7:8], g[:, 5:6], g[:, 6:7],
                                             None, ALU.subtract)
            return return_ins
        p.add("vector", stats_ab, c.sV, 1)
        v_ready = c.sV.n
        p.wait("scalar", c.sV, v_ready)
        p.add("scalar", lambda e: e.activation(c.gsc[:, 10:11], c.gsc[:, 7:8], ACT.Sqrt))
        p.add("scalar", lambda e: e.activation(c.gsc2[:, 10:11], c.gsc2[:, 7:8],
                                               ACT.Sqrt), c.sC, 1)
        sqrt_done = c.sC.n
        p.wait("vector", c.sC, sqrt_done)

        def newton_ab(e, ta=ta, tb_=tb_):
            cp = c.cpool
            A, B = c.gsc, c.gsc2
            e.reciprocal(A[:, 15:16], A[:, 10:11])
            e.reciprocal(B[:, 15:16], B[:, 10:11])
            for _ in range(2):
                for g in (A, B):
                    e.tensor_scalar(g[:, 11:12], g[:, 15:16], g[:, 15:16], None, ALU.mult)
                for g in (A, B):
                    e.tensor_scalar(g[:, 12:13], g[:, 11:12], g[:, 7:8], None, ALU.mult)
                for g in (A, B):
                    e.tensor_scalar(g[:, 13:14], g[:, 12:13], cp[:, C_NHALF:C_NHALF + 1],
                                    cp[:, C_3HALF:C_3HALF + 1], ALU.mult, ALU.add)
                for g in (A, B):
                    e.tensor_scalar(g[:, 15:16], g[:, 15:16], g[:, 13:14], None, ALU.mult)
            e.tensor_copy(c.stat_tm[:, 2 * ta:2 * ta + 1], A[:, 15:16])
            e.tensor_copy(c.stat_tm[:, 2 * tb_:2 * tb_ + 1], B[:, 15:16])
            e.tensor_scalar(c.stat_tm[:, 2 * ta + 1:2 * ta + 2], A[:, 15:16], A[:, 4:5],
                            None, ALU.mult)
            return e.tensor_scalar(c.stat_tm[:, 2 * tb_ + 1:2 * tb_ + 2], B[:, 15:16],
                                   B[:, 4:5], None, ALU.mult)
        p.add("vector", newton_ab, c.sV, 1)
        vec_done[tp] = c.sV.n
    c.stats_done = c.sV.n


def ln_block(c, tb):
    p = c.p
    p.wait("tensor", c.sV, c.sV.n)
    p.wait("tensor", c.sC, c.sC.n)
    for t in range(4):
        tile = 4 * tb + t

        def tr(e, t=t, tile=tile):
            return e.transpose(c.pb[5][0:2, 128 * t:128 * (t + 1)],
                               c.stat_tm[:, 2 * tile:2 * tile + 2], c.ident[:])
        p.add("tensor", tr, c.sT if t == 3 else None, 1)
    p.wait("vector", c.sT, c.sT.n)
    p.add("vector", lambda e: e.tensor_copy(c.rowbuf[:], c.pb[5][0:2, 0:TKB]), c.sV, 1)
    p.wait("tensor", c.sV, c.sV.n)
    p.add("tensor", lambda e: e.matmul(c.pb[4][:], c.self32[:, 0:128], c.rowbuf[:],
                                       start=True, stop=True))
    p.add("tensor", lambda e: e.matmul(c.pb[6][:], c.self32[:, 128:256], c.rowbuf[:],
                                       start=True, stop=True), c.sT, 1)
    p.wait("vector", c.sT, c.sT.n)
    for j in range(ND):
        def apply(e, j=j, tb=tb):
            xs = c.xres[j][:, TKB * tb:TKB * (tb + 1)]
            e.tensor_mul(c.ln_t1[:], xs, c.pb[4][:])
            e.tensor_sub(c.ln_t2[:], c.ln_t1[:], c.pb[6][:])
            e.tensor_copy(c.xln_h[:, TKB * j:TKB * (j + 1)], c.ln_t2[:])
            return e.tensor_sub(c.xln_l[:, TKB * j:TKB * (j + 1)], c.ln_t2[:],
                                c.xln_h[:, TKB * j:TKB * (j + 1)])
        p.add("vector", apply, c.sV if j == ND - 1 else None, 1)
    return c.sV.n


def emit_ffn(c, st, call):
    nc, p = c.nc, c.p
    sb = lambda name, shape, dt=F32: st.enter_context(nc.sbuf_tensor(name, shape, dt))
    s_h = sb(f"s_h{call}", [128, NER * TKB], BF16)
    s_l = sb(f"s_l{call}", [128, NER * TKB], BF16)
    wuvc = [sb(f"wuvc{call}_{i}", [128, 4096], BF16) for i in range(2)]
    woc = [sb(f"woc{call}_{i}", [128, 4096], BF16) for i in range(2)]
    fresh = (call == 0)
    if fresh:
        wst = [sb(f"wst{i}", [128, 2048]) for i in range(2)]
        wost = wst

    uv_consumed = {}
    y_consumed = {}
    schain_done = {}
    yadd_done = {}
    split_w = {}
    split_wo = {}
    whl_written = {}
    wohl_written = {}
    wa_ld = {}
    wb_ld = {}

    for tb in range(NTB):
        xln_ready = ln_block(c, tb)

        p.wait("tensor", c.sV, xln_ready)

        def gate_mm(e):
            first = True
            ins = None
            for cc in range(ND):
                gh = c.gw_h[:, 8 * cc:8 * (cc + 1)]
                gl = c.gw_l[:, 8 * cc:8 * (cc + 1)]
                xh = c.xln_h[:, TKB * cc:TKB * (cc + 1)]
                xl = c.xln_l[:, TKB * cc:TKB * (cc + 1)]
                e.matmul(c.pb[7][0:8, :], gh, xh, start=first, stop=False); first = False
                e.matmul(c.pb[7][0:8, :], gh, xl, start=False, stop=False)
                ins = e.matmul(c.pb[7][0:8, :], gl, xh, start=False, stop=(cc == ND - 1))
            return ins
        p.add("tensor", gate_mm, c.sMM, 1)
        p.wait("vector", c.sMM, c.sMM.n)
        p.add("vector", lambda e: e.tensor_copy(c.gbuf[:], c.pb[7][0:8, :]), c.sV, 1)
        p.wait("tensor", c.sV, c.sV.n)
        for t in range(4):
            def tr(e, t=t):
                return e.transpose(c.pb[5][:, 8 * t:8 * (t + 1)],
                                   c.gbuf[0:8, 128 * t:128 * (t + 1)], c.ident[0:8, 0:8])
            p.add("tensor", tr, c.sT if t == 3 else None, 1)
        p.wait("vector", c.sT, c.sT.n)
        p.add("vector", lambda e: e.tensor_copy(c.g_tm[:, 0:16], c.pb[5][:, 0:16]))
        p.add("vector", lambda e: e.tensor_copy(c.g_tm[:, 16:32], c.pb[5][:, 16:32]))

        def gate_math1(e):
            cp = c.cpool
            ins = None
            for tp in range(2):
                tA, tB = 2 * tp, 2 * tp + 1
                lgA = c.g_tm[:, 8 * tA:8 * (tA + 1)]
                lgB = c.g_tm[:, 8 * tB:8 * (tB + 1)]
                A, B = c.gsc, c.gsc2
                gA, gB = c.gt, c.gt2
                e.reduce_max(A[:, 0:1], lgA, axis=AXX)
                e.reduce_max(B[:, 0:1], lgB, axis=AXX)
                e.tensor_scalar(gA[0][:], lgA, A[:, 0:1], None, ALU.subtract)
                e.tensor_scalar(gB[0][:], lgB, B[:, 0:1], None, ALU.subtract)
                e.tensor_scalar(gA[1][:], gA[0][:], cp[:, C_ZERO:C_ZERO + 1], None, ALU.is_equal)
                e.tensor_scalar(gB[1][:], gB[0][:], cp[:, C_ZERO:C_ZERO + 1], None, ALU.is_equal)
                e.scalar_tensor_tensor(gA[2][:], gA[1][:], cp[:, C_BIGNEG:C_BIGNEG + 1],
                                       gA[0][:], ALU.mult, ALU.add)
                e.scalar_tensor_tensor(gB[2][:], gB[1][:], cp[:, C_BIGNEG:C_BIGNEG + 1],
                                       gB[0][:], ALU.mult, ALU.add)
                e.reduce_max(c.dbuf[:, tA:tA + 1], gA[2][:], axis=AXX)
                e.reduce_max(c.dbuf[:, tB:tB + 1], gB[2][:], axis=AXX)
                e.tensor_scalar(gA[3][:], gA[2][:], c.dbuf[:, tA:tA + 1], None, ALU.is_equal)
                e.tensor_scalar(gB[3][:], gB[2][:], c.dbuf[:, tB:tB + 1], None, ALU.is_equal)
                e.tensor_copy(c.ln_t1[:, 16 * tA:16 * tA + 8], gA[1][:])
                e.tensor_copy(c.ln_t1[:, 16 * tB:16 * tB + 8], gB[1][:])
                e.tensor_copy(c.ln_t1[:, 16 * tA + 8:16 * tA + 16], gA[3][:])
                ins = e.tensor_copy(c.ln_t1[:, 16 * tB + 8:16 * tB + 16], gB[3][:])
            return ins
        p.add("vector", gate_math1, c.sV, 1)
        p.wait("scalar", c.sV, c.sV.n)
        p.add("scalar", lambda e: e.activation(c.wbuf2[:], c.dbuf[:], ACT.Sigmoid), c.sC, 1)
        p.wait("vector", c.sC, c.sC.n)

        def gate_math2(e):
            cp = c.cpool
            ins = None
            for tp in range(2):
                tA, tB = 2 * tp, 2 * tp + 1
                A, B = c.gsc, c.gsc2
                gA, gB = c.gt, c.gt2
                w2A = c.wbuf2[:, tA:tA + 1]
                w2B = c.wbuf2[:, tB:tB + 1]
                e.tensor_scalar(A[:, 0:1], w2A, cp[:, C_NONE:C_NONE + 1],
                                cp[:, C_ONE:C_ONE + 1], ALU.mult, ALU.add)
                e.tensor_scalar(B[:, 0:1], w2B, cp[:, C_NONE:C_NONE + 1],
                                cp[:, C_ONE:C_ONE + 1], ALU.mult, ALU.add)
                e.tensor_scalar(gA[0][:], c.ln_t1[:, 16 * tA:16 * tA + 8], A[:, 0:1],
                                None, ALU.mult)
                e.tensor_scalar(gB[0][:], c.ln_t1[:, 16 * tB:16 * tB + 8], B[:, 0:1],
                                None, ALU.mult)
                e.scalar_tensor_tensor(c.g_tm[:, 8 * tA:8 * (tA + 1)],
                                       c.ln_t1[:, 16 * tA + 8:16 * tA + 16], w2A,
                                       gA[0][:], ALU.mult, ALU.add)
                ins = e.scalar_tensor_tensor(c.g_tm[:, 8 * tB:8 * (tB + 1)],
                                             c.ln_t1[:, 16 * tB + 8:16 * tB + 16], w2B,
                                             gB[0][:], ALU.mult, ALU.add)
            return ins
        p.add("vector", gate_math2, c.sV, 1)
        p.wait("tensor", c.sV, c.sV.n)
        for t in range(4):
            def tr(e, t=t):
                return e.transpose(c.pb[5][0:8, 128 * t:128 * (t + 1)],
                                   c.g_tm[:, 8 * t:8 * (t + 1)], c.ident[:])
            p.add("tensor", tr, c.sT if t == 3 else None, 1)
        p.wait("vector", c.sT, c.sT.n)
        p.add("vector", lambda e: e.tensor_copy(c.g_rows[:], c.pb[5][0:8, 0:TKB]))
        p.add("vector", lambda e: e.tensor_copy(c.grh[:], c.g_rows[:]))
        p.add("vector", lambda e: e.tensor_sub(c.grl[:], c.g_rows[:], c.grh[:]), c.sV, 1)
        grows_done = c.sV.n

        for k in range(NER):
            gch = tb * NER + k
            slot = gch % 2
            e_idx = k // 2
            if fresh and tb == 0:
                if gch >= 2:
                    p.wait("sync", c.sV, split_w[gch - 2])
                p.add("sync", lambda e, k=k, slot=slot: e.dma_start(
                    out=wst[slot][:].rearrange("p (u cc e) -> p u cc e", u=2, cc=8),
                    in_=c.wuv_in[k].rearrange("u (cc p) e -> p u cc e", p=128)), c.sWA, 16)
                wa_ld[gch] = c.sWA.n
                p.wait("vector", c.sWA, wa_ld[gch])
                if gch >= 2:
                    p.wait("vector", c.sMM, uv_consumed[gch - 2])
                    p.wait("vector", c.sWHL, whl_written[gch - 2])

                def spl(e, slot=slot):
                    wv = wst[slot][:].rearrange("p (u cc e) -> p u cc e", u=2, cc=8)
                    comb = wuvc[slot][:].rearrange("p (u cc x) -> p u cc x", u=2, cc=8)
                    e.tensor_copy(comb[:, :, :, 0:128], wv)
                    return e.tensor_sub(comb[:, :, :, 128:256], wv, comb[:, :, :, 0:128])
                p.add("vector", spl, c.sV, 1)
                split_w[gch] = c.sV.n
                p.wait("scalar", c.sV, split_w[gch])
                p.add("scalar", lambda e, k=k, slot=slot: e.dma_start(
                    out=c.wuv_hl[k].rearrange("u (cc p) x -> p u cc x", p=128),
                    in_=wuvc[slot][:].rearrange("p (u cc x) -> p u cc x", u=2, cc=8)),
                    c.sWHL, 16)
                whl_written[gch] = c.sWHL.n
            else:
                if fresh and tb == 1 and k < 2:
                    p.wait("sync", c.sWHL, whl_written[NER - 1])
                if gch >= 2 and (gch - 2) in uv_consumed:
                    p.wait("sync", c.sMM, uv_consumed[gch - 2])
                p.add("sync", lambda e, k=k, slot=slot: e.dma_start(
                    out=wuvc[slot][:].rearrange("p (u cc x) -> p u cc x", u=2, cc=8),
                    in_=c.wuv_hl[k].rearrange("u (cc p) x -> p u cc x", p=128)), c.sWA, 16)
                wa_ld[gch] = c.sWA.n

            gb = c.pb[4] if e_idx % 2 == 0 else c.pb[6]
            if k % 2 == 0:
                p.wait("tensor", c.sV, grows_done)
                if gch - 3 in schain_done:
                    p.wait("tensor", c.sV, schain_done[gch - 3])

                def gbc(e, gb=gb, e_idx=e_idx):
                    sel = c.selb[:, 128 * e_idx:128 * (e_idx + 1)]
                    e.matmul(gb[:], sel, c.grh[:], start=True, stop=False)
                    return e.matmul(gb[:], sel, c.grl[:], start=False, stop=True)
                p.add("tensor", gbc, c.sT, 1)

            bu = c.pb[(k % 2) * 2]
            bv = c.pb[(k % 2) * 2 + 1]
            p.wait("tensor", c.sWA, wa_ld[gch])
            if fresh and tb == 0:
                p.wait("tensor", c.sV, split_w[gch])
            if gch >= 2 and (gch - 2) in schain_done:
                p.wait("tensor", c.sV, schain_done[gch - 2])

            def uvmm(e, slot=slot, bu=bu, bv=bv):
                ins = None
                for uv, bank in ((0, bu), (1, bv)):
                    first = True
                    for cc in range(ND):
                        off = (uv * ND + cc) * 256
                        wh = wuvc[slot][:, off:off + 128]
                        wl = wuvc[slot][:, off + 128:off + 256]
                        xh = c.xln_h[:, TKB * cc:TKB * (cc + 1)]
                        xl = c.xln_l[:, TKB * cc:TKB * (cc + 1)]
                        e.matmul(bank[:], wh, xh, start=first, stop=False); first = False
                        e.matmul(bank[:], wh, xl, start=False, stop=False)
                        ins = e.matmul(bank[:], wl, xh, start=False, stop=(cc == ND - 1))
                return ins
            p.add("tensor", uvmm, c.sMM, 1)
            uv_consumed[gch] = c.sMM.n

            p.wait("scalar", c.sMM, uv_consumed[gch])
            p.add("scalar", lambda e, k=k, bu=bu: e.activation(
                c.sil[k % 2][:], bu[:], ACT.Silu), c.sC, 1)
            sil_done = c.sC.n

            p.wait("vector", c.sC, sil_done)

            def schain(e, k=k, bv=bv, gb=gb):
                e.tensor_mul(c.s_t1[:], c.sil[k % 2][:], bv[:])
                e.tensor_mul(c.s_t2[:], c.s_t1[:], gb[:])
                e.tensor_copy(s_h[:, TKB * k:TKB * (k + 1)], c.s_t2[:])
                return e.tensor_sub(s_l[:, TKB * k:TKB * (k + 1)], c.s_t2[:],
                                    s_h[:, TKB * k:TKB * (k + 1)])
            p.add("vector", schain, c.sV, 1)
            schain_done[gch] = c.sV.n

        all_s = schain_done[tb * NER + NER - 1]
        for j in range(ND):
            gj = tb * ND + j
            slot = gj % 2
            if fresh and tb == 0:
                if gj < 2:
                    p.wait("sync", c.sV, split_w[14 + gj])
                else:
                    p.wait("sync", c.sV, split_wo[gj - 2])
                p.add("sync", lambda e, j=j, slot=slot: e.dma_start(
                    out=wost[slot][:].rearrange("p (cc e) -> p cc e", cc=16),
                    in_=c.wo_in[j].rearrange("(cc p) e -> p cc e", p=128)), c.sWB, 16)
                wb_ld[gj] = c.sWB.n
                p.wait("vector", c.sWB, wb_ld[gj])
                if gj >= 2:
                    p.wait("vector", c.sMM, y_consumed[gj - 2])
                    p.wait("vector", c.sWHL, wohl_written[gj - 2])

                def splo(e, slot=slot):
                    wv = wost[slot][:].rearrange("p (cc e) -> p cc e", cc=16)
                    comb = woc[slot][:].rearrange("p (cc x) -> p cc x", cc=16)
                    e.tensor_copy(comb[:, :, 0:128], wv)
                    return e.tensor_sub(comb[:, :, 128:256], wv, comb[:, :, 0:128])
                p.add("vector", splo, c.sV, 1)
                split_wo[gj] = c.sV.n
                p.wait("scalar", c.sV, split_wo[gj])
                p.add("scalar", lambda e, j=j, slot=slot: e.dma_start(
                    out=c.wo_hl[j].rearrange("(cc p) x -> p cc x", p=128),
                    in_=woc[slot][:].rearrange("p (cc x) -> p cc x", cc=16)), c.sWHL, 16)
                wohl_written[gj] = c.sWHL.n
            else:
                if fresh and tb == 1 and j < 2:
                    p.wait("sync", c.sWHL, wohl_written[ND - 1])
                if gj >= 2 and (gj - 2) in y_consumed:
                    p.wait("sync", c.sMM, y_consumed[gj - 2])
                p.add("sync", lambda e, j=j, slot=slot: e.dma_start(
                    out=woc[slot][:].rearrange("p (cc x) -> p cc x", cc=16),
                    in_=c.wo_hl[j].rearrange("(cc p) x -> p cc x", p=128)), c.sWB, 16)
                wb_ld[gj] = c.sWB.n

            by = c.pb[j % 2]
            p.wait("tensor", c.sV, all_s)
            p.wait("tensor", c.sWB, wb_ld[gj])
            if fresh and tb == 0:
                p.wait("tensor", c.sV, split_wo[gj])
            if gj >= 2 and (gj - 2) in yadd_done:
                p.wait("tensor", c.sV, yadd_done[gj - 2])

            def ymm(e, slot=slot, by=by):
                first = True
                ins = None
                for k in range(NER):
                    off = k * 256
                    wh = woc[slot][:, off:off + 128]
                    wl = woc[slot][:, off + 128:off + 256]
                    sh_ = s_h[:, TKB * k:TKB * (k + 1)]
                    sl_ = s_l[:, TKB * k:TKB * (k + 1)]
                    e.matmul(by[:], wh, sh_, start=first, stop=False); first = False
                    e.matmul(by[:], wh, sl_, start=False, stop=False)
                    ins = e.matmul(by[:], wl, sh_, start=False, stop=(k == NER - 1))
                return ins
            p.add("tensor", ymm, c.sMM, 1)
            y_consumed[gj] = c.sMM.n

            p.wait("vector", c.sMM, y_consumed[gj])

            def yadd(e, j=j, tb=tb, by=by):
                xs = c.xres[j][:, TKB * tb:TKB * (tb + 1)]
                return e.scalar_tensor_tensor(xs, by[:], c.cpool[:, C_HALF:C_HALF + 1],
                                              xs, ALU.mult, ALU.add)
            p.add("vector", yadd, c.sV, 1)
            yadd_done[gj] = c.sV.n


def emit_p2(c, st):
    nc, p = c.nc, c.p
    sb = lambda name, shape, dt=F32: st.enter_context(nc.sbuf_tensor(name, shape, dt))
    c.qk_h = sb("qk_h", [128, NT], BF16)
    c.qk_l = sb("qk_l", [128, NT], BF16)
    xvstage = sb("xvstage", [128, 4 * 128], BF16)
    kv_base = c.sKV.n

    for tb in range(NTB):
        xln_ready = ln_block(c, tb)
        p.wait("tensor", c.sV, xln_ready)

        def qkmm(e):
            first = True
            ins = None
            for cc in range(ND):
                xh = c.xln_h[:, TKB * cc:TKB * (cc + 1)]
                xl = c.xln_l[:, TKB * cc:TKB * (cc + 1)]
                qh = c.wq_h[:, 64 * cc:64 * (cc + 1)]
                ql = c.wq_l[:, 64 * cc:64 * (cc + 1)]
                kh = c.wkv_h[:, 128 * cc:128 * cc + 64]
                kl = c.wkv_l[:, 128 * cc:128 * cc + 64]
                e.matmul(c.pb[7][0:64, :], qh, xh, start=first, stop=False)
                e.matmul(c.pb[6][0:64, :], kh, xh, start=first, stop=False)
                first = False
                e.matmul(c.pb[7][0:64, :], qh, xl, start=False, stop=False)
                e.matmul(c.pb[6][0:64, :], kh, xl, start=False, stop=False)
                e.matmul(c.pb[7][0:64, :], ql, xh, start=False, stop=(cc == ND - 1))
                ins = e.matmul(c.pb[6][0:64, :], kl, xh, start=False, stop=(cc == ND - 1))
            return ins
        p.add("tensor", qkmm, c.sMM, 1)
        p.wait("vector", c.sMM, c.sMM.n)

        def qk_split(e, tb=tb):
            cols = slice(TKB * tb, TKB * (tb + 1))
            e.tensor_copy(c.qk_h[0:64, cols], c.pb[7][0:64, :])
            e.tensor_copy(c.qk_h[64:128, cols], c.pb[6][0:64, :])
            e.tensor_sub(c.qk_l[0:64, cols], c.pb[7][0:64, :], c.qk_h[0:64, cols])
            return e.tensor_sub(c.qk_l[64:128, cols], c.pb[6][0:64, :], c.qk_h[64:128, cols])
        p.add("vector", qk_split, c.sV, 1)
        qk_done = c.sV.n

        xv_done = {}
        for t in range(4):
            bxv = c.pb[t % 2]

            def xvmm(e, t=t, bxv=bxv):
                first = True
                ins = None
                for cc in range(ND):
                    xh = c.xln_h[:, TKB * cc + 128 * t:TKB * cc + 128 * (t + 1)]
                    xl = c.xln_l[:, TKB * cc + 128 * t:TKB * cc + 128 * (t + 1)]
                    wh = c.wkv_h[:, 128 * cc + 64:128 * cc + 128]
                    wl = c.wkv_l[:, 128 * cc + 64:128 * cc + 128]
                    e.matmul(bxv[:, 0:64], xh, wh, start=first, stop=False); first = False
                    e.matmul(bxv[:, 0:64], xh, wl, start=False, stop=False)
                    ins = e.matmul(bxv[:, 0:64], xl, wh, start=False, stop=(cc == ND - 1))
                return ins
            if t >= 2:
                p.wait("tensor", c.sV, xv_done[t - 2])
            p.add("tensor", xvmm, c.sMM, 1)
            p.wait("vector", c.sMM, c.sMM.n)
            if t == 0 and tb > 0:
                p.wait("vector", c.sKV, kv_base + 16 * 4 * tb)

            def xv_split(e, t=t, bxv=bxv):
                base = 128 * t
                e.tensor_copy(xvstage[:, base:base + 64], bxv[:, 0:64])
                return e.tensor_sub(xvstage[:, base + 64:base + 128], bxv[:, 0:64],
                                    xvstage[:, base:base + 64])
            p.add("vector", xv_split, c.sV, 1)
            xv_done[t] = c.sV.n

        if True:
            p.wait("gpsimd", c.sV, qk_done)
            p.add("gpsimd", lambda e, tb=tb: e.dma_start(
                out=c.cc_in[64 * 4 * tb:64 * 4 * (tb + 1), 0:128]
                    .rearrange("(t p) f -> p t f", p=64),
                in_=c.qk_h[64:128, TKB * tb:TKB * (tb + 1)]
                    .rearrange("p (t f) -> p t f", t=4)), c.sKV, 16)
            p.add("gpsimd", lambda e, tb=tb: e.dma_start(
                out=c.cc_in[64 * 4 * tb:64 * 4 * (tb + 1), 128:256]
                    .rearrange("(t p) f -> p t f", p=64),
                in_=c.qk_l[64:128, TKB * tb:TKB * (tb + 1)]
                    .rearrange("p (t f) -> p t f", t=4)), c.sKV, 16)
            p.wait("gpsimd", c.sV, xv_done[3])
            for half in range(2):
                lt0 = 4 * tb + 2 * half

                def xvdma(e, lt0=lt0, half=half, tb=tb):
                    rows = slice(1024 + 128 * (lt0 // 2), 1024 + 128 * (lt0 // 2) + 128)
                    return e.dma_start(
                        out=c.cc_in[rows, :].rearrange("p (t f) -> p t f", t=2),
                        in_=xvstage[:, 256 * half:256 * (half + 1)]
                            .rearrange("p (t f) -> p t f", t=2))
                p.add("gpsimd", xvdma, c.sKV, 16)

    if True:
        p.wait("gpsimd", c.sKV, c.sKV.n)
        p.add("gpsimd", lambda e: e.collective_compute(
            "AllGather", ALU.bypass, replica_groups=[list(range(8))],
            ins=[c.cc_in_t.ap().opt()], outs=[c.cc_out_t.ap().opt()]), c.sCC, 1)


def emit_p3(c, st):
    nc, p = c.nc, c.p
    sb = lambda name, shape, dt=F32: st.enter_context(nc.sbuf_tensor(name, shape, dt))
    XVS = 72
    k_h = sb("k_h", [64, 64 * 128], BF16)
    k_l = sb("k_l", [64, 64 * 128], BF16)
    xvh = sb("xvh", [128, 64 * XVS], BF16)
    xvl = sb("xvl", [128, 64 * XVS], BF16)
    ebuf = [sb(f"ebuf{i}", [128, 1024]) for i in range(2)]
    eh = [sb(f"eh{i}", [128, 1024], BF16) for i in range(2)]
    el = [sb(f"el{i}", [128, 1024], BF16) for i in range(2)]
    aacc = sb("aacc", [65, 1024])
    aev = aacc
    c.masks = sb("masks", [128, 16 * 128], BF16)
    anh = sb("anh", [64, 1024], BF16)
    anl = sb("anl", [64, 1024], BF16)

    p.wait("sync", c.sCC, c.sCC.n)
    p.add("sync", lambda e: e.dma_start(
        out=c.masks[:].rearrange("p (k f) -> p k f", k=16),
        in_=c.masks_in.rearrange("k p f -> p k f")), c.sX, 16)
    masks_ld = c.sX.n

    def ones_cols(e):
        e.memset(xvh[:].rearrange("p (j x) -> p j x", j=64)[:, :, 64:72], 0.0)
        e.memset(xvl[:].rearrange("p (j x) -> p j x", j=64)[:, :, 64:72], 0.0)
        e.memset(xvh[:].rearrange("p (j x) -> p j x", j=64)[:, :, 64:65], 1.0)
        return e.memset(xvl[:].rearrange("p (j x) -> p j x", j=64)[:, :, 64:65], 0.0)
    p.add("gpsimd", ones_cols, c.sG, 1)
    ones_done = c.sG.n

    for h in range(2):
        if h == 1:
            p.wait("sync", c.sMM, c.sMM.n)
        owner_ready = {}
        for r in range(8):
            c0 = r if h == 0 else 7 - r
            for lo in (False, True):
                def kload(e, r=r, c0=c0, h=h, lo=lo):
                    dst = k_l if lo else k_h
                    cols = slice(128, 256) if lo else slice(0, 128)
                    src = c.cc_out[2048 * r + 512 * h:2048 * r + 512 * h + 512, cols]
                    src3 = src.rearrange("(i p) f -> p i f", p=64)
                    dst3 = dst[0:64, :].rearrange("p (j f) -> p j f", f=128)[:, c0:c0 + 57:8, :]
                    return e.dma_start(out=dst3, in_=src3)
                p.add("sync", kload, c.sX, 16)
            for par in (0, 1):
                for lo in (False, True):
                    def xvload(e, r=r, c0=c0, h=h, par=par, lo=lo):
                        dst = xvl if lo else xvh
                        rows0 = 2048 * r + 1024 + 128 * 4 * h
                        colh = 128 * par + (64 if lo else 0)
                        src = c.cc_out[rows0:rows0 + 512, colh:colh + 64]
                        src3 = src.rearrange("(b p) f -> p b f", p=128)
                        j0 = 8 * par + c0
                        dst3 = dst[:, XVS * j0:].rearrange("p (j x) -> p j x", x=XVS)[
                            :, 0:49:16, 0:64]
                        return e.dma_start(out=dst3, in_=src3)
                    p.add("sync", xvload, c.sX, 16)
            owner_ready[r] = c.sX.n

        e_split = {}
        e_done = {}
        a_after = {}
        for j in range(64):
            qlo = 128 * (j // 8)
            fd = 1024 - qlo
            zb = (c.pb[4], c.pb[5]) if j % 2 == 0 else (c.pb[6], c.pb[7])
            segs = segments(qlo, fd)
            r = (j % 8) if h == 0 else 7 - (j % 8)
            p.wait("tensor", c.sX, owner_ready[r])
            if j == 0:
                p.wait("tensor", c.sG, ones_done)
            if j >= 2:
                p.wait("tensor", c.sV, e_split[j - 2])

            def zmm(e, j=j, qlo=qlo, zb=zb, h=h, segs=segs):
                ins = None
                for pi, (lh, rh) in enumerate((("h", "h"), ("h", "l"), ("l", "h"))):
                    kv = k_h if lh == "h" else k_l
                    qv = c.qk_h if rh == "h" else c.qk_l
                    lhs = kv[0:64, 128 * j:128 * (j + 1)]
                    for (bh, bo, off, seg) in segs:
                        rhs = qv[0:64, 1024 * h + qlo + off:1024 * h + qlo + off + seg]
                        ins = e.matmul(zb[bh][:, bo:bo + seg], lhs, rhs,
                                       start=(pi == 0), stop=(pi == 2))
                return ins
            p.add("tensor", zmm, c.sMM, 1)
            z_done = c.sMM.n

            p.wait("vector", c.sMM, z_done)
            if h == 0 and j == 0:
                p.wait("vector", c.sX, masks_ld)

            def maskadd(e, j=j, qlo=qlo, zb=zb, h=h):
                bh = 0 if qlo < 512 else 1
                bo = qlo % 512
                mk = c.masks[:, 128 * (h * 8 + j % 8):128 * (h * 8 + j % 8 + 1)]
                t = zb[bh][:, bo:bo + 128]
                return e.tensor_add(t, t, mk)
            p.add("vector", maskadd, c.sV, 1)
            mask_done = c.sV.n

            p.wait("scalar", c.sV, mask_done)
            if j >= 2:
                p.wait("scalar", c.sV, e_split[j - 2])

            def eact(e, j=j, zb=zb, segs=segs):
                ins = None
                for (bh, bo, off, seg) in segs:
                    ins = e.activation(ebuf[j % 2][:, off:off + seg],
                                       zb[bh][:, bo:bo + seg], ACT.Exp,
                                       bias=c.cpool[:, C_NCSH:C_NCSH + 1],
                                       scale=c.cpool[:, C_SC:C_SC + 1])
                return ins
            p.add("scalar", eact, c.sC, 1)
            e_done[j] = c.sC.n
            p.wait("vector", c.sC, e_done[j])
            if j >= 2:
                p.wait("vector", c.sMM, a_after[j - 2])

            def esplit(e, j=j, fd=fd):
                hf = fd // 2
                e.tensor_copy(eh[j % 2][:, 0:hf], ebuf[j % 2][:, 0:hf])
                e.tensor_copy(eh[j % 2][:, hf:fd], ebuf[j % 2][:, hf:fd])
                e.tensor_sub(el[j % 2][:, 0:hf], ebuf[j % 2][:, 0:hf],
                             eh[j % 2][:, 0:hf])
                return e.tensor_sub(el[j % 2][:, hf:fd], ebuf[j % 2][:, hf:fd],
                                    eh[j % 2][:, hf:fd])
            p.add("vector", esplit, c.sV, 1)
            e_split[j] = c.sV.n

            p.wait("tensor", c.sV, e_split[j])
            blk_start = (j % 8 == 0)
            blk_end = (j % 8 == 7)
            if blk_start and j > 0:
                p.wait("tensor", c.sV, c.sV.n)

            def amm(e, j=j, qlo=qlo, segs=segs, blk_start=blk_start, blk_end=blk_end):
                ins = None
                for pi, (lh, rh) in enumerate((("h", "h"), ("h", "l"), ("l", "h"))):
                    xv = xvh if lh == "h" else xvl
                    ev = eh[j % 2] if rh == "h" else el[j % 2]
                    lhs = xv[:, XVS * j:XVS * j + 65]
                    for si, (bh, bo, off, seg) in enumerate(segs):
                        ins = e.matmul(c.pb[bh][0:65, bo:bo + seg], lhs,
                                       ev[:, off:off + seg],
                                       start=(blk_start and pi == 0),
                                       stop=(blk_end and pi == 2))
                return ins
            p.add("tensor", amm, c.sMM, 1)
            a_after[j] = c.sMM.n

            if blk_end:
                p.wait("vector", c.sMM, a_after[j])
                first_blk = (j < 8)
                qlo_b = 128 * ((j - 7) // 8)

                def aev_add(e, first_blk=first_blk, qlo_b=qlo_b):
                    if first_blk:
                        e.tensor_copy(aacc[:, 0:512], c.pb[0][0:65, :])
                        return e.tensor_copy(aacc[:, 512:1024], c.pb[1][0:65, :])
                    ins = None
                    for (bh, bo, off, seg) in segments(qlo_b, 1024 - qlo_b):
                        ins = e.tensor_add(aacc[:, qlo_b + off:qlo_b + off + seg],
                                           aacc[:, qlo_b + off:qlo_b + off + seg],
                                           c.pb[bh][0:65, bo:bo + seg])
                    return ins
                p.add("vector", aev_add, c.sV, 1)

        p.add("vector", lambda e: e.reciprocal(c.ln_t1[0:1, :], aev[64:65, 0:512]))
        p.add("vector", lambda e: e.reciprocal(c.ln_t2[0:1, :], aev[64:65, 512:1024]), c.sV, 1)
        p.wait("tensor", c.sV, c.sV.n)
        p.wait("tensor", c.sC, c.sC.n)
        p.add("tensor", lambda e: e.matmul(c.pb[2][0:64, :], c.onesf[:, 0:64],
                                           c.ln_t1[0:1, :], start=True, stop=True))
        p.add("tensor", lambda e: e.matmul(c.pb[3][0:64, :], c.onesf[:, 0:64],
                                           c.ln_t2[0:1, :], start=True, stop=True), c.sT, 1)
        p.wait("vector", c.sT, c.sT.n)

        def an_make(e):
            e.tensor_mul(aev[0:64, 0:512], aev[0:64, 0:512], c.pb[2][0:64, :])
            e.tensor_mul(aev[0:64, 512:1024], aev[0:64, 512:1024], c.pb[3][0:64, :])
            e.tensor_copy(anh[:], aev[0:64, :])
            return e.tensor_sub(anl[:], aev[0:64, :], anh[:])
        p.add("vector", an_make, c.sV, 1)
        an_done = c.sV.n
        p.wait("tensor", c.sV, an_done)

        out_add = {}
        for j in range(ND):
            ob = (c.pb[4], c.pb[5]) if j % 2 == 0 else (c.pb[6], c.pb[7])
            if j >= 2:
                p.wait("tensor", c.sV, out_add[j - 2])

            def omm(e, j=j, ob=ob):
                ins = None
                for pi, (lh, rh) in enumerate((("h", "h"), ("h", "l"), ("l", "h"))):
                    wv = c.wv2_h if lh == "h" else c.wv2_l
                    av = anh if rh == "h" else anl
                    lhs = wv[:, 128 * j:128 * (j + 1)]
                    for seg in range(2):
                        ins = e.matmul(ob[seg][:], lhs, av[:, 512 * seg:512 * (seg + 1)],
                                       start=(pi == 0), stop=(pi == 2))
                return ins
            p.add("tensor", omm, c.sMM, 1)
            og = c.sMM.n
            p.wait("vector", c.sMM, og)

            def oadd(e, j=j, ob=ob, h=h):
                xs0 = c.xres[j][:, 1024 * h:1024 * h + 512]
                xs1 = c.xres[j][:, 1024 * h + 512:1024 * h + 1024]
                e.tensor_add(xs0, xs0, ob[0][:])
                return e.tensor_add(xs1, xs1, ob[1][:])
            p.add("vector", oadd, c.sV, 1)
            out_add[j] = c.sV.n


def emit_p5(c, st, nbuf=2):
    nc, p = c.nc, c.p
    sb = lambda name, shape, dt=F32: st.enter_context(nc.sbuf_tensor(name, shape, dt))
    outstg = [sb(f"outstg{i}", [128, 1024]) for i in range(nbuf)]
    out_base = c.sOUT.n
    p.wait("tensor", c.sV, c.sV.n)
    p.wait("tensor", c.sC, c.sC.n)
    evac = {}
    for t in range(NTILE):
        b0 = c.pb[6] if t % 2 == 0 else c.pb[4]
        b1 = c.pb[7] if t % 2 == 0 else c.pb[5]
        if t >= 2:
            p.wait("tensor", c.sV, evac[t - 2])
        for j in range(ND):
            bnk = b0 if j < 4 else b1

            def tr(e, bnk=bnk, j=j, t=t):
                return e.transpose(bnk[:, 128 * (j % 4):128 * (j % 4 + 1)],
                                   c.xres[j][:, 128 * t:128 * (t + 1)], c.ident[:])
            p.add("tensor", tr, c.sT if j == ND - 1 else None, 1)
        grp = c.sT.n
        p.wait("vector", c.sT, grp)
        if t >= nbuf:
            p.wait("vector", c.sOUT, out_base + 16 * (t - nbuf + 1))

        def ev(e, t=t, b0=b0, b1=b1):
            e.tensor_copy(outstg[t % nbuf][:, 0:512], b0[:])
            return e.tensor_copy(outstg[t % nbuf][:, 512:1024], b1[:])
        p.add("vector", ev, c.sV, 1)
        evac[t] = c.sV.n
        p.wait("scalar", c.sV, evac[t])
        p.add("scalar", lambda e, t=t: e.dma_start(
            out=c.out[128 * t:128 * (t + 1), :], in_=outstg[t % nbuf][:]), c.sOUT, 16)
    p.wait("gpsimd", c.sOUT, out_base + 16 * NTILE)
    for s in (c.sX, c.sWA, c.sWB, c.sWHL, c.sKV):
        p.wait("gpsimd", s, s.n)


def host_prep(inputs):
    x = np.asarray(inputs["L0_attn_x"], dtype=np.float32)
    wq = np.ascontiguousarray(np.asarray(inputs["L0_attn_w_q"], dtype=np.float32))
    wk = np.asarray(inputs["L0_attn_w_k"], dtype=np.float32)
    wv1 = np.asarray(inputs["L0_attn_w_v1"], dtype=np.float32)
    wv2 = np.ascontiguousarray(np.asarray(inputs["L0_attn_w_v2"], dtype=np.float32))
    gw = np.ascontiguousarray(np.asarray(inputs["L0_ffn_gate_w"], dtype=np.float32))
    Wu = np.asarray(inputs["L0_ffn_Wu"], dtype=np.float32)
    Wv = np.asarray(inputs["L0_ffn_Wv"], dtype=np.float32)
    Wo = np.asarray(inputs["L0_ffn_Wo"], dtype=np.float32)

    Wu2 = Wu.transpose(1, 0, 2).reshape(1024, 2048)
    Wv2 = Wv.transpose(1, 0, 2).reshape(1024, 2048)
    wuv = np.empty((NER, 2, 1024, 128), dtype=np.float32)
    for k in range(NER):
        wuv[k, 0] = Wu2[:, 128 * k:128 * (k + 1)]
        wuv[k, 1] = Wv2[:, 128 * k:128 * (k + 1)]
    Wo2 = Wo.reshape(2048, 1024)
    wo = np.empty((ND, 2048, 128), dtype=np.float32)
    for j in range(ND):
        wo[j] = Wo2[:, 128 * j:128 * (j + 1)]
    wkv1 = np.ascontiguousarray(np.concatenate([wk, wv1], axis=1))

    tri = np.where(np.arange(128)[None, :] >= np.arange(128)[:, None], 0.0, -1e30
                   ).astype(np.float32)
    in_maps = []
    for core in range(8):
        t0 = [8 * i + core for i in range(8)]
        t1 = [8 * i + (7 - core) for i in range(8)]
        xs = np.concatenate([x[0, 128 * t:128 * (t + 1)] for t in t0]
                            + [x[1, 128 * t:128 * (t + 1)] for t in t1], axis=0)
        masks = np.zeros((16, 128, 128), dtype=np.float32)
        for half in range(2):
            pos = core if half == 0 else 7 - core
            for kk in range(8):
                if kk == pos:
                    masks[half * 8 + kk] = tri
                elif kk > pos:
                    masks[half * 8 + kk] = -1e30
        import ml_dtypes
        in_maps.append({
            "x_in": np.ascontiguousarray(xs),
            "wuv_in": wuv, "wo_in": wo, "wkv1_in": wkv1, "wq_in": wq,
            "wv2_in": wv2, "gw_in": gw,
            "masks_in": masks.astype(ml_dtypes.bfloat16),
        })
    return in_maps


def host_gather(results):
    out = np.empty((2, 8192, 1024), dtype=np.float32)
    for core in range(8):
        r = np.asarray(results[core]["out"])
        for i in range(8):
            t = 8 * i + core
            out[0, 128 * t:128 * (t + 1)] = r[128 * i:128 * (i + 1)]
            t = 8 * i + (7 - core)
            out[1, 128 * t:128 * (t + 1)] = r[1024 + 128 * i:1024 + 128 * (i + 1)]
    return out


def kernel(**inputs):
    in_maps = host_prep(inputs)
    nc = build()
    res = run_bass_kernel_spmd(nc, in_maps, core_ids=list(range(8)))
    return host_gather(res.results)


# version 02
# speedup vs baseline: 1.0056x; 1.0056x over previous
import numpy as np
from contextlib import ExitStack
import concourse.bass as bass
import concourse.mybir as mybir
from concourse.bass_utils import run_bass_kernel_spmd

F32 = mybir.dt.float32
BF16 = mybir.dt.bfloat16
I32 = mybir.dt.int32
ACT = mybir.ActivationFunctionType
ALU = mybir.AluOpType
AXX = mybir.AxisListType.X

ND = 8
NTB = 4
TKB = 512
NTILE = 16
NER = 16
NT = 2048
CSHIFT = 20.0
ENGINES = ("sync", "gpsimd", "tensor", "vector", "scalar")

C_NCSH, C_SC, C_HALF, C_ONE, C_EPS, C_INV1024, C_NHALF, C_3HALF, C_BIGNEG, C_ZERO, C_NONE = range(11)
CONSTS = [-CSHIFT, 0.125, 0.5, 1.0, 1e-5, 1.0 / 1024.0, -0.5, 1.5, -1e30, 0.0, -1.0]



class Sem:
    __slots__ = ("h", "n", "name")

    def __init__(self, st, nc, name):
        self.h = st.enter_context(nc.semaphore(name))
        self.n = 0
        self.name = name


class P:
    def __init__(self, nc):
        self.nc = nc
        self.ops = {e: [] for e in ENGINES}
        self.waitlog = []

    def check(self):
        bad = []
        for eng, sem, val in self.waitlog:
            if val > sem.n:
                bad.append((eng, sem.name, val, sem.n))
        return bad

    def add(self, eng, fn, sem=None, k=1):
        if sem is None:
            self.ops[eng].append(fn)
        else:
            sem.n += k

            def wrapped(e, fn=fn, sem=sem, k=k):
                fn(e).then_inc(sem.h, k)
            self.ops[eng].append(wrapped)

    def wait(self, eng, sem, val):
        if val <= 0:
            return
        self.waitlog.append((eng, sem, val))
        self.ops[eng].append(lambda e, h=sem.h, v=val: e.wait_ge(h, v))

    def emit(self):
        with self.nc.Block() as block:
            for eng in ENGINES:
                ops = self.ops[eng]
                deco = getattr(block, eng)

                def body(e, ops=ops):
                    for fn in ops:
                        fn(e)
                deco(body)


class Ctx:
    pass


def fence(c, engines=ENGINES):
    p = c.p
    sems = [c.sV, c.sC, c.sMM, c.sT, c.sX, c.sWA, c.sWB, c.sWHL, c.sKV, c.sOUT, c.sCC, c.sG]
    for eng in engines:
        for s in sems:
            p.wait(eng, s, s.n)


def segments(qlo, fd):
    out = []
    off = 0
    while off < fd:
        zcol = qlo + off
        bh = 0 if zcol < 512 else 1
        bo = zcol % 512
        seg = min(512 - bo, fd - off)
        out.append((bh, bo, off, seg))
        off += seg
    return out


def build():
    nc = bass.Bass("TRN2", target_bir_lowering=False, detect_race_conditions=False)
    c = Ctx()
    c.nc = nc
    p = P(nc)
    c.p = p

    c.x_in = nc.dram_tensor("x_in", [NT, 1024], F32, kind="ExternalInput").ap()
    c.wuv_in = nc.dram_tensor("wuv_in", [NER, 2, 1024, 128], F32, kind="ExternalInput").ap()
    c.wo_in = nc.dram_tensor("wo_in", [ND, 2048, 128], F32, kind="ExternalInput").ap()
    c.wkv1_in = nc.dram_tensor("wkv1_in", [1024, 128], F32, kind="ExternalInput").ap()
    c.wq_in = nc.dram_tensor("wq_in", [1024, 64], F32, kind="ExternalInput").ap()
    c.wv2_in = nc.dram_tensor("wv2_in", [64, 1024], F32, kind="ExternalInput").ap()
    c.gw_in = nc.dram_tensor("gw_in", [1024, 8], F32, kind="ExternalInput").ap()
    c.masks_in = nc.dram_tensor("masks_in", [16, 128, 128], BF16, kind="ExternalInput").ap()
    c.out = nc.dram_tensor("out", [NT, 1024], F32, kind="ExternalOutput").ap()
    c.wuv_hl = nc.dram_tensor("wuv_hl", [NER, 2, 1024, 256], BF16).ap()
    c.wo_hl = nc.dram_tensor("wo_hl", [ND, 2048, 256], BF16).ap()
    c.cc_in_t = nc.dram_tensor("cc_in", [NT, 256], BF16)
    c.cc_out_t = nc.dram_tensor("cc_out", [8 * NT, 256], BF16)
    c.cc_in = c.cc_in_t.ap()
    c.cc_out = c.cc_out_t.ap()

    with ExitStack() as st:
        sb = lambda name, shape, dt=F32: st.enter_context(nc.sbuf_tensor(name, shape, dt))
        c.xres = [sb(f"xres{j}", [128, NT]) for j in range(ND)]
        c.xln_h = sb("xln_h", [128, ND * TKB], BF16)
        c.xln_l = sb("xln_l", [128, ND * TKB], BF16)
        c.ident = sb("ident", [128, 128])
        c.onesf = sb("onesf", [1, 128])
        c.onesb = sb("onesb", [1, 128], BF16)
        c.cpool = sb("cpool", [128, 16])
        c.cpool_i = sb("cpool_i", [128, 2], I32)
        c.stat_tm = sb("stat_tm", [128, 2 * NTILE])
        c.rowbuf = sb("rowbuf", [2, TKB])
        c.ln_scr = sb("ln_scr", [128, 512])
        c.ln_t1 = sb("ln_t1", [128, TKB])
        c.ln_t2 = sb("ln_t2", [128, TKB])
        c.gbuf = sb("gbuf", [8, TKB])
        c.g_tm = sb("g_tm", [128, 32])
        c.g_rows = sb("g_rows", [8, TKB])
        c.grh = sb("grh", [8, TKB], BF16)
        c.grl = sb("grl", [8, TKB], BF16)
        c.dbuf = sb("dbuf", [128, 4])
        c.wbuf2 = sb("wbuf2", [128, 4])
        c.gsc = sb("gsc", [128, 16])
        c.gsc2 = sb("gsc2", [128, 16])
        c.gt = [sb(f"gt{i}", [128, 8]) for i in range(4)]
        c.gt2 = [sb(f"gt2_{i}", [128, 8]) for i in range(4)]
        c.selb = sb("selb", [8, 8 * 128], BF16)
        c.self32 = sb("self32", [2, 2 * 128])
        c.wq_h = sb("wq_h", [128, ND * 64], BF16)
        c.wq_l = sb("wq_l", [128, ND * 64], BF16)
        c.wkv_h = sb("wkv_h", [128, ND * 128], BF16)
        c.wkv_l = sb("wkv_l", [128, ND * 128], BF16)
        c.wv2_h = sb("wv2_h", [64, 1024], BF16)
        c.wv2_l = sb("wv2_l", [64, 1024], BF16)
        c.gw_h = sb("gw_h", [128, ND * 8], BF16)
        c.gw_l = sb("gw_l", [128, ND * 8], BF16)
        c.sil = [sb(f"sil{i}", [128, TKB]) for i in range(2)]
        c.s_t1 = sb("s_t1", [128, TKB])
        c.s_t2 = sb("s_t2", [128, TKB])

        c.pb = [st.enter_context(nc.psum_tensor(f"pb{i}", [128, 512], F32)) for i in range(8)]

        S = lambda name: Sem(st, nc, name)
        c.sX = S("sX"); c.sT = S("sT"); c.sV = S("sV"); c.sC = S("sC")
        c.sMM = S("sMM"); c.sWA = S("sWA"); c.sWB = S("sWB"); c.sWHL = S("sWHL")
        c.sCC = S("sCC"); c.sOUT = S("sOUT"); c.sKV = S("sKV"); c.sG = S("sG")

        with ExitStack() as stP0:
            emit_p0(c, stP0)
            fence(c)
        emit_stats(c)
        with ExitStack() as stA:
            emit_ffn(c, stA, call=0)
            fence(c)
        emit_stats(c)
        with ExitStack() as stB:
            emit_p2(c, stB)
            emit_p3(c, stB)
            fence(c)
        emit_stats(c)
        with ExitStack() as stC:
            emit_ffn(c, stC, call=1)
            emit_p5(c, stC)
        p.emit()
    return nc


def emit_p0(c, st):
    nc, p = c.nc, c.p
    sb = lambda name, shape, dt=F32: st.enter_context(nc.sbuf_tensor(name, shape, dt))
    xtm = [sb(f"xtm{t}", [128, 1024]) for t in range(NTILE)]
    c.wsm = sb("wsm", [128, 1024])
    iota_c = sb("iota_c", [128, 128])
    iota_p = sb("iota_p", [128, 1])
    iota_e8 = sb("iota_e8", [8, 8 * 128])
    iota_e2 = sb("iota_e2", [2, 2 * 128])

    def g_consts(e):
        e.iota(iota_c[:], [[1, 128]], channel_multiplier=0, allow_small_or_imprecise_dtypes=True)
        e.iota(iota_p[:], [[0, 1]], channel_multiplier=1, allow_small_or_imprecise_dtypes=True)
        e.memset(c.onesf[:], 1.0)
        e.memset(c.onesb[:], 1.0)
        for i, v in enumerate(CONSTS):
            e.memset(c.cpool[:, i:i + 1], v)
        e.iota(iota_e8[:].rearrange("p (b f) -> p b f", b=8), [[1, 8], [0, 128]],
               channel_multiplier=0, allow_small_or_imprecise_dtypes=True)
        e.iota(iota_e2[:].rearrange("p (b f) -> p b f", b=2), [[1, 2], [0, 128]],
               channel_multiplier=0, allow_small_or_imprecise_dtypes=True)
        return e.memset(c.onesb[:], 1.0)
    p.add("gpsimd", g_consts, c.sV, 1)
    consts_done = c.sV.n

    xld = []
    for t in range(NTILE):
        p.add("sync", lambda e, t=t: e.dma_start(
            out=xtm[t][:], in_=c.x_in[128 * t:128 * (t + 1), :]), c.sX, 16)
        xld.append(c.sX.n)
    smalls = [
        (lambda: c.wsm[:, 0:512].rearrange("p (cc e) -> p cc e", cc=8),
         lambda: c.wq_in.rearrange("(cc p) e -> p cc e", p=128), c.wq_h, c.wq_l, 128, 512),
        (lambda: c.wsm[:, 0:1024].rearrange("p (cc e) -> p cc e", cc=8),
         lambda: c.wkv1_in.rearrange("(cc p) e -> p cc e", p=128), c.wkv_h, c.wkv_l, 128, 1024),
        (lambda: c.wsm[:, 0:64].rearrange("p (cc e) -> p cc e", cc=8),
         lambda: c.gw_in.rearrange("(cc p) e -> p cc e", p=128), c.gw_h, c.gw_l, 128, 64),
        (lambda: c.wsm[:64, 0:1024], lambda: c.wv2_in, c.wv2_h, c.wv2_l, 64, 1024),
    ]
    for i, (dst, src, hi, lo, rows, cols) in enumerate(smalls):
        p.add("sync", lambda e, dst=dst, src=src: e.dma_start(out=dst(), in_=src()), c.sX, 16)
        ld = c.sX.n
        p.wait("vector", c.sX, ld)
        p.add("vector", lambda e, hi=hi, rows=rows, cols=cols: e.tensor_copy(
            hi[:], c.wsm[:rows, 0:cols]))
        p.add("vector", lambda e, lo=lo, hi=hi, rows=rows, cols=cols: e.tensor_sub(
            lo[:], c.wsm[:rows, 0:cols], hi[:]), c.sV, 1)
        if i + 1 < len(smalls):
            p.wait("sync", c.sV, c.sV.n)

    p.wait("vector", c.sV, consts_done)
    p.add("vector", lambda e: e.tensor_scalar(
        c.ident[:], iota_c[:], iota_p[:], None, ALU.is_equal))
    p.add("vector", lambda e: e.tensor_scalar(
        c.selb[:], iota_e8[:], iota_p[0:8, :], None, ALU.is_equal))
    p.add("vector", lambda e: e.tensor_scalar(
        c.self32[:], iota_e2[:], iota_p[0:2, :], None, ALU.is_equal), c.sV, 1)
    ident_done = c.sV.n

    p.wait("tensor", c.sV, ident_done)
    evac = {}
    for g in range(NTB * ND):
        tb, j = divmod(g, ND)
        bank = c.pb[g % 4]
        if g >= 4:
            p.wait("tensor", c.sV, evac[g - 4])
        for t in range(4):
            tile = 4 * tb + t
            p.wait("tensor", c.sX, xld[tile])

            def tr(e, bank=bank, t=t, tile=tile, j=j):
                return e.transpose(bank[:, 128 * t:128 * (t + 1)],
                                   xtm[tile][:, 128 * j:128 * (j + 1)], c.ident[:])
            p.add("tensor", tr, c.sT if t == 3 else None, 1)
        grp = c.sT.n
        p.wait("vector", c.sT, grp)
        p.add("vector", lambda e, tb=tb, j=j, bank=bank: e.tensor_copy(
            c.xres[j][:, TKB * tb:TKB * (tb + 1)], bank[:]), c.sV, 1)
        evac[g] = c.sV.n


def emit_stats(c):
    p = c.p
    p.wait("tensor", c.sV, c.sV.n)
    p.wait("tensor", c.sC, c.sC.n)
    vec_done = {}
    for tp in range(NTILE // 2):
        ta, tb_ = 2 * tp, 2 * tp + 1
        if tp % 2 == 0:
            bA, bB = (c.pb[6], c.pb[7]), (c.pb[4], c.pb[5])
        else:
            bA, bB = (c.pb[0], c.pb[1]), (c.pb[2], c.pb[3])
        if tp >= 2:
            p.wait("tensor", c.sV, vec_done[tp - 2])
            p.wait("tensor", c.sC, c.sC.n - 4)
        for t, banks in ((ta, bA), (tb_, bB)):
            for j in range(ND):
                bnk = banks[0] if j < 4 else banks[1]

                def tr(e, bnk=bnk, j=j, t=t):
                    return e.transpose(bnk[:, 128 * (j % 4):128 * (j % 4 + 1)],
                                       c.xres[j][:, 128 * t:128 * (t + 1)], c.ident[:])
                p.add("tensor", tr, c.sT if j == ND - 1 else None, 1)
        grp = c.sT.n
        p.wait("scalar", c.sT, grp)
        p.add("scalar", lambda e, bA=bA: e.activation(
            c.ln_scr[:, 0:512], bA[0][:], ACT.Square,
            accum_out=c.gsc[:, 8:9]))
        p.add("scalar", lambda e, bA=bA: e.activation(
            c.ln_scr[:, 0:512], bA[1][:], ACT.Square,
            accum_out=c.gsc[:, 9:10]), c.sC, 1)
        p.add("scalar", lambda e, bB=bB: e.activation(
            c.ln_scr[:, 0:512], bB[0][:], ACT.Square,
            accum_out=c.gsc2[:, 8:9]))
        p.add("scalar", lambda e, bB=bB: e.activation(
            c.ln_scr[:, 0:512], bB[1][:], ACT.Square,
            accum_out=c.gsc2[:, 9:10]), c.sC, 1)
        sq_done = c.sC.n
        p.wait("vector", c.sT, grp)
        p.wait("vector", c.sC, sq_done)
        p.add("vector", lambda e, bA=bA: e.reduce_sum(c.gsc[:, 0:1], bA[0][:], axis=AXX))
        p.add("vector", lambda e, bB=bB: e.reduce_sum(c.gsc2[:, 0:1], bB[0][:], axis=AXX))
        p.add("vector", lambda e, bA=bA: e.reduce_sum(c.gsc[:, 1:2], bA[1][:], axis=AXX))
        p.add("vector", lambda e, bB=bB: e.reduce_sum(c.gsc2[:, 1:2], bB[1][:], axis=AXX))

        def stats_ab(e):
            cp = c.cpool
            A, B = c.gsc, c.gsc2
            for g in (A, B):
                e.tensor_scalar(g[:, 2:3], g[:, 0:1], g[:, 1:2], None, ALU.add)
            for g in (A, B):
                e.tensor_scalar(g[:, 3:4], g[:, 8:9], g[:, 9:10], None, ALU.add)
            for g in (A, B):
                e.tensor_scalar(g[:, 4:5], g[:, 2:3], cp[:, C_INV1024:C_INV1024 + 1],
                                None, ALU.mult)
            for g in (A, B):
                e.tensor_scalar(g[:, 5:6], g[:, 3:4], cp[:, C_INV1024:C_INV1024 + 1],
                                cp[:, C_EPS:C_EPS + 1], ALU.mult, ALU.add)
            for g in (A, B):
                e.tensor_scalar(g[:, 6:7], g[:, 4:5], g[:, 4:5], None, ALU.mult)
            for g in (A, B):
                return_ins = e.tensor_scalar(g[:, 7:8], g[:, 5:6], g[:, 6:7],
                                             None, ALU.subtract)
            return return_ins
        p.add("vector", stats_ab, c.sV, 1)
        v_ready = c.sV.n
        p.wait("scalar", c.sV, v_ready)
        p.add("scalar", lambda e: e.activation(c.gsc[:, 10:11], c.gsc[:, 7:8], ACT.Sqrt))
        p.add("scalar", lambda e: e.activation(c.gsc2[:, 10:11], c.gsc2[:, 7:8],
                                               ACT.Sqrt), c.sC, 1)
        sqrt_done = c.sC.n
        p.wait("vector", c.sC, sqrt_done)

        def newton_ab(e, ta=ta, tb_=tb_):
            cp = c.cpool
            A, B = c.gsc, c.gsc2
            e.reciprocal(A[:, 15:16], A[:, 10:11])
            e.reciprocal(B[:, 15:16], B[:, 10:11])
            for _ in range(2):
                for g in (A, B):
                    e.tensor_scalar(g[:, 11:12], g[:, 15:16], g[:, 15:16], None, ALU.mult)
                for g in (A, B):
                    e.tensor_scalar(g[:, 12:13], g[:, 11:12], g[:, 7:8], None, ALU.mult)
                for g in (A, B):
                    e.tensor_scalar(g[:, 13:14], g[:, 12:13], cp[:, C_NHALF:C_NHALF + 1],
                                    cp[:, C_3HALF:C_3HALF + 1], ALU.mult, ALU.add)
                for g in (A, B):
                    e.tensor_scalar(g[:, 15:16], g[:, 15:16], g[:, 13:14], None, ALU.mult)
            e.tensor_copy(c.stat_tm[:, 2 * ta:2 * ta + 1], A[:, 15:16])
            e.tensor_copy(c.stat_tm[:, 2 * tb_:2 * tb_ + 1], B[:, 15:16])
            e.tensor_scalar(c.stat_tm[:, 2 * ta + 1:2 * ta + 2], A[:, 15:16], A[:, 4:5],
                            None, ALU.mult)
            return e.tensor_scalar(c.stat_tm[:, 2 * tb_ + 1:2 * tb_ + 2], B[:, 15:16],
                                   B[:, 4:5], None, ALU.mult)
        p.add("vector", newton_ab, c.sV, 1)
        vec_done[tp] = c.sV.n
    c.stats_done = c.sV.n


def ln_block(c, tb):
    p = c.p
    p.wait("tensor", c.sV, c.sV.n)
    p.wait("tensor", c.sC, c.sC.n)
    for t in range(4):
        tile = 4 * tb + t

        def tr(e, t=t, tile=tile):
            return e.transpose(c.pb[5][0:2, 128 * t:128 * (t + 1)],
                               c.stat_tm[:, 2 * tile:2 * tile + 2], c.ident[:])
        p.add("tensor", tr, c.sT if t == 3 else None, 1)
    p.wait("vector", c.sT, c.sT.n)
    p.add("vector", lambda e: e.tensor_copy(c.rowbuf[:], c.pb[5][0:2, 0:TKB]), c.sV, 1)
    p.wait("tensor", c.sV, c.sV.n)
    p.add("tensor", lambda e: e.matmul(c.pb[4][:], c.self32[:, 0:128], c.rowbuf[:],
                                       start=True, stop=True))
    p.add("tensor", lambda e: e.matmul(c.pb[6][:], c.self32[:, 128:256], c.rowbuf[:],
                                       start=True, stop=True), c.sT, 1)
    p.wait("vector", c.sT, c.sT.n)
    for j in range(ND):
        def apply(e, j=j, tb=tb):
            xs = c.xres[j][:, TKB * tb:TKB * (tb + 1)]
            e.tensor_mul(c.ln_t1[:], xs, c.pb[4][:])
            e.tensor_sub(c.ln_t2[:], c.ln_t1[:], c.pb[6][:])
            e.tensor_copy(c.xln_h[:, TKB * j:TKB * (j + 1)], c.ln_t2[:])
            return e.tensor_sub(c.xln_l[:, TKB * j:TKB * (j + 1)], c.ln_t2[:],
                                c.xln_h[:, TKB * j:TKB * (j + 1)])
        p.add("vector", apply, c.sV if j == ND - 1 else None, 1)
    return c.sV.n


def emit_ffn(c, st, call):
    nc, p = c.nc, c.p
    sb = lambda name, shape, dt=F32: st.enter_context(nc.sbuf_tensor(name, shape, dt))
    s_h = sb(f"s_h{call}", [128, NER * TKB], BF16)
    s_l = sb(f"s_l{call}", [128, NER * TKB], BF16)
    wuvc = [sb(f"wuvc{call}_{i}", [128, 4096], BF16) for i in range(2)]
    woc = [sb(f"woc{call}_{i}", [128, 4096], BF16) for i in range(2)]
    fresh = (call == 0)
    if fresh:
        wst = [sb(f"wst{i}", [128, 2048]) for i in range(2)]
        wost = wst

    uv_consumed = {}
    y_consumed = {}
    schain_done = {}
    yadd_done = {}
    split_w = {}
    split_wo = {}
    whl_written = {}
    wohl_written = {}
    wa_ld = {}
    wb_ld = {}

    for tb in range(NTB):
        xln_ready = ln_block(c, tb)

        p.wait("tensor", c.sV, xln_ready)

        def gate_mm(e):
            first = True
            ins = None
            for cc in range(ND):
                gh = c.gw_h[:, 8 * cc:8 * (cc + 1)]
                gl = c.gw_l[:, 8 * cc:8 * (cc + 1)]
                xh = c.xln_h[:, TKB * cc:TKB * (cc + 1)]
                xl = c.xln_l[:, TKB * cc:TKB * (cc + 1)]
                e.matmul(c.pb[7][0:8, :], gh, xh, start=first, stop=False); first = False
                e.matmul(c.pb[7][0:8, :], gh, xl, start=False, stop=False)
                ins = e.matmul(c.pb[7][0:8, :], gl, xh, start=False, stop=(cc == ND - 1))
            return ins
        p.add("tensor", gate_mm, c.sMM, 1)
        p.wait("vector", c.sMM, c.sMM.n)
        p.add("vector", lambda e: e.tensor_copy(c.gbuf[:], c.pb[7][0:8, :]), c.sV, 1)
        p.wait("tensor", c.sV, c.sV.n)
        for t in range(4):
            def tr(e, t=t):
                return e.transpose(c.pb[5][:, 8 * t:8 * (t + 1)],
                                   c.gbuf[0:8, 128 * t:128 * (t + 1)], c.ident[0:8, 0:8])
            p.add("tensor", tr, c.sT if t == 3 else None, 1)
        p.wait("vector", c.sT, c.sT.n)
        p.add("vector", lambda e: e.tensor_copy(c.g_tm[:, 0:16], c.pb[5][:, 0:16]))
        p.add("vector", lambda e: e.tensor_copy(c.g_tm[:, 16:32], c.pb[5][:, 16:32]))

        def gate_math1(e):
            cp = c.cpool
            ins = None
            for tp in range(2):
                tA, tB = 2 * tp, 2 * tp + 1
                lgA = c.g_tm[:, 8 * tA:8 * (tA + 1)]
                lgB = c.g_tm[:, 8 * tB:8 * (tB + 1)]
                A, B = c.gsc, c.gsc2
                gA, gB = c.gt, c.gt2
                e.reduce_max(A[:, 0:1], lgA, axis=AXX)
                e.reduce_max(B[:, 0:1], lgB, axis=AXX)
                e.tensor_scalar(gA[0][:], lgA, A[:, 0:1], None, ALU.subtract)
                e.tensor_scalar(gB[0][:], lgB, B[:, 0:1], None, ALU.subtract)
                e.tensor_scalar(gA[1][:], gA[0][:], cp[:, C_ZERO:C_ZERO + 1], None, ALU.is_equal)
                e.tensor_scalar(gB[1][:], gB[0][:], cp[:, C_ZERO:C_ZERO + 1], None, ALU.is_equal)
                e.scalar_tensor_tensor(gA[2][:], gA[1][:], cp[:, C_BIGNEG:C_BIGNEG + 1],
                                       gA[0][:], ALU.mult, ALU.add)
                e.scalar_tensor_tensor(gB[2][:], gB[1][:], cp[:, C_BIGNEG:C_BIGNEG + 1],
                                       gB[0][:], ALU.mult, ALU.add)
                e.reduce_max(c.dbuf[:, tA:tA + 1], gA[2][:], axis=AXX)
                e.reduce_max(c.dbuf[:, tB:tB + 1], gB[2][:], axis=AXX)
                e.tensor_scalar(gA[3][:], gA[2][:], c.dbuf[:, tA:tA + 1], None, ALU.is_equal)
                e.tensor_scalar(gB[3][:], gB[2][:], c.dbuf[:, tB:tB + 1], None, ALU.is_equal)
                e.tensor_copy(c.ln_t1[:, 16 * tA:16 * tA + 8], gA[1][:])
                e.tensor_copy(c.ln_t1[:, 16 * tB:16 * tB + 8], gB[1][:])
                e.tensor_copy(c.ln_t1[:, 16 * tA + 8:16 * tA + 16], gA[3][:])
                ins = e.tensor_copy(c.ln_t1[:, 16 * tB + 8:16 * tB + 16], gB[3][:])
            return ins
        p.add("vector", gate_math1, c.sV, 1)
        p.wait("scalar", c.sV, c.sV.n)
        p.add("scalar", lambda e: e.activation(c.wbuf2[:], c.dbuf[:], ACT.Sigmoid), c.sC, 1)
        p.wait("vector", c.sC, c.sC.n)

        def gate_math2(e):
            cp = c.cpool
            ins = None
            for tp in range(2):
                tA, tB = 2 * tp, 2 * tp + 1
                A, B = c.gsc, c.gsc2
                gA, gB = c.gt, c.gt2
                w2A = c.wbuf2[:, tA:tA + 1]
                w2B = c.wbuf2[:, tB:tB + 1]
                e.tensor_scalar(A[:, 0:1], w2A, cp[:, C_NONE:C_NONE + 1],
                                cp[:, C_ONE:C_ONE + 1], ALU.mult, ALU.add)
                e.tensor_scalar(B[:, 0:1], w2B, cp[:, C_NONE:C_NONE + 1],
                                cp[:, C_ONE:C_ONE + 1], ALU.mult, ALU.add)
                e.tensor_scalar(gA[0][:], c.ln_t1[:, 16 * tA:16 * tA + 8], A[:, 0:1],
                                None, ALU.mult)
                e.tensor_scalar(gB[0][:], c.ln_t1[:, 16 * tB:16 * tB + 8], B[:, 0:1],
                                None, ALU.mult)
                e.scalar_tensor_tensor(c.g_tm[:, 8 * tA:8 * (tA + 1)],
                                       c.ln_t1[:, 16 * tA + 8:16 * tA + 16], w2A,
                                       gA[0][:], ALU.mult, ALU.add)
                ins = e.scalar_tensor_tensor(c.g_tm[:, 8 * tB:8 * (tB + 1)],
                                             c.ln_t1[:, 16 * tB + 8:16 * tB + 16], w2B,
                                             gB[0][:], ALU.mult, ALU.add)
            return ins
        p.add("vector", gate_math2, c.sV, 1)
        p.wait("tensor", c.sV, c.sV.n)
        for t in range(4):
            def tr(e, t=t):
                return e.transpose(c.pb[5][0:8, 128 * t:128 * (t + 1)],
                                   c.g_tm[:, 8 * t:8 * (t + 1)], c.ident[:])
            p.add("tensor", tr, c.sT if t == 3 else None, 1)
        p.wait("vector", c.sT, c.sT.n)
        p.add("vector", lambda e: e.tensor_copy(c.g_rows[:], c.pb[5][0:8, 0:TKB]))
        p.add("vector", lambda e: e.tensor_copy(c.grh[:], c.g_rows[:]))
        p.add("vector", lambda e: e.tensor_sub(c.grl[:], c.g_rows[:], c.grh[:]), c.sV, 1)
        grows_done = c.sV.n

        for k in range(NER):
            gch = tb * NER + k
            slot = gch % 2
            e_idx = k // 2
            if fresh and tb == 0:
                if gch >= 2:
                    p.wait("sync", c.sV, split_w[gch - 2])
                p.add("sync", lambda e, k=k, slot=slot: e.dma_start(
                    out=wst[slot][:].rearrange("p (u cc e) -> p u cc e", u=2, cc=8),
                    in_=c.wuv_in[k].rearrange("u (cc p) e -> p u cc e", p=128)), c.sWA, 16)
                wa_ld[gch] = c.sWA.n
                p.wait("vector", c.sWA, wa_ld[gch])
                if gch >= 2:
                    p.wait("vector", c.sMM, uv_consumed[gch - 2])
                    p.wait("vector", c.sWHL, whl_written[gch - 2])

                def spl(e, slot=slot):
                    wv = wst[slot][:].rearrange("p (u cc e) -> p u cc e", u=2, cc=8)
                    comb = wuvc[slot][:].rearrange("p (u cc x) -> p u cc x", u=2, cc=8)
                    e.tensor_copy(comb[:, :, :, 0:128], wv)
                    return e.tensor_sub(comb[:, :, :, 128:256], wv, comb[:, :, :, 0:128])
                p.add("vector", spl, c.sV, 1)
                split_w[gch] = c.sV.n
                p.wait("scalar", c.sV, split_w[gch])
                p.add("scalar", lambda e, k=k, slot=slot: e.dma_start(
                    out=c.wuv_hl[k].rearrange("u (cc p) x -> p u cc x", p=128),
                    in_=wuvc[slot][:].rearrange("p (u cc x) -> p u cc x", u=2, cc=8)),
                    c.sWHL, 16)
                whl_written[gch] = c.sWHL.n
            else:
                if fresh and tb == 1 and k < 2:
                    p.wait("sync", c.sWHL, whl_written[NER - 1])
                if gch >= 2 and (gch - 2) in uv_consumed:
                    p.wait("sync", c.sMM, uv_consumed[gch - 2])
                p.add("sync", lambda e, k=k, slot=slot: e.dma_start(
                    out=wuvc[slot][:].rearrange("p (u cc x) -> p u cc x", u=2, cc=8),
                    in_=c.wuv_hl[k].rearrange("u (cc p) x -> p u cc x", p=128)), c.sWA, 16)
                wa_ld[gch] = c.sWA.n

            gb = c.pb[4] if e_idx % 2 == 0 else c.pb[6]
            bu = c.pb[(k % 2) * 2]
            bv = c.pb[(k % 2) * 2 + 1]
            p.wait("tensor", c.sWA, wa_ld[gch])
            if fresh and tb == 0:
                p.wait("tensor", c.sV, split_w[gch])
            if gch >= 2 and (gch - 2) in schain_done:
                p.wait("tensor", c.sV, schain_done[gch - 2])

            def uvmm(e, slot=slot, bu=bu, bv=bv):
                ins = None
                for uv, bank in ((0, bu), (1, bv)):
                    first = True
                    for cc in range(ND):
                        off = (uv * ND + cc) * 256
                        wh = wuvc[slot][:, off:off + 128]
                        wl = wuvc[slot][:, off + 128:off + 256]
                        xh = c.xln_h[:, TKB * cc:TKB * (cc + 1)]
                        xl = c.xln_l[:, TKB * cc:TKB * (cc + 1)]
                        e.matmul(bank[:], wh, xh, start=first, stop=False); first = False
                        e.matmul(bank[:], wh, xl, start=False, stop=False)
                        ins = e.matmul(bank[:], wl, xh, start=False, stop=(cc == ND - 1))
                return ins
            p.add("tensor", uvmm, c.sMM, 1)
            uv_consumed[gch] = c.sMM.n

            if k % 2 == 0:
                p.wait("tensor", c.sV, grows_done)
                if gch - 3 in schain_done:
                    p.wait("tensor", c.sV, schain_done[gch - 3])

                def gbc(e, gb=gb, e_idx=e_idx):
                    sel = c.selb[:, 128 * e_idx:128 * (e_idx + 1)]
                    e.matmul(gb[:], sel, c.grh[:], start=True, stop=False)
                    return e.matmul(gb[:], sel, c.grl[:], start=False, stop=True)
                p.add("tensor", gbc, c.sT, 1)
            gbc_done = c.sT.n

            p.wait("scalar", c.sMM, uv_consumed[gch])
            p.add("scalar", lambda e, k=k, bu=bu: e.activation(
                c.sil[k % 2][:], bu[:], ACT.Silu), c.sC, 1)
            sil_done = c.sC.n

            p.wait("vector", c.sC, sil_done)
            p.wait("vector", c.sT, gbc_done)

            def schain(e, k=k, bv=bv, gb=gb):
                e.tensor_mul(c.s_t1[:], c.sil[k % 2][:], bv[:])
                e.tensor_mul(c.s_t2[:], c.s_t1[:], gb[:])
                e.tensor_copy(s_h[:, TKB * k:TKB * (k + 1)], c.s_t2[:])
                return e.tensor_sub(s_l[:, TKB * k:TKB * (k + 1)], c.s_t2[:],
                                    s_h[:, TKB * k:TKB * (k + 1)])
            p.add("vector", schain, c.sV, 1)
            schain_done[gch] = c.sV.n

        all_s = schain_done[tb * NER + NER - 1]
        for j in range(ND):
            gj = tb * ND + j
            slot = gj % 2
            if fresh and tb == 0:
                if gj < 2:
                    p.wait("sync", c.sV, split_w[14 + gj])
                else:
                    p.wait("sync", c.sV, split_wo[gj - 2])
                p.add("sync", lambda e, j=j, slot=slot: e.dma_start(
                    out=wost[slot][:].rearrange("p (cc e) -> p cc e", cc=16),
                    in_=c.wo_in[j].rearrange("(cc p) e -> p cc e", p=128)), c.sWB, 16)
                wb_ld[gj] = c.sWB.n
                p.wait("vector", c.sWB, wb_ld[gj])
                if gj >= 2:
                    p.wait("vector", c.sMM, y_consumed[gj - 2])
                    p.wait("vector", c.sWHL, wohl_written[gj - 2])

                def splo(e, slot=slot):
                    wv = wost[slot][:].rearrange("p (cc e) -> p cc e", cc=16)
                    comb = woc[slot][:].rearrange("p (cc x) -> p cc x", cc=16)
                    e.tensor_copy(comb[:, :, 0:128], wv)
                    return e.tensor_sub(comb[:, :, 128:256], wv, comb[:, :, 0:128])
                p.add("vector", splo, c.sV, 1)
                split_wo[gj] = c.sV.n
                p.wait("scalar", c.sV, split_wo[gj])
                p.add("scalar", lambda e, j=j, slot=slot: e.dma_start(
                    out=c.wo_hl[j].rearrange("(cc p) x -> p cc x", p=128),
                    in_=woc[slot][:].rearrange("p (cc x) -> p cc x", cc=16)), c.sWHL, 16)
                wohl_written[gj] = c.sWHL.n
            else:
                if fresh and tb == 1 and j < 2:
                    p.wait("sync", c.sWHL, wohl_written[ND - 1])
                if gj >= 2 and (gj - 2) in y_consumed:
                    p.wait("sync", c.sMM, y_consumed[gj - 2])
                p.add("sync", lambda e, j=j, slot=slot: e.dma_start(
                    out=woc[slot][:].rearrange("p (cc x) -> p cc x", cc=16),
                    in_=c.wo_hl[j].rearrange("(cc p) x -> p cc x", p=128)), c.sWB, 16)
                wb_ld[gj] = c.sWB.n

            by = c.pb[j % 2]
            p.wait("tensor", c.sV, all_s)
            p.wait("tensor", c.sWB, wb_ld[gj])
            if fresh and tb == 0:
                p.wait("tensor", c.sV, split_wo[gj])
            if gj >= 2 and (gj - 2) in yadd_done:
                p.wait("tensor", c.sV, yadd_done[gj - 2])

            def ymm(e, slot=slot, by=by):
                first = True
                ins = None
                for k in range(NER):
                    off = k * 256
                    wh = woc[slot][:, off:off + 128]
                    wl = woc[slot][:, off + 128:off + 256]
                    sh_ = s_h[:, TKB * k:TKB * (k + 1)]
                    sl_ = s_l[:, TKB * k:TKB * (k + 1)]
                    e.matmul(by[:], wh, sh_, start=first, stop=False); first = False
                    e.matmul(by[:], wh, sl_, start=False, stop=False)
                    ins = e.matmul(by[:], wl, sh_, start=False, stop=(k == NER - 1))
                return ins
            p.add("tensor", ymm, c.sMM, 1)
            y_consumed[gj] = c.sMM.n

            p.wait("vector", c.sMM, y_consumed[gj])

            def yadd(e, j=j, tb=tb, by=by):
                xs = c.xres[j][:, TKB * tb:TKB * (tb + 1)]
                return e.scalar_tensor_tensor(xs, by[:], c.cpool[:, C_HALF:C_HALF + 1],
                                              xs, ALU.mult, ALU.add)
            p.add("vector", yadd, c.sV, 1)
            yadd_done[gj] = c.sV.n


def emit_p2(c, st):
    nc, p = c.nc, c.p
    sb = lambda name, shape, dt=F32: st.enter_context(nc.sbuf_tensor(name, shape, dt))
    c.qk_h = sb("qk_h", [128, NT], BF16)
    c.qk_l = sb("qk_l", [128, NT], BF16)
    xvstage = sb("xvstage", [128, 4 * 128], BF16)
    kv_base = c.sKV.n

    for tb in range(NTB):
        xln_ready = ln_block(c, tb)
        p.wait("tensor", c.sV, xln_ready)

        def qkmm(e):
            first = True
            ins = None
            for cc in range(ND):
                xh = c.xln_h[:, TKB * cc:TKB * (cc + 1)]
                xl = c.xln_l[:, TKB * cc:TKB * (cc + 1)]
                qh = c.wq_h[:, 64 * cc:64 * (cc + 1)]
                ql = c.wq_l[:, 64 * cc:64 * (cc + 1)]
                kh = c.wkv_h[:, 128 * cc:128 * cc + 64]
                kl = c.wkv_l[:, 128 * cc:128 * cc + 64]
                e.matmul(c.pb[7][0:64, :], qh, xh, start=first, stop=False)
                e.matmul(c.pb[6][0:64, :], kh, xh, start=first, stop=False)
                first = False
                e.matmul(c.pb[7][0:64, :], qh, xl, start=False, stop=False)
                e.matmul(c.pb[6][0:64, :], kh, xl, start=False, stop=False)
                e.matmul(c.pb[7][0:64, :], ql, xh, start=False, stop=(cc == ND - 1))
                ins = e.matmul(c.pb[6][0:64, :], kl, xh, start=False, stop=(cc == ND - 1))
            return ins
        p.add("tensor", qkmm, c.sMM, 1)
        p.wait("vector", c.sMM, c.sMM.n)

        def qk_split(e, tb=tb):
            cols = slice(TKB * tb, TKB * (tb + 1))
            e.tensor_copy(c.qk_h[0:64, cols], c.pb[7][0:64, :])
            e.tensor_copy(c.qk_h[64:128, cols], c.pb[6][0:64, :])
            e.tensor_sub(c.qk_l[0:64, cols], c.pb[7][0:64, :], c.qk_h[0:64, cols])
            return e.tensor_sub(c.qk_l[64:128, cols], c.pb[6][0:64, :], c.qk_h[64:128, cols])
        p.add("vector", qk_split, c.sV, 1)
        qk_done = c.sV.n

        xv_done = {}
        for t in range(4):
            bxv = c.pb[t % 2]

            def xvmm(e, t=t, bxv=bxv):
                first = True
                ins = None
                for cc in range(ND):
                    xh = c.xln_h[:, TKB * cc + 128 * t:TKB * cc + 128 * (t + 1)]
                    xl = c.xln_l[:, TKB * cc + 128 * t:TKB * cc + 128 * (t + 1)]
                    wh = c.wkv_h[:, 128 * cc + 64:128 * cc + 128]
                    wl = c.wkv_l[:, 128 * cc + 64:128 * cc + 128]
                    e.matmul(bxv[:, 0:64], xh, wh, start=first, stop=False); first = False
                    e.matmul(bxv[:, 0:64], xh, wl, start=False, stop=False)
                    ins = e.matmul(bxv[:, 0:64], xl, wh, start=False, stop=(cc == ND - 1))
                return ins
            if t >= 2:
                p.wait("tensor", c.sV, xv_done[t - 2])
            p.add("tensor", xvmm, c.sMM, 1)
            p.wait("vector", c.sMM, c.sMM.n)
            if t == 0 and tb > 0:
                p.wait("vector", c.sKV, kv_base + 16 * 4 * tb)

            def xv_split(e, t=t, bxv=bxv):
                base = 128 * t
                e.tensor_copy(xvstage[:, base:base + 64], bxv[:, 0:64])
                return e.tensor_sub(xvstage[:, base + 64:base + 128], bxv[:, 0:64],
                                    xvstage[:, base:base + 64])
            p.add("vector", xv_split, c.sV, 1)
            xv_done[t] = c.sV.n

        if True:
            p.wait("gpsimd", c.sV, qk_done)
            p.add("gpsimd", lambda e, tb=tb: e.dma_start(
                out=c.cc_in[64 * 4 * tb:64 * 4 * (tb + 1), 0:128]
                    .rearrange("(t p) f -> p t f", p=64),
                in_=c.qk_h[64:128, TKB * tb:TKB * (tb + 1)]
                    .rearrange("p (t f) -> p t f", t=4)), c.sKV, 16)
            p.add("gpsimd", lambda e, tb=tb: e.dma_start(
                out=c.cc_in[64 * 4 * tb:64 * 4 * (tb + 1), 128:256]
                    .rearrange("(t p) f -> p t f", p=64),
                in_=c.qk_l[64:128, TKB * tb:TKB * (tb + 1)]
                    .rearrange("p (t f) -> p t f", t=4)), c.sKV, 16)
            p.wait("gpsimd", c.sV, xv_done[3])
            for half in range(2):
                lt0 = 4 * tb + 2 * half

                def xvdma(e, lt0=lt0, half=half, tb=tb):
                    rows = slice(1024 + 128 * (lt0 // 2), 1024 + 128 * (lt0 // 2) + 128)
                    return e.dma_start(
                        out=c.cc_in[rows, :].rearrange("p (t f) -> p t f", t=2),
                        in_=xvstage[:, 256 * half:256 * (half + 1)]
                            .rearrange("p (t f) -> p t f", t=2))
                p.add("gpsimd", xvdma, c.sKV, 16)

    if True:
        p.wait("gpsimd", c.sKV, c.sKV.n)
        p.add("gpsimd", lambda e: e.collective_compute(
            "AllGather", ALU.bypass, replica_groups=[list(range(8))],
            ins=[c.cc_in_t.ap().opt()], outs=[c.cc_out_t.ap().opt()]), c.sCC, 1)


def emit_p3(c, st):
    nc, p = c.nc, c.p
    sb = lambda name, shape, dt=F32: st.enter_context(nc.sbuf_tensor(name, shape, dt))
    XVS = 72
    k_h = sb("k_h", [64, 64 * 128], BF16)
    k_l = sb("k_l", [64, 64 * 128], BF16)
    xvh = sb("xvh", [128, 64 * XVS], BF16)
    xvl = sb("xvl", [128, 64 * XVS], BF16)
    ebuf = [sb(f"ebuf{i}", [128, 1024]) for i in range(2)]
    eh = [sb(f"eh{i}", [128, 1024], BF16) for i in range(2)]
    el = [sb(f"el{i}", [128, 1024], BF16) for i in range(2)]
    aacc = sb("aacc", [65, 1024])
    aev = aacc
    c.masks = sb("masks", [128, 16 * 128], BF16)
    anh = sb("anh", [64, 1024], BF16)
    anl = sb("anl", [64, 1024], BF16)

    p.wait("sync", c.sCC, c.sCC.n)
    p.add("sync", lambda e: e.dma_start(
        out=c.masks[:].rearrange("p (k f) -> p k f", k=16),
        in_=c.masks_in.rearrange("k p f -> p k f")), c.sX, 16)
    masks_ld = c.sX.n

    def ones_cols(e):
        e.memset(xvh[:].rearrange("p (j x) -> p j x", j=64)[:, :, 64:72], 0.0)
        e.memset(xvl[:].rearrange("p (j x) -> p j x", j=64)[:, :, 64:72], 0.0)
        e.memset(xvh[:].rearrange("p (j x) -> p j x", j=64)[:, :, 64:65], 1.0)
        return e.memset(xvl[:].rearrange("p (j x) -> p j x", j=64)[:, :, 64:65], 0.0)
    p.add("gpsimd", ones_cols, c.sG, 1)
    ones_done = c.sG.n

    for h in range(2):
        if h == 1:
            p.wait("sync", c.sMM, c.sMM.n)
        owner_ready = {}
        for r in range(8):
            c0 = r if h == 0 else 7 - r
            for lo in (False, True):
                def kload(e, r=r, c0=c0, h=h, lo=lo):
                    dst = k_l if lo else k_h
                    cols = slice(128, 256) if lo else slice(0, 128)
                    src = c.cc_out[2048 * r + 512 * h:2048 * r + 512 * h + 512, cols]
                    src3 = src.rearrange("(i p) f -> p i f", p=64)
                    dst3 = dst[0:64, :].rearrange("p (j f) -> p j f", f=128)[:, c0:c0 + 57:8, :]
                    return e.dma_start(out=dst3, in_=src3)
                p.add("sync", kload, c.sX, 16)
            for par in (0, 1):
                for lo in (False, True):
                    def xvload(e, r=r, c0=c0, h=h, par=par, lo=lo):
                        dst = xvl if lo else xvh
                        rows0 = 2048 * r + 1024 + 128 * 4 * h
                        colh = 128 * par + (64 if lo else 0)
                        src = c.cc_out[rows0:rows0 + 512, colh:colh + 64]
                        src3 = src.rearrange("(b p) f -> p b f", p=128)
                        j0 = 8 * par + c0
                        dst3 = dst[:, XVS * j0:].rearrange("p (j x) -> p j x", x=XVS)[
                            :, 0:49:16, 0:64]
                        return e.dma_start(out=dst3, in_=src3)
                    p.add("sync", xvload, c.sX, 16)
            owner_ready[r] = c.sX.n

        e_split = {}
        e_done = {}
        a_after = {}
        for j in range(64):
            qlo = 128 * (j // 8)
            fd = 1024 - qlo
            zb = [(c.pb[4], c.pb[5]), (c.pb[6], c.pb[7]), (c.pb[2], c.pb[3])][j % 3]
            segs = segments(qlo, fd)
            r = (j % 8) if h == 0 else 7 - (j % 8)
            p.wait("tensor", c.sX, owner_ready[r])
            if j == 0:
                p.wait("tensor", c.sG, ones_done)
            if j >= 3:
                p.wait("tensor", c.sV, e_split[j - 3])

            def zmm(e, j=j, qlo=qlo, zb=zb, h=h, segs=segs):
                ins = None
                for pi, (lh, rh) in enumerate((("h", "h"), ("h", "l"), ("l", "h"))):
                    kv = k_h if lh == "h" else k_l
                    qv = c.qk_h if rh == "h" else c.qk_l
                    lhs = kv[0:64, 128 * j:128 * (j + 1)]
                    for (bh, bo, off, seg) in segs:
                        rhs = qv[0:64, 1024 * h + qlo + off:1024 * h + qlo + off + seg]
                        ins = e.matmul(zb[bh][:, bo:bo + seg], lhs, rhs,
                                       start=(pi == 0), stop=(pi == 2))
                return ins
            p.add("tensor", zmm, c.sMM, 1)
            z_done = c.sMM.n

            p.wait("vector", c.sMM, z_done)
            if h == 0 and j == 0:
                p.wait("vector", c.sX, masks_ld)

            def maskadd(e, j=j, qlo=qlo, zb=zb, h=h):
                bh = 0 if qlo < 512 else 1
                bo = qlo % 512
                mk = c.masks[:, 128 * (h * 8 + j % 8):128 * (h * 8 + j % 8 + 1)]
                t = zb[bh][:, bo:bo + 128]
                return e.tensor_add(t, t, mk)
            p.add("vector", maskadd, c.sV, 1)
            mask_done = c.sV.n

            p.wait("scalar", c.sV, mask_done)
            if j >= 2:
                p.wait("scalar", c.sV, e_split[j - 2])

            def eact(e, j=j, zb=zb, segs=segs):
                ins = None
                for (bh, bo, off, seg) in segs:
                    ins = e.activation(ebuf[j % 2][:, off:off + seg],
                                       zb[bh][:, bo:bo + seg], ACT.Exp,
                                       bias=c.cpool[:, C_NCSH:C_NCSH + 1],
                                       scale=c.cpool[:, C_SC:C_SC + 1])
                return ins
            p.add("scalar", eact, c.sC, 1)
            e_done[j] = c.sC.n
            p.wait("vector", c.sC, e_done[j])
            if j >= 2:
                p.wait("vector", c.sMM, a_after[j - 2])

            def esplit(e, j=j, fd=fd):
                hf = fd // 2
                e.tensor_copy(eh[j % 2][:, 0:hf], ebuf[j % 2][:, 0:hf])
                e.tensor_copy(eh[j % 2][:, hf:fd], ebuf[j % 2][:, hf:fd])
                e.tensor_sub(el[j % 2][:, 0:hf], ebuf[j % 2][:, 0:hf],
                             eh[j % 2][:, 0:hf])
                return e.tensor_sub(el[j % 2][:, hf:fd], ebuf[j % 2][:, hf:fd],
                                    eh[j % 2][:, hf:fd])
            p.add("vector", esplit, c.sV, 1)
            e_split[j] = c.sV.n

            p.wait("tensor", c.sV, e_split[j])
            blk_start = (j % 8 == 0)
            blk_end = (j % 8 == 7)
            if blk_start and j > 0:
                p.wait("tensor", c.sV, c.sV.n)

            def amm(e, j=j, qlo=qlo, segs=segs, blk_start=blk_start, blk_end=blk_end):
                ins = None
                for pi, (lh, rh) in enumerate((("h", "h"), ("h", "l"), ("l", "h"))):
                    xv = xvh if lh == "h" else xvl
                    ev = eh[j % 2] if rh == "h" else el[j % 2]
                    lhs = xv[:, XVS * j:XVS * j + 65]
                    for si, (bh, bo, off, seg) in enumerate(segs):
                        ins = e.matmul(c.pb[bh][0:65, bo:bo + seg], lhs,
                                       ev[:, off:off + seg],
                                       start=(blk_start and pi == 0),
                                       stop=(blk_end and pi == 2))
                return ins
            p.add("tensor", amm, c.sMM, 1)
            a_after[j] = c.sMM.n

            if blk_end:
                p.wait("vector", c.sMM, a_after[j])
                first_blk = (j < 8)
                qlo_b = 128 * ((j - 7) // 8)

                def aev_add(e, first_blk=first_blk, qlo_b=qlo_b):
                    if first_blk:
                        e.tensor_copy(aacc[:, 0:512], c.pb[0][0:65, :])
                        return e.tensor_copy(aacc[:, 512:1024], c.pb[1][0:65, :])
                    ins = None
                    for (bh, bo, off, seg) in segments(qlo_b, 1024 - qlo_b):
                        ins = e.tensor_add(aacc[:, qlo_b + off:qlo_b + off + seg],
                                           aacc[:, qlo_b + off:qlo_b + off + seg],
                                           c.pb[bh][0:65, bo:bo + seg])
                    return ins
                p.add("vector", aev_add, c.sV, 1)

        p.add("vector", lambda e: e.reciprocal(c.ln_t1[0:1, :], aev[64:65, 0:512]))
        p.add("vector", lambda e: e.reciprocal(c.ln_t2[0:1, :], aev[64:65, 512:1024]), c.sV, 1)
        p.wait("tensor", c.sV, c.sV.n)
        p.wait("tensor", c.sC, c.sC.n)
        p.add("tensor", lambda e: e.matmul(c.pb[2][0:64, :], c.onesf[:, 0:64],
                                           c.ln_t1[0:1, :], start=True, stop=True))
        p.add("tensor", lambda e: e.matmul(c.pb[3][0:64, :], c.onesf[:, 0:64],
                                           c.ln_t2[0:1, :], start=True, stop=True), c.sT, 1)
        p.wait("vector", c.sT, c.sT.n)

        def an_make(e):
            e.tensor_mul(aev[0:64, 0:512], aev[0:64, 0:512], c.pb[2][0:64, :])
            e.tensor_mul(aev[0:64, 512:1024], aev[0:64, 512:1024], c.pb[3][0:64, :])
            e.tensor_copy(anh[:], aev[0:64, :])
            return e.tensor_sub(anl[:], aev[0:64, :], anh[:])
        p.add("vector", an_make, c.sV, 1)
        an_done = c.sV.n
        p.wait("tensor", c.sV, an_done)

        out_add = {}
        for j in range(ND):
            ob = (c.pb[4], c.pb[5]) if j % 2 == 0 else (c.pb[6], c.pb[7])
            if j >= 2:
                p.wait("tensor", c.sV, out_add[j - 2])

            def omm(e, j=j, ob=ob):
                ins = None
                for pi, (lh, rh) in enumerate((("h", "h"), ("h", "l"), ("l", "h"))):
                    wv = c.wv2_h if lh == "h" else c.wv2_l
                    av = anh if rh == "h" else anl
                    lhs = wv[:, 128 * j:128 * (j + 1)]
                    for seg in range(2):
                        ins = e.matmul(ob[seg][:], lhs, av[:, 512 * seg:512 * (seg + 1)],
                                       start=(pi == 0), stop=(pi == 2))
                return ins
            p.add("tensor", omm, c.sMM, 1)
            og = c.sMM.n
            p.wait("vector", c.sMM, og)

            def oadd(e, j=j, ob=ob, h=h):
                xs0 = c.xres[j][:, 1024 * h:1024 * h + 512]
                xs1 = c.xres[j][:, 1024 * h + 512:1024 * h + 1024]
                e.tensor_add(xs0, xs0, ob[0][:])
                return e.tensor_add(xs1, xs1, ob[1][:])
            p.add("vector", oadd, c.sV, 1)
            out_add[j] = c.sV.n


def emit_p5(c, st, nbuf=2):
    nc, p = c.nc, c.p
    sb = lambda name, shape, dt=F32: st.enter_context(nc.sbuf_tensor(name, shape, dt))
    outstg = [sb(f"outstg{i}", [128, 1024]) for i in range(nbuf)]
    out_base = c.sOUT.n
    p.wait("tensor", c.sV, c.sV.n)
    p.wait("tensor", c.sC, c.sC.n)
    evac = {}
    for t in range(NTILE):
        b0 = c.pb[6] if t % 2 == 0 else c.pb[4]
        b1 = c.pb[7] if t % 2 == 0 else c.pb[5]
        if t >= 2:
            p.wait("tensor", c.sV, evac[t - 2])
        for j in range(ND):
            bnk = b0 if j < 4 else b1

            def tr(e, bnk=bnk, j=j, t=t):
                return e.transpose(bnk[:, 128 * (j % 4):128 * (j % 4 + 1)],
                                   c.xres[j][:, 128 * t:128 * (t + 1)], c.ident[:])
            p.add("tensor", tr, c.sT if j == ND - 1 else None, 1)
        grp = c.sT.n
        p.wait("vector", c.sT, grp)
        if t >= nbuf:
            p.wait("vector", c.sOUT, out_base + 16 * (t - nbuf + 1))

        def ev(e, t=t, b0=b0, b1=b1):
            e.tensor_copy(outstg[t % nbuf][:, 0:512], b0[:])
            return e.tensor_copy(outstg[t % nbuf][:, 512:1024], b1[:])
        p.add("vector", ev, c.sV, 1)
        evac[t] = c.sV.n
        p.wait("scalar", c.sV, evac[t])
        p.add("scalar", lambda e, t=t: e.dma_start(
            out=c.out[128 * t:128 * (t + 1), :], in_=outstg[t % nbuf][:]), c.sOUT, 16)
    p.wait("gpsimd", c.sOUT, out_base + 16 * NTILE)
    for s in (c.sX, c.sWA, c.sWB, c.sWHL, c.sKV):
        p.wait("gpsimd", s, s.n)


def host_prep(inputs):
    x = np.asarray(inputs["L0_attn_x"], dtype=np.float32)
    wq = np.ascontiguousarray(np.asarray(inputs["L0_attn_w_q"], dtype=np.float32))
    wk = np.asarray(inputs["L0_attn_w_k"], dtype=np.float32)
    wv1 = np.asarray(inputs["L0_attn_w_v1"], dtype=np.float32)
    wv2 = np.ascontiguousarray(np.asarray(inputs["L0_attn_w_v2"], dtype=np.float32))
    gw = np.ascontiguousarray(np.asarray(inputs["L0_ffn_gate_w"], dtype=np.float32))
    Wu = np.asarray(inputs["L0_ffn_Wu"], dtype=np.float32)
    Wv = np.asarray(inputs["L0_ffn_Wv"], dtype=np.float32)
    Wo = np.asarray(inputs["L0_ffn_Wo"], dtype=np.float32)

    Wu2 = Wu.transpose(1, 0, 2).reshape(1024, 2048)
    Wv2 = Wv.transpose(1, 0, 2).reshape(1024, 2048)
    wuv = np.empty((NER, 2, 1024, 128), dtype=np.float32)
    for k in range(NER):
        wuv[k, 0] = Wu2[:, 128 * k:128 * (k + 1)]
        wuv[k, 1] = Wv2[:, 128 * k:128 * (k + 1)]
    Wo2 = Wo.reshape(2048, 1024)
    wo = np.empty((ND, 2048, 128), dtype=np.float32)
    for j in range(ND):
        wo[j] = Wo2[:, 128 * j:128 * (j + 1)]
    wkv1 = np.ascontiguousarray(np.concatenate([wk, wv1], axis=1))

    tri = np.where(np.arange(128)[None, :] >= np.arange(128)[:, None], 0.0, -1e30
                   ).astype(np.float32)
    in_maps = []
    for core in range(8):
        t0 = [8 * i + core for i in range(8)]
        t1 = [8 * i + (7 - core) for i in range(8)]
        xs = np.concatenate([x[0, 128 * t:128 * (t + 1)] for t in t0]
                            + [x[1, 128 * t:128 * (t + 1)] for t in t1], axis=0)
        masks = np.zeros((16, 128, 128), dtype=np.float32)
        for half in range(2):
            pos = core if half == 0 else 7 - core
            for kk in range(8):
                if kk == pos:
                    masks[half * 8 + kk] = tri
                elif kk > pos:
                    masks[half * 8 + kk] = -1e30
        import ml_dtypes
        in_maps.append({
            "x_in": np.ascontiguousarray(xs),
            "wuv_in": wuv, "wo_in": wo, "wkv1_in": wkv1, "wq_in": wq,
            "wv2_in": wv2, "gw_in": gw,
            "masks_in": masks.astype(ml_dtypes.bfloat16),
        })
    return in_maps


def host_gather(results):
    out = np.empty((2, 8192, 1024), dtype=np.float32)
    for core in range(8):
        r = np.asarray(results[core]["out"])
        for i in range(8):
            t = 8 * i + core
            out[0, 128 * t:128 * (t + 1)] = r[128 * i:128 * (i + 1)]
            t = 8 * i + (7 - core)
            out[1, 128 * t:128 * (t + 1)] = r[1024 + 128 * i:1024 + 128 * (i + 1)]
    return out


def kernel(**inputs):
    in_maps = host_prep(inputs)
    nc = build()
    res = run_bass_kernel_spmd(nc, in_maps, core_ids=list(range(8)))
    return host_gather(res.results)


# version 03
# speedup vs baseline: 1.0401x; 1.0343x over previous
import numpy as np
from contextlib import ExitStack
import concourse.bass as bass
import concourse.mybir as mybir
from concourse.bass_utils import run_bass_kernel_spmd

F32 = mybir.dt.float32
BF16 = mybir.dt.bfloat16
I32 = mybir.dt.int32
ACT = mybir.ActivationFunctionType
ALU = mybir.AluOpType
AXX = mybir.AxisListType.X

ND = 8
NTB = 4
TKB = 512
NTILE = 16
NER = 16
NT = 2048
CSHIFT = 20.0
ENGINES = ("sync", "gpsimd", "tensor", "vector", "scalar")

C_NCSH, C_SC, C_HALF, C_ONE, C_EPS, C_INV1024, C_NHALF, C_3HALF, C_BIGNEG, C_ZERO, C_NONE = range(11)
CONSTS = [-CSHIFT, 0.125, 0.5, 1.0, 1e-5, 1.0 / 1024.0, -0.5, 1.5, -1e30, 0.0, -1.0]



class Sem:
    __slots__ = ("h", "n", "name")

    def __init__(self, st, nc, name):
        self.h = st.enter_context(nc.semaphore(name))
        self.n = 0
        self.name = name


class P:
    def __init__(self, nc):
        self.nc = nc
        self.ops = {e: [] for e in ENGINES}
        self.waitlog = []

    def check(self):
        bad = []
        for eng, sem, val in self.waitlog:
            if val > sem.n:
                bad.append((eng, sem.name, val, sem.n))
        return bad

    def add(self, eng, fn, sem=None, k=1):
        if sem is None:
            self.ops[eng].append(fn)
        else:
            sem.n += k

            def wrapped(e, fn=fn, sem=sem, k=k):
                fn(e).then_inc(sem.h, k)
            self.ops[eng].append(wrapped)

    def wait(self, eng, sem, val):
        if val <= 0:
            return
        self.waitlog.append((eng, sem, val))
        self.ops[eng].append(lambda e, h=sem.h, v=val: e.wait_ge(h, v))

    def emit(self):
        with self.nc.Block() as block:
            for eng in ENGINES:
                ops = self.ops[eng]
                deco = getattr(block, eng)

                def body(e, ops=ops):
                    for fn in ops:
                        fn(e)
                deco(body)


class Ctx:
    pass


def fence(c, engines=ENGINES):
    p = c.p
    sems = [c.sV, c.sC, c.sMM, c.sT, c.sX, c.sWA, c.sWB, c.sWHL, c.sKV, c.sOUT, c.sCC, c.sG]
    for eng in engines:
        for s in sems:
            p.wait(eng, s, s.n)


def segments(qlo, fd):
    out = []
    off = 0
    while off < fd:
        zcol = qlo + off
        bh = 0 if zcol < 512 else 1
        bo = zcol % 512
        seg = min(512 - bo, fd - off)
        out.append((bh, bo, off, seg))
        off += seg
    return out


def build():
    nc = bass.Bass("TRN2", target_bir_lowering=False, detect_race_conditions=False)
    c = Ctx()
    c.nc = nc
    p = P(nc)
    c.p = p

    c.x_in = nc.dram_tensor("x_in", [NT, 1024], F32, kind="ExternalInput").ap()
    c.wuv_in = nc.dram_tensor("wuv_in", [NER, 2, 1024, 128], F32, kind="ExternalInput").ap()
    c.wo_in = nc.dram_tensor("wo_in", [ND, 2048, 128], F32, kind="ExternalInput").ap()
    c.wkv1_in = nc.dram_tensor("wkv1_in", [1024, 128], F32, kind="ExternalInput").ap()
    c.wq_in = nc.dram_tensor("wq_in", [1024, 64], F32, kind="ExternalInput").ap()
    c.wv2_in = nc.dram_tensor("wv2_in", [64, 1024], F32, kind="ExternalInput").ap()
    c.gw_in = nc.dram_tensor("gw_in", [1024, 8], F32, kind="ExternalInput").ap()
    c.masks_in = nc.dram_tensor("masks_in", [16, 128, 128], BF16, kind="ExternalInput").ap()
    c.out = nc.dram_tensor("out", [NT, 1024], F32, kind="ExternalOutput").ap()
    c.wuv_hl = nc.dram_tensor("wuv_hl", [NER, 2, 1024, 256], BF16).ap()
    c.wo_hl = nc.dram_tensor("wo_hl", [ND, 2048, 256], BF16).ap()
    c.cc_in_t = nc.dram_tensor("cc_in", [NT, 256], BF16)
    c.cc_out_t = nc.dram_tensor("cc_out", [8 * NT, 256], BF16)
    c.cc_in = c.cc_in_t.ap()
    c.cc_out = c.cc_out_t.ap()

    with ExitStack() as st:
        sb = lambda name, shape, dt=F32: st.enter_context(nc.sbuf_tensor(name, shape, dt))
        c.xres = [sb(f"xres{j}", [128, NT]) for j in range(ND)]
        c.xln_h = sb("xln_h", [128, ND * TKB], BF16)
        c.xln_l = sb("xln_l", [128, ND * TKB], BF16)
        c.ident = sb("ident", [128, 128])
        c.onesf = sb("onesf", [1, 128])
        c.onesb = sb("onesb", [1, 128], BF16)
        c.cpool = sb("cpool", [128, 16])
        c.cpool_i = sb("cpool_i", [128, 2], I32)
        c.stat_tm = sb("stat_tm", [128, 2 * NTILE])
        c.rowbuf = sb("rowbuf", [2, TKB])
        c.ln_scr = sb("ln_scr", [128, 512])
        c.ln_t1 = sb("ln_t1", [128, TKB])
        c.ln_t2 = sb("ln_t2", [128, TKB])
        c.gbuf = sb("gbuf", [8, TKB])
        c.g_tm = sb("g_tm", [128, 32])
        c.g_rows = sb("g_rows", [8, TKB])
        c.grh = sb("grh", [8, TKB], BF16)
        c.grl = sb("grl", [8, TKB], BF16)
        c.dbuf = sb("dbuf", [128, 4])
        c.wbuf2 = sb("wbuf2", [128, 4])
        c.gsc = sb("gsc", [128, 16])
        c.gsc2 = sb("gsc2", [128, 16])
        c.gt = [sb(f"gt{i}", [128, 8]) for i in range(4)]
        c.gt2 = [sb(f"gt2_{i}", [128, 8]) for i in range(4)]
        c.selb = sb("selb", [8, 8 * 128], BF16)
        c.self32 = sb("self32", [2, 2 * 128])
        c.wq_h = sb("wq_h", [128, ND * 64], BF16)
        c.wq_l = sb("wq_l", [128, ND * 64], BF16)
        c.wkv_h = sb("wkv_h", [128, ND * 128], BF16)
        c.wkv_l = sb("wkv_l", [128, ND * 128], BF16)
        c.wv2_h = sb("wv2_h", [64, 1024], BF16)
        c.wv2_l = sb("wv2_l", [64, 1024], BF16)
        c.gw_h = sb("gw_h", [128, ND * 8], BF16)
        c.gw_l = sb("gw_l", [128, ND * 8], BF16)
        c.sil = [sb(f"sil{i}", [128, TKB]) for i in range(2)]
        c.s_t1 = sb("s_t1", [128, TKB])
        c.s_t2 = sb("s_t2", [128, TKB])

        c.pb = [st.enter_context(nc.psum_tensor(f"pb{i}", [128, 512], F32)) for i in range(8)]

        S = lambda name: Sem(st, nc, name)
        c.sX = S("sX"); c.sT = S("sT"); c.sV = S("sV"); c.sC = S("sC")
        c.sMM = S("sMM"); c.sWA = S("sWA"); c.sWB = S("sWB"); c.sWHL = S("sWHL")
        c.sCC = S("sCC"); c.sOUT = S("sOUT"); c.sKV = S("sKV"); c.sG = S("sG")

        with ExitStack() as stP0:
            emit_p0(c, stP0)
            fence(c)
        emit_stats(c)
        with ExitStack() as stA:
            emit_ffn(c, stA, call=0)
            fence(c)
        emit_stats(c)
        with ExitStack() as stB:
            emit_p2(c, stB)
            emit_p3(c, stB)
            fence(c)
        emit_stats(c)
        with ExitStack() as stC:
            emit_ffn(c, stC, call=1)
            emit_p5(c, stC)
        p.emit()
    return nc


def emit_p0(c, st):
    nc, p = c.nc, c.p
    sb = lambda name, shape, dt=F32: st.enter_context(nc.sbuf_tensor(name, shape, dt))
    xtm = [sb(f"xtm{t}", [128, 1024]) for t in range(NTILE)]
    c.wsm = sb("wsm", [128, 1024])
    iota_c = sb("iota_c", [128, 128])
    iota_p = sb("iota_p", [128, 1])
    iota_e8 = sb("iota_e8", [8, 8 * 128])
    iota_e2 = sb("iota_e2", [2, 2 * 128])

    def g_consts(e):
        e.iota(iota_c[:], [[1, 128]], channel_multiplier=0, allow_small_or_imprecise_dtypes=True)
        e.iota(iota_p[:], [[0, 1]], channel_multiplier=1, allow_small_or_imprecise_dtypes=True)
        e.memset(c.onesf[:], 1.0)
        e.memset(c.onesb[:], 1.0)
        for i, v in enumerate(CONSTS):
            e.memset(c.cpool[:, i:i + 1], v)
        e.iota(iota_e8[:].rearrange("p (b f) -> p b f", b=8), [[1, 8], [0, 128]],
               channel_multiplier=0, allow_small_or_imprecise_dtypes=True)
        e.iota(iota_e2[:].rearrange("p (b f) -> p b f", b=2), [[1, 2], [0, 128]],
               channel_multiplier=0, allow_small_or_imprecise_dtypes=True)
        return e.memset(c.onesb[:], 1.0)
    p.add("gpsimd", g_consts, c.sV, 1)
    consts_done = c.sV.n

    xld = []
    for t in range(NTILE):
        p.add("sync", lambda e, t=t: e.dma_start(
            out=xtm[t][:], in_=c.x_in[128 * t:128 * (t + 1), :]), c.sX, 16)
        xld.append(c.sX.n)
    smalls = [
        (lambda: c.wsm[:, 0:512].rearrange("p (cc e) -> p cc e", cc=8),
         lambda: c.wq_in.rearrange("(cc p) e -> p cc e", p=128), c.wq_h, c.wq_l, 128, 512),
        (lambda: c.wsm[:, 0:1024].rearrange("p (cc e) -> p cc e", cc=8),
         lambda: c.wkv1_in.rearrange("(cc p) e -> p cc e", p=128), c.wkv_h, c.wkv_l, 128, 1024),
        (lambda: c.wsm[:, 0:64].rearrange("p (cc e) -> p cc e", cc=8),
         lambda: c.gw_in.rearrange("(cc p) e -> p cc e", p=128), c.gw_h, c.gw_l, 128, 64),
        (lambda: c.wsm[:64, 0:1024], lambda: c.wv2_in, c.wv2_h, c.wv2_l, 64, 1024),
    ]
    for i, (dst, src, hi, lo, rows, cols) in enumerate(smalls):
        p.add("sync", lambda e, dst=dst, src=src: e.dma_start(out=dst(), in_=src()), c.sX, 16)
        ld = c.sX.n
        p.wait("vector", c.sX, ld)
        p.add("vector", lambda e, hi=hi, rows=rows, cols=cols: e.tensor_copy(
            hi[:], c.wsm[:rows, 0:cols]))
        p.add("vector", lambda e, lo=lo, hi=hi, rows=rows, cols=cols: e.tensor_sub(
            lo[:], c.wsm[:rows, 0:cols], hi[:]), c.sV, 1)
        if i + 1 < len(smalls):
            p.wait("sync", c.sV, c.sV.n)

    p.wait("vector", c.sV, consts_done)
    p.add("vector", lambda e: e.tensor_scalar(
        c.ident[:], iota_c[:], iota_p[:], None, ALU.is_equal))
    p.add("vector", lambda e: e.tensor_scalar(
        c.selb[:], iota_e8[:], iota_p[0:8, :], None, ALU.is_equal))
    p.add("vector", lambda e: e.tensor_scalar(
        c.self32[:], iota_e2[:], iota_p[0:2, :], None, ALU.is_equal), c.sV, 1)
    ident_done = c.sV.n

    p.wait("tensor", c.sV, ident_done)
    evac = {}
    for g in range(NTB * ND):
        tb, j = divmod(g, ND)
        bank = c.pb[g % 4]
        if g >= 4:
            p.wait("tensor", c.sV, evac[g - 4])
        for t in range(4):
            tile = 4 * tb + t
            p.wait("tensor", c.sX, xld[tile])

            def tr(e, bank=bank, t=t, tile=tile, j=j):
                return e.transpose(bank[:, 128 * t:128 * (t + 1)],
                                   xtm[tile][:, 128 * j:128 * (j + 1)], c.ident[:])
            p.add("tensor", tr, c.sT if t == 3 else None, 1)
        grp = c.sT.n
        p.wait("vector", c.sT, grp)
        p.add("vector", lambda e, tb=tb, j=j, bank=bank: e.tensor_copy(
            c.xres[j][:, TKB * tb:TKB * (tb + 1)], bank[:]), c.sV, 1)
        evac[g] = c.sV.n


def emit_stats(c):
    p = c.p
    p.wait("tensor", c.sV, c.sV.n)
    p.wait("tensor", c.sC, c.sC.n)
    vec_done = {}
    for tp in range(NTILE // 2):
        ta, tb_ = 2 * tp, 2 * tp + 1
        if tp % 2 == 0:
            bA, bB = (c.pb[6], c.pb[7]), (c.pb[4], c.pb[5])
        else:
            bA, bB = (c.pb[0], c.pb[1]), (c.pb[2], c.pb[3])
        if tp >= 2:
            p.wait("tensor", c.sV, vec_done[tp - 2])
            p.wait("tensor", c.sC, c.sC.n - 4)
        for t, banks in ((ta, bA), (tb_, bB)):
            for j in range(ND):
                bnk = banks[0] if j < 4 else banks[1]

                def tr(e, bnk=bnk, j=j, t=t):
                    return e.transpose(bnk[:, 128 * (j % 4):128 * (j % 4 + 1)],
                                       c.xres[j][:, 128 * t:128 * (t + 1)], c.ident[:])
                p.add("tensor", tr, c.sT if j == ND - 1 else None, 1)
        grp = c.sT.n
        p.wait("scalar", c.sT, grp)
        p.add("scalar", lambda e, bA=bA: e.activation(
            c.ln_scr[:, 0:512], bA[0][:], ACT.Square,
            accum_out=c.gsc[:, 8:9]))
        p.add("scalar", lambda e, bA=bA: e.activation(
            c.ln_scr[:, 0:512], bA[1][:], ACT.Square,
            accum_out=c.gsc[:, 9:10]), c.sC, 1)
        p.add("scalar", lambda e, bB=bB: e.activation(
            c.ln_scr[:, 0:512], bB[0][:], ACT.Square,
            accum_out=c.gsc2[:, 8:9]))
        p.add("scalar", lambda e, bB=bB: e.activation(
            c.ln_scr[:, 0:512], bB[1][:], ACT.Square,
            accum_out=c.gsc2[:, 9:10]), c.sC, 1)
        sq_done = c.sC.n
        p.wait("vector", c.sT, grp)
        p.wait("vector", c.sC, sq_done)
        p.add("vector", lambda e, bA=bA: e.reduce_sum(c.gsc[:, 0:1], bA[0][:], axis=AXX))
        p.add("vector", lambda e, bB=bB: e.reduce_sum(c.gsc2[:, 0:1], bB[0][:], axis=AXX))
        p.add("vector", lambda e, bA=bA: e.reduce_sum(c.gsc[:, 1:2], bA[1][:], axis=AXX))
        p.add("vector", lambda e, bB=bB: e.reduce_sum(c.gsc2[:, 1:2], bB[1][:], axis=AXX))

        def stats_ab(e):
            cp = c.cpool
            A, B = c.gsc, c.gsc2
            for g in (A, B):
                e.tensor_scalar(g[:, 2:3], g[:, 0:1], g[:, 1:2], None, ALU.add)
            for g in (A, B):
                e.tensor_scalar(g[:, 3:4], g[:, 8:9], g[:, 9:10], None, ALU.add)
            for g in (A, B):
                e.tensor_scalar(g[:, 4:5], g[:, 2:3], cp[:, C_INV1024:C_INV1024 + 1],
                                None, ALU.mult)
            for g in (A, B):
                e.tensor_scalar(g[:, 5:6], g[:, 3:4], cp[:, C_INV1024:C_INV1024 + 1],
                                cp[:, C_EPS:C_EPS + 1], ALU.mult, ALU.add)
            for g in (A, B):
                e.tensor_scalar(g[:, 6:7], g[:, 4:5], g[:, 4:5], None, ALU.mult)
            for g in (A, B):
                return_ins = e.tensor_scalar(g[:, 7:8], g[:, 5:6], g[:, 6:7],
                                             None, ALU.subtract)
            return return_ins
        p.add("vector", stats_ab, c.sV, 1)
        v_ready = c.sV.n
        p.wait("scalar", c.sV, v_ready)
        p.add("scalar", lambda e: e.activation(c.gsc[:, 10:11], c.gsc[:, 7:8], ACT.Sqrt))
        p.add("scalar", lambda e: e.activation(c.gsc2[:, 10:11], c.gsc2[:, 7:8],
                                               ACT.Sqrt), c.sC, 1)
        sqrt_done = c.sC.n
        p.wait("vector", c.sC, sqrt_done)

        def newton_ab(e, ta=ta, tb_=tb_):
            cp = c.cpool
            A, B = c.gsc, c.gsc2
            e.reciprocal(A[:, 15:16], A[:, 10:11])
            e.reciprocal(B[:, 15:16], B[:, 10:11])
            for _ in range(2):
                for g in (A, B):
                    e.tensor_scalar(g[:, 11:12], g[:, 15:16], g[:, 15:16], None, ALU.mult)
                for g in (A, B):
                    e.tensor_scalar(g[:, 12:13], g[:, 11:12], g[:, 7:8], None, ALU.mult)
                for g in (A, B):
                    e.tensor_scalar(g[:, 13:14], g[:, 12:13], cp[:, C_NHALF:C_NHALF + 1],
                                    cp[:, C_3HALF:C_3HALF + 1], ALU.mult, ALU.add)
                for g in (A, B):
                    e.tensor_scalar(g[:, 15:16], g[:, 15:16], g[:, 13:14], None, ALU.mult)
            e.tensor_copy(c.stat_tm[:, 2 * ta:2 * ta + 1], A[:, 15:16])
            e.tensor_copy(c.stat_tm[:, 2 * tb_:2 * tb_ + 1], B[:, 15:16])
            e.tensor_scalar(c.stat_tm[:, 2 * ta + 1:2 * ta + 2], A[:, 15:16], A[:, 4:5],
                            None, ALU.mult)
            return e.tensor_scalar(c.stat_tm[:, 2 * tb_ + 1:2 * tb_ + 2], B[:, 15:16],
                                   B[:, 4:5], None, ALU.mult)
        p.add("vector", newton_ab, c.sV, 1)
        vec_done[tp] = c.sV.n
    c.stats_done = c.sV.n


def ln_block(c, tb):
    p = c.p
    p.wait("tensor", c.sV, c.sV.n)
    p.wait("tensor", c.sC, c.sC.n)
    for t in range(4):
        tile = 4 * tb + t

        def tr(e, t=t, tile=tile):
            return e.transpose(c.pb[5][0:2, 128 * t:128 * (t + 1)],
                               c.stat_tm[:, 2 * tile:2 * tile + 2], c.ident[:])
        p.add("tensor", tr, c.sT if t == 3 else None, 1)
    p.wait("vector", c.sT, c.sT.n)
    p.add("vector", lambda e: e.tensor_copy(c.rowbuf[:], c.pb[5][0:2, 0:TKB]), c.sV, 1)
    p.wait("tensor", c.sV, c.sV.n)
    p.add("tensor", lambda e: e.matmul(c.pb[4][:], c.self32[:, 0:128], c.rowbuf[:],
                                       start=True, stop=True))
    p.add("tensor", lambda e: e.matmul(c.pb[6][:], c.self32[:, 128:256], c.rowbuf[:],
                                       start=True, stop=True), c.sT, 1)
    p.wait("vector", c.sT, c.sT.n)
    for j in range(ND):
        def apply(e, j=j, tb=tb):
            xs = c.xres[j][:, TKB * tb:TKB * (tb + 1)]
            e.tensor_mul(c.ln_t1[:], xs, c.pb[4][:])
            e.tensor_sub(c.ln_t2[:], c.ln_t1[:], c.pb[6][:])
            e.tensor_copy(c.xln_h[:, TKB * j:TKB * (j + 1)], c.ln_t2[:])
            return e.tensor_sub(c.xln_l[:, TKB * j:TKB * (j + 1)], c.ln_t2[:],
                                c.xln_h[:, TKB * j:TKB * (j + 1)])
        p.add("vector", apply, c.sV if j == ND - 1 else None, 1)
    return c.sV.n


def emit_ffn(c, st, call):
    nc, p = c.nc, c.p
    sb = lambda name, shape, dt=F32: st.enter_context(nc.sbuf_tensor(name, shape, dt))
    s_h = sb(f"s_h{call}", [128, NER * TKB], BF16)
    s_l = sb(f"s_l{call}", [128, NER * TKB], BF16)
    wuvc = [sb(f"wuvc{call}_{i}", [128, 4096], BF16) for i in range(2)]
    woc = [sb(f"woc{call}_{i}", [128, 4096], BF16) for i in range(2)]
    fresh = (call == 0)
    if fresh:
        wst = [sb(f"wst{i}", [128, 2048]) for i in range(2)]
        wost = wst

    uv_consumed = {}
    y_consumed = {}
    schain_done = {}
    yadd_done = {}
    split_w = {}
    split_wo = {}
    whl_written = {}
    wohl_written = {}
    wa_ld = {}
    wb_ld = {}

    def gate_mm(e):
        first = True
        ins = None
        for cc in range(ND):
            gh = c.gw_h[:, 8 * cc:8 * (cc + 1)]
            gl = c.gw_l[:, 8 * cc:8 * (cc + 1)]
            xh = c.xln_h[:, TKB * cc:TKB * (cc + 1)]
            xl = c.xln_l[:, TKB * cc:TKB * (cc + 1)]
            e.matmul(c.pb[7][0:8, :], gh, xh, start=first, stop=False); first = False
            e.matmul(c.pb[7][0:8, :], gh, xl, start=False, stop=False)
            ins = e.matmul(c.pb[7][0:8, :], gl, xh, start=False, stop=(cc == ND - 1))
        return ins

    def gate_math1(e):
        cp = c.cpool
        ins = None
        for tp in range(2):
            tA, tB = 2 * tp, 2 * tp + 1
            lgA = c.g_tm[:, 8 * tA:8 * (tA + 1)]
            lgB = c.g_tm[:, 8 * tB:8 * (tB + 1)]
            A, B = c.gsc, c.gsc2
            gA, gB = c.gt, c.gt2
            e.reduce_max(A[:, 0:1], lgA, axis=AXX)
            e.reduce_max(B[:, 0:1], lgB, axis=AXX)
            e.tensor_scalar(gA[0][:], lgA, A[:, 0:1], None, ALU.subtract)
            e.tensor_scalar(gB[0][:], lgB, B[:, 0:1], None, ALU.subtract)
            e.tensor_scalar(gA[1][:], gA[0][:], cp[:, C_ZERO:C_ZERO + 1], None, ALU.is_equal)
            e.tensor_scalar(gB[1][:], gB[0][:], cp[:, C_ZERO:C_ZERO + 1], None, ALU.is_equal)
            e.scalar_tensor_tensor(gA[2][:], gA[1][:], cp[:, C_BIGNEG:C_BIGNEG + 1],
                                   gA[0][:], ALU.mult, ALU.add)
            e.scalar_tensor_tensor(gB[2][:], gB[1][:], cp[:, C_BIGNEG:C_BIGNEG + 1],
                                   gB[0][:], ALU.mult, ALU.add)
            e.reduce_max(c.dbuf[:, tA:tA + 1], gA[2][:], axis=AXX)
            e.reduce_max(c.dbuf[:, tB:tB + 1], gB[2][:], axis=AXX)
            e.tensor_scalar(gA[3][:], gA[2][:], c.dbuf[:, tA:tA + 1], None, ALU.is_equal)
            e.tensor_scalar(gB[3][:], gB[2][:], c.dbuf[:, tB:tB + 1], None, ALU.is_equal)
            e.tensor_copy(c.ln_t1[:, 16 * tA:16 * tA + 8], gA[1][:])
            e.tensor_copy(c.ln_t1[:, 16 * tB:16 * tB + 8], gB[1][:])
            e.tensor_copy(c.ln_t1[:, 16 * tA + 8:16 * tA + 16], gA[3][:])
            ins = e.tensor_copy(c.ln_t1[:, 16 * tB + 8:16 * tB + 16], gB[3][:])
        return ins

    def gate_math2(e):
        cp = c.cpool
        ins = None
        for tp in range(2):
            tA, tB = 2 * tp, 2 * tp + 1
            A, B = c.gsc, c.gsc2
            gA, gB = c.gt, c.gt2
            w2A = c.wbuf2[:, tA:tA + 1]
            w2B = c.wbuf2[:, tB:tB + 1]
            e.tensor_scalar(A[:, 0:1], w2A, cp[:, C_NONE:C_NONE + 1],
                            cp[:, C_ONE:C_ONE + 1], ALU.mult, ALU.add)
            e.tensor_scalar(B[:, 0:1], w2B, cp[:, C_NONE:C_NONE + 1],
                            cp[:, C_ONE:C_ONE + 1], ALU.mult, ALU.add)
            e.tensor_scalar(gA[0][:], c.ln_t1[:, 16 * tA:16 * tA + 8], A[:, 0:1],
                            None, ALU.mult)
            e.tensor_scalar(gB[0][:], c.ln_t1[:, 16 * tB:16 * tB + 8], B[:, 0:1],
                            None, ALU.mult)
            e.scalar_tensor_tensor(c.g_tm[:, 8 * tA:8 * (tA + 1)],
                                   c.ln_t1[:, 16 * tA + 8:16 * tA + 16], w2A,
                                   gA[0][:], ALU.mult, ALU.add)
            ins = e.scalar_tensor_tensor(c.g_tm[:, 8 * tB:8 * (tB + 1)],
                                         c.ln_t1[:, 16 * tB + 8:16 * tB + 16], w2B,
                                         gB[0][:], ALU.mult, ALU.add)
        return ins

    def pstage(ctx, s, tb):
        if s == 0:
            for t in range(4):
                tile = 4 * tb + t

                def tr(e, t=t, tile=tile):
                    return e.transpose(c.pb[5][0:2, 128 * t:128 * (t + 1)],
                                       c.stat_tm[:, 2 * tile:2 * tile + 2], c.ident[:])
                p.add("tensor", tr, c.sT if t == 3 else None, 1)
            ctx[0] = c.sT.n
        elif s == 1:
            p.wait("vector", c.sT, ctx[0])
            p.add("vector", lambda e: e.tensor_copy(c.rowbuf[:], c.pb[5][0:2, 0:TKB]),
                  c.sV, 1)
            ctx[1] = c.sV.n
        elif s == 2:
            p.wait("tensor", c.sV, ctx[1])
            p.add("tensor", lambda e: e.matmul(c.pb[4][:], c.self32[:, 0:128],
                                               c.rowbuf[:], start=True, stop=True))
            p.add("tensor", lambda e: e.matmul(c.pb[6][:], c.self32[:, 128:256],
                                               c.rowbuf[:], start=True, stop=True),
                  c.sT, 1)
            ctx[2] = c.sT.n
        elif s == 3:
            p.wait("vector", c.sT, ctx[2])
            for j in range(ND):
                def apply(e, j=j, tb=tb):
                    xs = c.xres[j][:, TKB * tb:TKB * (tb + 1)]
                    e.tensor_mul(c.ln_t1[:], xs, c.pb[4][:])
                    e.tensor_sub(c.ln_t2[:], c.ln_t1[:], c.pb[6][:])
                    e.tensor_copy(c.xln_h[:, TKB * j:TKB * (j + 1)], c.ln_t2[:])
                    return e.tensor_sub(c.xln_l[:, TKB * j:TKB * (j + 1)], c.ln_t2[:],
                                        c.xln_h[:, TKB * j:TKB * (j + 1)])
                p.add("vector", apply, c.sV if j == ND - 1 else None, 1)
            ctx["xln"] = c.sV.n
        elif s == 4:
            p.wait("tensor", c.sV, ctx["xln"])
            p.add("tensor", gate_mm, c.sMM, 1)
            ctx[4] = c.sMM.n
        elif s == 5:
            p.wait("vector", c.sMM, ctx[4])
            p.add("vector", lambda e: e.tensor_copy(c.gbuf[:], c.pb[7][0:8, :]), c.sV, 1)
            ctx[5] = c.sV.n
        elif s == 6:
            p.wait("tensor", c.sV, ctx[5])
            for t in range(4):
                def tr(e, t=t):
                    return e.transpose(c.pb[5][:, 8 * t:8 * (t + 1)],
                                       c.gbuf[0:8, 128 * t:128 * (t + 1)],
                                       c.ident[0:8, 0:8])
                p.add("tensor", tr, c.sT if t == 3 else None, 1)
            ctx[6] = c.sT.n
        elif s == 7:
            p.wait("vector", c.sT, ctx[6])
            p.add("vector", lambda e: e.tensor_copy(c.g_tm[:, 0:16], c.pb[5][:, 0:16]))
            p.add("vector", lambda e: e.tensor_copy(c.g_tm[:, 16:32], c.pb[5][:, 16:32]))
            p.add("vector", gate_math1, c.sV, 1)
            ctx[7] = c.sV.n
        elif s == 8:
            p.wait("scalar", c.sV, ctx[7])
            p.add("scalar", lambda e: e.activation(c.wbuf2[:], c.dbuf[:], ACT.Sigmoid),
                  c.sC, 1)
            ctx[8] = c.sC.n
        elif s == 9:
            p.wait("vector", c.sC, ctx[8])
            p.add("vector", gate_math2, c.sV, 1)
            ctx[9] = c.sV.n
        elif s == 10:
            p.wait("tensor", c.sV, ctx[9])
            for t in range(4):
                def tr(e, t=t):
                    return e.transpose(c.pb[5][0:8, 128 * t:128 * (t + 1)],
                                       c.g_tm[:, 8 * t:8 * (t + 1)], c.ident[:])
                p.add("tensor", tr, c.sT if t == 3 else None, 1)
            ctx[10] = c.sT.n
        elif s == 11:
            p.wait("vector", c.sT, ctx[10])
            p.add("vector", lambda e: e.tensor_copy(c.g_rows[:], c.pb[5][0:8, 0:TKB]))
            p.add("vector", lambda e: e.tensor_copy(c.grh[:], c.g_rows[:]))
            p.add("vector", lambda e: e.tensor_sub(c.grl[:], c.g_rows[:], c.grh[:]),
                  c.sV, 1)
            ctx["grows"] = c.sV.n

    TE_AFTER = {0: [0], 1: [2], 2: [4], 3: [6], 5: [10]}
    VEC_AFTER = {0: [1], 1: [3], 2: [5], 3: [7, 8], 4: [9], 5: [11]}

    nctx = None
    for tb in range(NTB):
        if tb == 0:
            ctx = {}
            p.wait("tensor", c.sV, c.sV.n)
            p.wait("tensor", c.sC, c.sC.n)
            for s in range(12):
                pstage(ctx, s, tb)
        else:
            ctx = nctx
        xln_ready = ctx["xln"]
        grows_done = ctx["grows"]

        for k in range(NER):
            gch = tb * NER + k
            slot = gch % 2
            e_idx = k // 2
            if fresh and tb == 0:
                if gch >= 2:
                    p.wait("sync", c.sV, split_w[gch - 2])
                p.add("sync", lambda e, k=k, slot=slot: e.dma_start(
                    out=wst[slot][:].rearrange("p (u cc e) -> p u cc e", u=2, cc=8),
                    in_=c.wuv_in[k].rearrange("u (cc p) e -> p u cc e", p=128)), c.sWA, 16)
                wa_ld[gch] = c.sWA.n
                p.wait("vector", c.sWA, wa_ld[gch])
                if gch >= 2:
                    p.wait("vector", c.sMM, uv_consumed[gch - 2])
                    p.wait("vector", c.sWHL, whl_written[gch - 2])

                def spl(e, slot=slot):
                    wv = wst[slot][:].rearrange("p (u cc e) -> p u cc e", u=2, cc=8)
                    comb = wuvc[slot][:].rearrange("p (u cc x) -> p u cc x", u=2, cc=8)
                    e.tensor_copy(comb[:, :, :, 0:128], wv)
                    return e.tensor_sub(comb[:, :, :, 128:256], wv, comb[:, :, :, 0:128])
                p.add("vector", spl, c.sV, 1)
                split_w[gch] = c.sV.n
                p.wait("scalar", c.sV, split_w[gch])
                p.add("scalar", lambda e, k=k, slot=slot: e.dma_start(
                    out=c.wuv_hl[k].rearrange("u (cc p) x -> p u cc x", p=128),
                    in_=wuvc[slot][:].rearrange("p (u cc x) -> p u cc x", u=2, cc=8)),
                    c.sWHL, 16)
                whl_written[gch] = c.sWHL.n
            else:
                if fresh and tb == 1 and k < 2:
                    p.wait("sync", c.sWHL, whl_written[NER - 1])
                if gch >= 2 and (gch - 2) in uv_consumed:
                    p.wait("sync", c.sMM, uv_consumed[gch - 2])
                p.add("sync", lambda e, k=k, slot=slot: e.dma_start(
                    out=wuvc[slot][:].rearrange("p (u cc x) -> p u cc x", u=2, cc=8),
                    in_=c.wuv_hl[k].rearrange("u (cc p) x -> p u cc x", p=128)), c.sWA, 16)
                wa_ld[gch] = c.sWA.n

            gb = c.pb[4] if e_idx % 2 == 0 else c.pb[6]
            bu = c.pb[(k % 2) * 2]
            bv = c.pb[(k % 2) * 2 + 1]
            p.wait("tensor", c.sWA, wa_ld[gch])
            if fresh and tb == 0:
                p.wait("tensor", c.sV, split_w[gch])
            if gch >= 2 and (gch - 2) in schain_done:
                p.wait("tensor", c.sV, schain_done[gch - 2])

            def uvmm(e, slot=slot, bu=bu, bv=bv):
                ins = None
                for uv, bank in ((0, bu), (1, bv)):
                    first = True
                    for cc in range(ND):
                        off = (uv * ND + cc) * 256
                        wh = wuvc[slot][:, off:off + 128]
                        wl = wuvc[slot][:, off + 128:off + 256]
                        xh = c.xln_h[:, TKB * cc:TKB * (cc + 1)]
                        xl = c.xln_l[:, TKB * cc:TKB * (cc + 1)]
                        e.matmul(bank[:], wh, xh, start=first, stop=False); first = False
                        e.matmul(bank[:], wh, xl, start=False, stop=False)
                        ins = e.matmul(bank[:], wl, xh, start=False, stop=(cc == ND - 1))
                return ins
            p.add("tensor", uvmm, c.sMM, 1)
            uv_consumed[gch] = c.sMM.n

            if k % 2 == 0:
                p.wait("tensor", c.sV, grows_done)
                if gch - 3 in schain_done:
                    p.wait("tensor", c.sV, schain_done[gch - 3])

                def gbc(e, gb=gb, e_idx=e_idx):
                    sel = c.selb[:, 128 * e_idx:128 * (e_idx + 1)]
                    e.matmul(gb[:], sel, c.grh[:], start=True, stop=False)
                    return e.matmul(gb[:], sel, c.grl[:], start=False, stop=True)
                p.add("tensor", gbc, c.sT, 1)
            gbc_done = c.sT.n

            p.wait("scalar", c.sMM, uv_consumed[gch])
            p.add("scalar", lambda e, k=k, bu=bu: e.activation(
                c.sil[k % 2][:], bu[:], ACT.Silu), c.sC, 1)
            sil_done = c.sC.n

            p.wait("vector", c.sC, sil_done)
            p.wait("vector", c.sT, gbc_done)

            def schain(e, k=k, bv=bv, gb=gb):
                e.tensor_mul(c.s_t1[:], c.sil[k % 2][:], bv[:])
                e.tensor_mul(c.s_t2[:], c.s_t1[:], gb[:])
                e.tensor_copy(s_h[:, TKB * k:TKB * (k + 1)], c.s_t2[:])
                return e.tensor_sub(s_l[:, TKB * k:TKB * (k + 1)], c.s_t2[:],
                                    s_h[:, TKB * k:TKB * (k + 1)])
            p.add("vector", schain, c.sV, 1)
            schain_done[gch] = c.sV.n

        nctx = {}
        all_s = schain_done[tb * NER + NER - 1]
        for j in range(ND):
            gj = tb * ND + j
            slot = gj % 2
            if fresh and tb == 0:
                if gj < 2:
                    p.wait("sync", c.sV, split_w[14 + gj])
                else:
                    p.wait("sync", c.sV, split_wo[gj - 2])
                p.add("sync", lambda e, j=j, slot=slot: e.dma_start(
                    out=wost[slot][:].rearrange("p (cc e) -> p cc e", cc=16),
                    in_=c.wo_in[j].rearrange("(cc p) e -> p cc e", p=128)), c.sWB, 16)
                wb_ld[gj] = c.sWB.n
                p.wait("vector", c.sWB, wb_ld[gj])
                if gj >= 2:
                    p.wait("vector", c.sMM, y_consumed[gj - 2])
                    p.wait("vector", c.sWHL, wohl_written[gj - 2])

                def splo(e, slot=slot):
                    wv = wost[slot][:].rearrange("p (cc e) -> p cc e", cc=16)
                    comb = woc[slot][:].rearrange("p (cc x) -> p cc x", cc=16)
                    e.tensor_copy(comb[:, :, 0:128], wv)
                    return e.tensor_sub(comb[:, :, 128:256], wv, comb[:, :, 0:128])
                p.add("vector", splo, c.sV, 1)
                split_wo[gj] = c.sV.n
                p.wait("scalar", c.sV, split_wo[gj])
                p.add("scalar", lambda e, j=j, slot=slot: e.dma_start(
                    out=c.wo_hl[j].rearrange("(cc p) x -> p cc x", p=128),
                    in_=woc[slot][:].rearrange("p (cc x) -> p cc x", cc=16)), c.sWHL, 16)
                wohl_written[gj] = c.sWHL.n
            else:
                if fresh and tb == 1 and j < 2:
                    p.wait("sync", c.sWHL, wohl_written[ND - 1])
                if gj >= 2 and (gj - 2) in y_consumed:
                    p.wait("sync", c.sMM, y_consumed[gj - 2])
                p.add("sync", lambda e, j=j, slot=slot: e.dma_start(
                    out=woc[slot][:].rearrange("p (cc x) -> p cc x", cc=16),
                    in_=c.wo_hl[j].rearrange("(cc p) x -> p cc x", p=128)), c.sWB, 16)
                wb_ld[gj] = c.sWB.n

            by = c.pb[j % 2]
            p.wait("tensor", c.sV, all_s)
            p.wait("tensor", c.sWB, wb_ld[gj])
            if fresh and tb == 0:
                p.wait("tensor", c.sV, split_wo[gj])
            if gj >= 2 and (gj - 2) in yadd_done:
                p.wait("tensor", c.sV, yadd_done[gj - 2])

            def ymm(e, slot=slot, by=by):
                first = True
                ins = None
                for k in range(NER):
                    off = k * 256
                    wh = woc[slot][:, off:off + 128]
                    wl = woc[slot][:, off + 128:off + 256]
                    sh_ = s_h[:, TKB * k:TKB * (k + 1)]
                    sl_ = s_l[:, TKB * k:TKB * (k + 1)]
                    e.matmul(by[:], wh, sh_, start=first, stop=False); first = False
                    e.matmul(by[:], wh, sl_, start=False, stop=False)
                    ins = e.matmul(by[:], wl, sh_, start=False, stop=(k == NER - 1))
                return ins
            p.add("tensor", ymm, c.sMM, 1)
            y_consumed[gj] = c.sMM.n
            if tb + 1 < NTB:
                for s in TE_AFTER.get(j, []):
                    pstage(nctx, s, tb + 1)

            p.wait("vector", c.sMM, y_consumed[gj])

            def yadd(e, j=j, tb=tb, by=by):
                xs = c.xres[j][:, TKB * tb:TKB * (tb + 1)]
                return e.scalar_tensor_tensor(xs, by[:], c.cpool[:, C_HALF:C_HALF + 1],
                                              xs, ALU.mult, ALU.add)
            p.add("vector", yadd, c.sV, 1)
            yadd_done[gj] = c.sV.n
            if tb + 1 < NTB:
                for s in VEC_AFTER.get(j, []):
                    pstage(nctx, s, tb + 1)


def emit_p2(c, st):
    nc, p = c.nc, c.p
    sb = lambda name, shape, dt=F32: st.enter_context(nc.sbuf_tensor(name, shape, dt))
    c.qk_h = sb("qk_h", [128, NT], BF16)
    c.qk_l = sb("qk_l", [128, NT], BF16)
    xvstage = sb("xvstage", [128, 4 * 128], BF16)
    kv_base = c.sKV.n

    for tb in range(NTB):
        xln_ready = ln_block(c, tb)
        p.wait("tensor", c.sV, xln_ready)

        def qkmm(e):
            first = True
            ins = None
            for cc in range(ND):
                xh = c.xln_h[:, TKB * cc:TKB * (cc + 1)]
                xl = c.xln_l[:, TKB * cc:TKB * (cc + 1)]
                qh = c.wq_h[:, 64 * cc:64 * (cc + 1)]
                ql = c.wq_l[:, 64 * cc:64 * (cc + 1)]
                kh = c.wkv_h[:, 128 * cc:128 * cc + 64]
                kl = c.wkv_l[:, 128 * cc:128 * cc + 64]
                e.matmul(c.pb[7][0:64, :], qh, xh, start=first, stop=False)
                e.matmul(c.pb[6][0:64, :], kh, xh, start=first, stop=False)
                first = False
                e.matmul(c.pb[7][0:64, :], qh, xl, start=False, stop=False)
                e.matmul(c.pb[6][0:64, :], kh, xl, start=False, stop=False)
                e.matmul(c.pb[7][0:64, :], ql, xh, start=False, stop=(cc == ND - 1))
                ins = e.matmul(c.pb[6][0:64, :], kl, xh, start=False, stop=(cc == ND - 1))
            return ins
        p.add("tensor", qkmm, c.sMM, 1)
        p.wait("vector", c.sMM, c.sMM.n)

        def qk_split(e, tb=tb):
            cols = slice(TKB * tb, TKB * (tb + 1))
            e.tensor_copy(c.qk_h[0:64, cols], c.pb[7][0:64, :])
            e.tensor_copy(c.qk_h[64:128, cols], c.pb[6][0:64, :])
            e.tensor_sub(c.qk_l[0:64, cols], c.pb[7][0:64, :], c.qk_h[0:64, cols])
            return e.tensor_sub(c.qk_l[64:128, cols], c.pb[6][0:64, :], c.qk_h[64:128, cols])
        p.add("vector", qk_split, c.sV, 1)
        qk_done = c.sV.n

        xv_done = {}
        for t in range(4):
            bxv = c.pb[t % 2]

            def xvmm(e, t=t, bxv=bxv):
                first = True
                ins = None
                for cc in range(ND):
                    xh = c.xln_h[:, TKB * cc + 128 * t:TKB * cc + 128 * (t + 1)]
                    xl = c.xln_l[:, TKB * cc + 128 * t:TKB * cc + 128 * (t + 1)]
                    wh = c.wkv_h[:, 128 * cc + 64:128 * cc + 128]
                    wl = c.wkv_l[:, 128 * cc + 64:128 * cc + 128]
                    e.matmul(bxv[:, 0:64], xh, wh, start=first, stop=False); first = False
                    e.matmul(bxv[:, 0:64], xh, wl, start=False, stop=False)
                    ins = e.matmul(bxv[:, 0:64], xl, wh, start=False, stop=(cc == ND - 1))
                return ins
            if t >= 2:
                p.wait("tensor", c.sV, xv_done[t - 2])
            p.add("tensor", xvmm, c.sMM, 1)
            p.wait("vector", c.sMM, c.sMM.n)
            if t == 0 and tb > 0:
                p.wait("vector", c.sKV, kv_base + 16 * 4 * tb)

            def xv_split(e, t=t, bxv=bxv):
                base = 128 * t
                e.tensor_copy(xvstage[:, base:base + 64], bxv[:, 0:64])
                return e.tensor_sub(xvstage[:, base + 64:base + 128], bxv[:, 0:64],
                                    xvstage[:, base:base + 64])
            p.add("vector", xv_split, c.sV, 1)
            xv_done[t] = c.sV.n

        if True:
            p.wait("gpsimd", c.sV, qk_done)
            p.add("gpsimd", lambda e, tb=tb: e.dma_start(
                out=c.cc_in[64 * 4 * tb:64 * 4 * (tb + 1), 0:128]
                    .rearrange("(t p) f -> p t f", p=64),
                in_=c.qk_h[64:128, TKB * tb:TKB * (tb + 1)]
                    .rearrange("p (t f) -> p t f", t=4)), c.sKV, 16)
            p.add("gpsimd", lambda e, tb=tb: e.dma_start(
                out=c.cc_in[64 * 4 * tb:64 * 4 * (tb + 1), 128:256]
                    .rearrange("(t p) f -> p t f", p=64),
                in_=c.qk_l[64:128, TKB * tb:TKB * (tb + 1)]
                    .rearrange("p (t f) -> p t f", t=4)), c.sKV, 16)
            p.wait("gpsimd", c.sV, xv_done[3])
            for half in range(2):
                lt0 = 4 * tb + 2 * half

                def xvdma(e, lt0=lt0, half=half, tb=tb):
                    rows = slice(1024 + 128 * (lt0 // 2), 1024 + 128 * (lt0 // 2) + 128)
                    return e.dma_start(
                        out=c.cc_in[rows, :].rearrange("p (t f) -> p t f", t=2),
                        in_=xvstage[:, 256 * half:256 * (half + 1)]
                            .rearrange("p (t f) -> p t f", t=2))
                p.add("gpsimd", xvdma, c.sKV, 16)

    if True:
        p.wait("gpsimd", c.sKV, c.sKV.n)
        p.add("gpsimd", lambda e: e.collective_compute(
            "AllGather", ALU.bypass, replica_groups=[list(range(8))],
            ins=[c.cc_in_t.ap().opt()], outs=[c.cc_out_t.ap().opt()]), c.sCC, 1)


def emit_p3(c, st):
    nc, p = c.nc, c.p
    sb = lambda name, shape, dt=F32: st.enter_context(nc.sbuf_tensor(name, shape, dt))
    XVS = 72
    k_h = sb("k_h", [64, 64 * 128], BF16)
    k_l = sb("k_l", [64, 64 * 128], BF16)
    xvh = sb("xvh", [128, 64 * XVS], BF16)
    xvl = sb("xvl", [128, 64 * XVS], BF16)
    ebuf = [sb(f"ebuf{i}", [128, 1024]) for i in range(2)]
    eh = [sb(f"eh{i}", [128, 1024], BF16) for i in range(2)]
    el = [sb(f"el{i}", [128, 1024], BF16) for i in range(2)]
    aacc = sb("aacc", [65, 1024])
    aev = aacc
    c.masks = sb("masks", [128, 16 * 128], BF16)
    anh = sb("anh", [64, 1024], BF16)
    anl = sb("anl", [64, 1024], BF16)

    p.wait("sync", c.sCC, c.sCC.n)
    p.add("sync", lambda e: e.dma_start(
        out=c.masks[:].rearrange("p (k f) -> p k f", k=16),
        in_=c.masks_in.rearrange("k p f -> p k f")), c.sX, 16)
    masks_ld = c.sX.n

    def ones_cols(e):
        e.memset(xvh[:].rearrange("p (j x) -> p j x", j=64)[:, :, 64:72], 0.0)
        e.memset(xvl[:].rearrange("p (j x) -> p j x", j=64)[:, :, 64:72], 0.0)
        e.memset(xvh[:].rearrange("p (j x) -> p j x", j=64)[:, :, 64:65], 1.0)
        return e.memset(xvl[:].rearrange("p (j x) -> p j x", j=64)[:, :, 64:65], 0.0)
    p.add("gpsimd", ones_cols, c.sG, 1)
    ones_done = c.sG.n

    for h in range(2):
        if h == 1:
            p.wait("sync", c.sMM, c.sMM.n)
        owner_ready = {}
        for r in range(8):
            c0 = r if h == 0 else 7 - r
            for lo in (False, True):
                def kload(e, r=r, c0=c0, h=h, lo=lo):
                    dst = k_l if lo else k_h
                    cols = slice(128, 256) if lo else slice(0, 128)
                    src = c.cc_out[2048 * r + 512 * h:2048 * r + 512 * h + 512, cols]
                    src3 = src.rearrange("(i p) f -> p i f", p=64)
                    dst3 = dst[0:64, :].rearrange("p (j f) -> p j f", f=128)[:, c0:c0 + 57:8, :]
                    return e.dma_start(out=dst3, in_=src3)
                p.add("sync", kload, c.sX, 16)
            for par in (0, 1):
                for lo in (False, True):
                    def xvload(e, r=r, c0=c0, h=h, par=par, lo=lo):
                        dst = xvl if lo else xvh
                        rows0 = 2048 * r + 1024 + 128 * 4 * h
                        colh = 128 * par + (64 if lo else 0)
                        src = c.cc_out[rows0:rows0 + 512, colh:colh + 64]
                        src3 = src.rearrange("(b p) f -> p b f", p=128)
                        j0 = 8 * par + c0
                        dst3 = dst[:, XVS * j0:].rearrange("p (j x) -> p j x", x=XVS)[
                            :, 0:49:16, 0:64]
                        return e.dma_start(out=dst3, in_=src3)
                    p.add("sync", xvload, c.sX, 16)
            owner_ready[r] = c.sX.n

        e_split = {}
        e_done = {}
        a_after = {}
        for j in range(64):
            qlo = 128 * (j // 8)
            fd = 1024 - qlo
            zb = [(c.pb[4], c.pb[5]), (c.pb[6], c.pb[7]), (c.pb[2], c.pb[3])][j % 3]
            segs = segments(qlo, fd)
            r = (j % 8) if h == 0 else 7 - (j % 8)
            p.wait("tensor", c.sX, owner_ready[r])
            if j == 0:
                p.wait("tensor", c.sG, ones_done)
            if j >= 3:
                p.wait("tensor", c.sV, e_split[j - 3])

            def zmm(e, j=j, qlo=qlo, zb=zb, h=h, segs=segs):
                ins = None
                for pi, (lh, rh) in enumerate((("h", "h"), ("h", "l"), ("l", "h"))):
                    kv = k_h if lh == "h" else k_l
                    qv = c.qk_h if rh == "h" else c.qk_l
                    lhs = kv[0:64, 128 * j:128 * (j + 1)]
                    for (bh, bo, off, seg) in segs:
                        rhs = qv[0:64, 1024 * h + qlo + off:1024 * h + qlo + off + seg]
                        ins = e.matmul(zb[bh][:, bo:bo + seg], lhs, rhs,
                                       start=(pi == 0), stop=(pi == 2))
                return ins
            p.add("tensor", zmm, c.sMM, 1)
            z_done = c.sMM.n

            p.wait("vector", c.sMM, z_done)
            if h == 0 and j == 0:
                p.wait("vector", c.sX, masks_ld)

            def maskadd(e, j=j, qlo=qlo, zb=zb, h=h):
                bh = 0 if qlo < 512 else 1
                bo = qlo % 512
                mk = c.masks[:, 128 * (h * 8 + j % 8):128 * (h * 8 + j % 8 + 1)]
                t = zb[bh][:, bo:bo + 128]
                return e.tensor_add(t, t, mk)
            p.add("vector", maskadd, c.sV, 1)
            mask_done = c.sV.n

            p.wait("scalar", c.sV, mask_done)
            if j >= 2:
                p.wait("scalar", c.sV, e_split[j - 2])

            def eact(e, j=j, zb=zb, segs=segs):
                ins = None
                for (bh, bo, off, seg) in segs:
                    ins = e.activation(ebuf[j % 2][:, off:off + seg],
                                       zb[bh][:, bo:bo + seg], ACT.Exp,
                                       bias=c.cpool[:, C_NCSH:C_NCSH + 1],
                                       scale=c.cpool[:, C_SC:C_SC + 1])
                return ins
            p.add("scalar", eact, c.sC, 1)
            e_done[j] = c.sC.n
            p.wait("vector", c.sC, e_done[j])
            if j >= 2:
                p.wait("vector", c.sMM, a_after[j - 2])

            def esplit(e, j=j, fd=fd):
                hf = fd // 2
                e.tensor_copy(eh[j % 2][:, 0:hf], ebuf[j % 2][:, 0:hf])
                e.tensor_copy(eh[j % 2][:, hf:fd], ebuf[j % 2][:, hf:fd])
                e.tensor_sub(el[j % 2][:, 0:hf], ebuf[j % 2][:, 0:hf],
                             eh[j % 2][:, 0:hf])
                return e.tensor_sub(el[j % 2][:, hf:fd], ebuf[j % 2][:, hf:fd],
                                    eh[j % 2][:, hf:fd])
            p.add("vector", esplit, c.sV, 1)
            e_split[j] = c.sV.n

            p.wait("tensor", c.sV, e_split[j])
            blk_start = (j % 8 == 0)
            blk_end = (j % 8 == 7)
            if blk_start and j > 0:
                p.wait("tensor", c.sV, c.sV.n)

            def amm(e, j=j, qlo=qlo, segs=segs, blk_start=blk_start, blk_end=blk_end):
                ins = None
                for pi, (lh, rh) in enumerate((("h", "h"), ("h", "l"), ("l", "h"))):
                    xv = xvh if lh == "h" else xvl
                    ev = eh[j % 2] if rh == "h" else el[j % 2]
                    lhs = xv[:, XVS * j:XVS * j + 65]
                    for si, (bh, bo, off, seg) in enumerate(segs):
                        ins = e.matmul(c.pb[bh][0:65, bo:bo + seg], lhs,
                                       ev[:, off:off + seg],
                                       start=(blk_start and pi == 0),
                                       stop=(blk_end and pi == 2))
                return ins
            p.add("tensor", amm, c.sMM, 1)
            a_after[j] = c.sMM.n

            if blk_end:
                p.wait("vector", c.sMM, a_after[j])
                first_blk = (j < 8)
                qlo_b = 128 * ((j - 7) // 8)

                def aev_add(e, first_blk=first_blk, qlo_b=qlo_b):
                    if first_blk:
                        e.tensor_copy(aacc[:, 0:512], c.pb[0][0:65, :])
                        return e.tensor_copy(aacc[:, 512:1024], c.pb[1][0:65, :])
                    ins = None
                    for (bh, bo, off, seg) in segments(qlo_b, 1024 - qlo_b):
                        ins = e.tensor_add(aacc[:, qlo_b + off:qlo_b + off + seg],
                                           aacc[:, qlo_b + off:qlo_b + off + seg],
                                           c.pb[bh][0:65, bo:bo + seg])
                    return ins
                p.add("vector", aev_add, c.sV, 1)

        p.add("vector", lambda e: e.reciprocal(c.ln_t1[0:1, :], aev[64:65, 0:512]))
        p.add("vector", lambda e: e.reciprocal(c.ln_t2[0:1, :], aev[64:65, 512:1024]), c.sV, 1)
        p.wait("tensor", c.sV, c.sV.n)
        p.wait("tensor", c.sC, c.sC.n)
        p.add("tensor", lambda e: e.matmul(c.pb[2][0:64, :], c.onesf[:, 0:64],
                                           c.ln_t1[0:1, :], start=True, stop=True))
        p.add("tensor", lambda e: e.matmul(c.pb[3][0:64, :], c.onesf[:, 0:64],
                                           c.ln_t2[0:1, :], start=True, stop=True), c.sT, 1)
        p.wait("vector", c.sT, c.sT.n)

        def an_make(e):
            e.tensor_mul(aev[0:64, 0:512], aev[0:64, 0:512], c.pb[2][0:64, :])
            e.tensor_mul(aev[0:64, 512:1024], aev[0:64, 512:1024], c.pb[3][0:64, :])
            e.tensor_copy(anh[:], aev[0:64, :])
            return e.tensor_sub(anl[:], aev[0:64, :], anh[:])
        p.add("vector", an_make, c.sV, 1)
        an_done = c.sV.n
        p.wait("tensor", c.sV, an_done)

        out_add = {}
        for j in range(ND):
            ob = (c.pb[4], c.pb[5]) if j % 2 == 0 else (c.pb[6], c.pb[7])
            if j >= 2:
                p.wait("tensor", c.sV, out_add[j - 2])

            def omm(e, j=j, ob=ob):
                ins = None
                for pi, (lh, rh) in enumerate((("h", "h"), ("h", "l"), ("l", "h"))):
                    wv = c.wv2_h if lh == "h" else c.wv2_l
                    av = anh if rh == "h" else anl
                    lhs = wv[:, 128 * j:128 * (j + 1)]
                    for seg in range(2):
                        ins = e.matmul(ob[seg][:], lhs, av[:, 512 * seg:512 * (seg + 1)],
                                       start=(pi == 0), stop=(pi == 2))
                return ins
            p.add("tensor", omm, c.sMM, 1)
            og = c.sMM.n
            p.wait("vector", c.sMM, og)

            def oadd(e, j=j, ob=ob, h=h):
                xs0 = c.xres[j][:, 1024 * h:1024 * h + 512]
                xs1 = c.xres[j][:, 1024 * h + 512:1024 * h + 1024]
                e.tensor_add(xs0, xs0, ob[0][:])
                return e.tensor_add(xs1, xs1, ob[1][:])
            p.add("vector", oadd, c.sV, 1)
            out_add[j] = c.sV.n


def emit_p5(c, st, nbuf=2):
    nc, p = c.nc, c.p
    sb = lambda name, shape, dt=F32: st.enter_context(nc.sbuf_tensor(name, shape, dt))
    outstg = [sb(f"outstg{i}", [128, 1024]) for i in range(nbuf)]
    out_base = c.sOUT.n
    p.wait("tensor", c.sV, c.sV.n)
    p.wait("tensor", c.sC, c.sC.n)
    evac = {}
    for t in range(NTILE):
        b0 = c.pb[6] if t % 2 == 0 else c.pb[4]
        b1 = c.pb[7] if t % 2 == 0 else c.pb[5]
        if t >= 2:
            p.wait("tensor", c.sV, evac[t - 2])
        for j in range(ND):
            bnk = b0 if j < 4 else b1

            def tr(e, bnk=bnk, j=j, t=t):
                return e.transpose(bnk[:, 128 * (j % 4):128 * (j % 4 + 1)],
                                   c.xres[j][:, 128 * t:128 * (t + 1)], c.ident[:])
            p.add("tensor", tr, c.sT if j == ND - 1 else None, 1)
        grp = c.sT.n
        p.wait("vector", c.sT, grp)
        if t >= nbuf:
            p.wait("vector", c.sOUT, out_base + 16 * (t - nbuf + 1))

        def ev(e, t=t, b0=b0, b1=b1):
            e.tensor_copy(outstg[t % nbuf][:, 0:512], b0[:])
            return e.tensor_copy(outstg[t % nbuf][:, 512:1024], b1[:])
        p.add("vector", ev, c.sV, 1)
        evac[t] = c.sV.n
        p.wait("scalar", c.sV, evac[t])
        p.add("scalar", lambda e, t=t: e.dma_start(
            out=c.out[128 * t:128 * (t + 1), :], in_=outstg[t % nbuf][:]), c.sOUT, 16)
    p.wait("gpsimd", c.sOUT, out_base + 16 * NTILE)
    for s in (c.sX, c.sWA, c.sWB, c.sWHL, c.sKV):
        p.wait("gpsimd", s, s.n)


def host_prep(inputs):
    x = np.asarray(inputs["L0_attn_x"], dtype=np.float32)
    wq = np.ascontiguousarray(np.asarray(inputs["L0_attn_w_q"], dtype=np.float32))
    wk = np.asarray(inputs["L0_attn_w_k"], dtype=np.float32)
    wv1 = np.asarray(inputs["L0_attn_w_v1"], dtype=np.float32)
    wv2 = np.ascontiguousarray(np.asarray(inputs["L0_attn_w_v2"], dtype=np.float32))
    gw = np.ascontiguousarray(np.asarray(inputs["L0_ffn_gate_w"], dtype=np.float32))
    Wu = np.asarray(inputs["L0_ffn_Wu"], dtype=np.float32)
    Wv = np.asarray(inputs["L0_ffn_Wv"], dtype=np.float32)
    Wo = np.asarray(inputs["L0_ffn_Wo"], dtype=np.float32)

    Wu2 = Wu.transpose(1, 0, 2).reshape(1024, 2048)
    Wv2 = Wv.transpose(1, 0, 2).reshape(1024, 2048)
    wuv = np.empty((NER, 2, 1024, 128), dtype=np.float32)
    for k in range(NER):
        wuv[k, 0] = Wu2[:, 128 * k:128 * (k + 1)]
        wuv[k, 1] = Wv2[:, 128 * k:128 * (k + 1)]
    Wo2 = Wo.reshape(2048, 1024)
    wo = np.empty((ND, 2048, 128), dtype=np.float32)
    for j in range(ND):
        wo[j] = Wo2[:, 128 * j:128 * (j + 1)]
    wkv1 = np.ascontiguousarray(np.concatenate([wk, wv1], axis=1))

    tri = np.where(np.arange(128)[None, :] >= np.arange(128)[:, None], 0.0, -1e30
                   ).astype(np.float32)
    in_maps = []
    for core in range(8):
        t0 = [8 * i + core for i in range(8)]
        t1 = [8 * i + (7 - core) for i in range(8)]
        xs = np.concatenate([x[0, 128 * t:128 * (t + 1)] for t in t0]
                            + [x[1, 128 * t:128 * (t + 1)] for t in t1], axis=0)
        masks = np.zeros((16, 128, 128), dtype=np.float32)
        for half in range(2):
            pos = core if half == 0 else 7 - core
            for kk in range(8):
                if kk == pos:
                    masks[half * 8 + kk] = tri
                elif kk > pos:
                    masks[half * 8 + kk] = -1e30
        import ml_dtypes
        in_maps.append({
            "x_in": np.ascontiguousarray(xs),
            "wuv_in": wuv, "wo_in": wo, "wkv1_in": wkv1, "wq_in": wq,
            "wv2_in": wv2, "gw_in": gw,
            "masks_in": masks.astype(ml_dtypes.bfloat16),
        })
    return in_maps


def host_gather(results):
    out = np.empty((2, 8192, 1024), dtype=np.float32)
    for core in range(8):
        r = np.asarray(results[core]["out"])
        for i in range(8):
            t = 8 * i + core
            out[0, 128 * t:128 * (t + 1)] = r[128 * i:128 * (i + 1)]
            t = 8 * i + (7 - core)
            out[1, 128 * t:128 * (t + 1)] = r[1024 + 128 * i:1024 + 128 * (i + 1)]
    return out


def kernel(**inputs):
    in_maps = host_prep(inputs)
    nc = build()
    res = run_bass_kernel_spmd(nc, in_maps, core_ids=list(range(8)))
    return host_gather(res.results)


# version 06
# speedup vs baseline: 1.0750x; 1.0336x over previous
import numpy as np
from contextlib import ExitStack
import concourse.bass as bass
import concourse.mybir as mybir
from concourse.bass_utils import run_bass_kernel_spmd

F32 = mybir.dt.float32
BF16 = mybir.dt.bfloat16
I32 = mybir.dt.int32
ACT = mybir.ActivationFunctionType
ALU = mybir.AluOpType
AXX = mybir.AxisListType.X

ND = 8
NTB = 4
TKB = 512
NTILE = 16
NER = 16
NT = 2048
CSHIFT = 20.0
ENGINES = ("sync", "gpsimd", "tensor", "vector", "scalar")

C_NCSH, C_SC, C_HALF, C_ONE, C_EPS, C_INV1024, C_NHALF, C_3HALF, C_BIGNEG, C_ZERO, C_NONE = range(11)
CONSTS = [-CSHIFT, 0.125, 0.5, 1.0, 1e-5, 1.0 / 1024.0, -0.5, 1.5, -1e30, 0.0, -1.0]



class Sem:
    __slots__ = ("h", "n", "name")

    def __init__(self, st, nc, name):
        self.h = st.enter_context(nc.semaphore(name))
        self.n = 0
        self.name = name


class P:
    def __init__(self, nc):
        self.nc = nc
        self.ops = {e: [] for e in ENGINES}
        self.waitlog = []

    def check(self):
        bad = []
        for eng, sem, val in self.waitlog:
            if val > sem.n:
                bad.append((eng, sem.name, val, sem.n))
        return bad

    def add(self, eng, fn, sem=None, k=1):
        if sem is None:
            self.ops[eng].append(fn)
        else:
            sem.n += k

            def wrapped(e, fn=fn, sem=sem, k=k):
                fn(e).then_inc(sem.h, k)
            self.ops[eng].append(wrapped)

    def wait(self, eng, sem, val):
        if val <= 0:
            return
        self.waitlog.append((eng, sem, val))
        self.ops[eng].append(lambda e, h=sem.h, v=val: e.wait_ge(h, v))

    def emit(self):
        with self.nc.Block() as block:
            for eng in ENGINES:
                ops = self.ops[eng]
                deco = getattr(block, eng)

                def body(e, ops=ops):
                    for fn in ops:
                        fn(e)
                deco(body)


class Ctx:
    pass


def fence(c, engines=ENGINES):
    p = c.p
    sems = [c.sV, c.sC, c.sMM, c.sT, c.sX, c.sWA, c.sWB, c.sWHL, c.sKV, c.sOUT, c.sCC, c.sG, c.sWAg, c.sWBg]
    for eng in engines:
        for s in sems:
            p.wait(eng, s, s.n)


def segments(qlo, fd):
    out = []
    off = 0
    while off < fd:
        zcol = qlo + off
        bh = 0 if zcol < 512 else 1
        bo = zcol % 512
        seg = min(512 - bo, fd - off)
        out.append((bh, bo, off, seg))
        off += seg
    return out


def build():
    nc = bass.Bass("TRN2", target_bir_lowering=False, detect_race_conditions=False)
    c = Ctx()
    c.nc = nc
    p = P(nc)
    c.p = p

    c.x_in = nc.dram_tensor("x_in", [NT, 1024], F32, kind="ExternalInput").ap()
    c.wuv_in = nc.dram_tensor("wuv_in", [NER, 2, 1024, 128], F32, kind="ExternalInput").ap()
    c.wo_in = nc.dram_tensor("wo_in", [ND, 2048, 128], F32, kind="ExternalInput").ap()
    c.wkv1_in = nc.dram_tensor("wkv1_in", [1024, 128], F32, kind="ExternalInput").ap()
    c.wq_in = nc.dram_tensor("wq_in", [1024, 64], F32, kind="ExternalInput").ap()
    c.wv2_in = nc.dram_tensor("wv2_in", [64, 1024], F32, kind="ExternalInput").ap()
    c.gw_in = nc.dram_tensor("gw_in", [1024, 8], F32, kind="ExternalInput").ap()
    c.masks_in = nc.dram_tensor("masks_in", [16, 128, 128], BF16, kind="ExternalInput").ap()
    c.out = nc.dram_tensor("out", [NT, 1024], F32, kind="ExternalOutput").ap()
    c.wuv_hl = nc.dram_tensor("wuv_hl", [NER, 2, 1024, 256], BF16).ap()
    c.wo_hl = nc.dram_tensor("wo_hl", [ND, 2048, 256], BF16).ap()
    c.cc_in_t = nc.dram_tensor("cc_in", [NT, 256], BF16)
    c.cc_out_t = nc.dram_tensor("cc_out", [8 * NT, 256], BF16)
    c.cc_in = c.cc_in_t.ap()
    c.cc_out = c.cc_out_t.ap()

    with ExitStack() as st:
        sb = lambda name, shape, dt=F32: st.enter_context(nc.sbuf_tensor(name, shape, dt))
        c.xres = [sb(f"xres{j}", [128, NT]) for j in range(ND)]
        c.xln_h = sb("xln_h", [128, ND * TKB], BF16)
        c.xln_l = sb("xln_l", [128, ND * TKB], BF16)
        c.ident = sb("ident", [128, 128])
        c.onesf = sb("onesf", [1, 128])
        c.onesb = sb("onesb", [1, 128], BF16)
        c.cpool = sb("cpool", [128, 16])
        c.cpool_i = sb("cpool_i", [128, 2], I32)
        c.stat_tm = sb("stat_tm", [128, 2 * NTILE])
        c.rowbuf = sb("rowbuf", [2, TKB])
        c.ln_scr = sb("ln_scr", [128, 512])
        c.ln_t1 = sb("ln_t1", [128, TKB])
        c.ln_t2 = sb("ln_t2", [128, TKB])
        c.gbuf = sb("gbuf", [8, TKB])
        c.g_tm = sb("g_tm", [128, 32])
        c.g_rows = sb("g_rows", [8, TKB])
        c.grh = sb("grh", [8, TKB], BF16)
        c.grl = sb("grl", [8, TKB], BF16)
        c.dbuf = sb("dbuf", [128, 4])
        c.wbuf2 = sb("wbuf2", [128, 4])
        c.gsc = sb("gsc", [128, 16])
        c.gsc2 = sb("gsc2", [128, 16])
        c.gt = [sb(f"gt{i}", [128, 8]) for i in range(4)]
        c.gt2 = [sb(f"gt2_{i}", [128, 8]) for i in range(4)]
        c.selb = sb("selb", [8, 8 * 128], BF16)
        c.self32 = sb("self32", [2, 2 * 128])
        c.wq_h = sb("wq_h", [128, ND * 64], BF16)
        c.wq_l = sb("wq_l", [128, ND * 64], BF16)
        c.wkv_h = sb("wkv_h", [128, ND * 128], BF16)
        c.wkv_l = sb("wkv_l", [128, ND * 128], BF16)
        c.wv2_h = sb("wv2_h", [64, 1024], BF16)
        c.wv2_l = sb("wv2_l", [64, 1024], BF16)
        c.gw_h = sb("gw_h", [128, ND * 8], BF16)
        c.gw_l = sb("gw_l", [128, ND * 8], BF16)
        c.sil = [sb(f"sil{i}", [128, TKB]) for i in range(2)]
        c.s_t1 = sb("s_t1", [128, TKB])
        c.s_t2 = sb("s_t2", [128, TKB])

        c.pb = [st.enter_context(nc.psum_tensor(f"pb{i}", [128, 512], F32)) for i in range(8)]

        S = lambda name: Sem(st, nc, name)
        c.sX = S("sX"); c.sT = S("sT"); c.sV = S("sV"); c.sC = S("sC")
        c.sMM = S("sMM"); c.sWA = S("sWA"); c.sWB = S("sWB"); c.sWHL = S("sWHL")
        c.sWAg = S("sWAg"); c.sWBg = S("sWBg")
        c.sCC = S("sCC"); c.sOUT = S("sOUT"); c.sKV = S("sKV"); c.sG = S("sG")

        with ExitStack() as stP0:
            emit_p0(c, stP0)
            fence(c)
        emit_stats(c)
        with ExitStack() as stA:
            emit_ffn(c, stA, call=0)
            fence(c)
        emit_stats(c)
        with ExitStack() as stB:
            emit_p2(c, stB)
            emit_p3(c, stB)
            fence(c)
        emit_stats(c)
        with ExitStack() as stC:
            emit_ffn(c, stC, call=1)
            emit_p5(c, stC)
        p.emit()
    return nc


def emit_p0(c, st):
    nc, p = c.nc, c.p
    sb = lambda name, shape, dt=F32: st.enter_context(nc.sbuf_tensor(name, shape, dt))
    xtm = [sb(f"xtm{t}", [128, 1024]) for t in range(NTILE)]
    c.wsm = sb("wsm", [128, 1024])
    iota_c = sb("iota_c", [128, 128])
    iota_p = sb("iota_p", [128, 1])
    iota_e8 = sb("iota_e8", [8, 8 * 128])
    iota_e2 = sb("iota_e2", [2, 2 * 128])

    def g_consts(e):
        e.iota(iota_c[:], [[1, 128]], channel_multiplier=0, allow_small_or_imprecise_dtypes=True)
        e.iota(iota_p[:], [[0, 1]], channel_multiplier=1, allow_small_or_imprecise_dtypes=True)
        e.memset(c.onesf[:], 1.0)
        e.memset(c.onesb[:], 1.0)
        for i, v in enumerate(CONSTS):
            e.memset(c.cpool[:, i:i + 1], v)
        e.iota(iota_e8[:].rearrange("p (b f) -> p b f", b=8), [[1, 8], [0, 128]],
               channel_multiplier=0, allow_small_or_imprecise_dtypes=True)
        e.iota(iota_e2[:].rearrange("p (b f) -> p b f", b=2), [[1, 2], [0, 128]],
               channel_multiplier=0, allow_small_or_imprecise_dtypes=True)
        return e.memset(c.onesb[:], 1.0)
    p.add("gpsimd", g_consts, c.sV, 1)
    consts_done = c.sV.n

    xld = []
    for t in range(NTILE):
        p.add("sync", lambda e, t=t: e.dma_start(
            out=xtm[t][:], in_=c.x_in[128 * t:128 * (t + 1), :]), c.sX, 16)
        xld.append(c.sX.n)
    smalls = [
        (lambda: c.wsm[:, 0:512].rearrange("p (cc e) -> p cc e", cc=8),
         lambda: c.wq_in.rearrange("(cc p) e -> p cc e", p=128), c.wq_h, c.wq_l, 128, 512),
        (lambda: c.wsm[:, 0:1024].rearrange("p (cc e) -> p cc e", cc=8),
         lambda: c.wkv1_in.rearrange("(cc p) e -> p cc e", p=128), c.wkv_h, c.wkv_l, 128, 1024),
        (lambda: c.wsm[:, 0:64].rearrange("p (cc e) -> p cc e", cc=8),
         lambda: c.gw_in.rearrange("(cc p) e -> p cc e", p=128), c.gw_h, c.gw_l, 128, 64),
        (lambda: c.wsm[:64, 0:1024], lambda: c.wv2_in, c.wv2_h, c.wv2_l, 64, 1024),
    ]
    for i, (dst, src, hi, lo, rows, cols) in enumerate(smalls):
        p.add("sync", lambda e, dst=dst, src=src: e.dma_start(out=dst(), in_=src()), c.sX, 16)
        ld = c.sX.n
        p.wait("vector", c.sX, ld)
        p.add("vector", lambda e, hi=hi, rows=rows, cols=cols: e.tensor_copy(
            hi[:], c.wsm[:rows, 0:cols]))
        p.add("vector", lambda e, lo=lo, hi=hi, rows=rows, cols=cols: e.tensor_sub(
            lo[:], c.wsm[:rows, 0:cols], hi[:]), c.sV, 1)
        if i + 1 < len(smalls):
            p.wait("sync", c.sV, c.sV.n)

    p.wait("vector", c.sV, consts_done)
    p.add("vector", lambda e: e.tensor_scalar(
        c.ident[:], iota_c[:], iota_p[:], None, ALU.is_equal))
    p.add("vector", lambda e: e.tensor_scalar(
        c.selb[:], iota_e8[:], iota_p[0:8, :], None, ALU.is_equal))
    p.add("vector", lambda e: e.tensor_scalar(
        c.self32[:], iota_e2[:], iota_p[0:2, :], None, ALU.is_equal), c.sV, 1)
    ident_done = c.sV.n

    p.wait("tensor", c.sV, ident_done)
    evac = {}
    for g in range(NTB * ND):
        tb, j = divmod(g, ND)
        bank = c.pb[g % 4]
        if g >= 4:
            p.wait("tensor", c.sV, evac[g - 4])
        for t in range(4):
            tile = 4 * tb + t
            p.wait("tensor", c.sX, xld[tile])

            def tr(e, bank=bank, t=t, tile=tile, j=j):
                return e.transpose(bank[:, 128 * t:128 * (t + 1)],
                                   xtm[tile][:, 128 * j:128 * (j + 1)], c.ident[:])
            p.add("tensor", tr, c.sT if t == 3 else None, 1)
        grp = c.sT.n
        p.wait("vector", c.sT, grp)
        p.add("vector", lambda e, tb=tb, j=j, bank=bank: e.tensor_copy(
            c.xres[j][:, TKB * tb:TKB * (tb + 1)], bank[:]), c.sV, 1)
        evac[g] = c.sV.n


def emit_stats(c):
    p = c.p
    p.wait("tensor", c.sV, c.sV.n)
    p.wait("tensor", c.sC, c.sC.n)
    vec_done = {}
    for tp in range(NTILE // 2):
        ta, tb_ = 2 * tp, 2 * tp + 1
        if tp % 2 == 0:
            bA, bB = (c.pb[6], c.pb[7]), (c.pb[4], c.pb[5])
        else:
            bA, bB = (c.pb[0], c.pb[1]), (c.pb[2], c.pb[3])
        if tp >= 2:
            p.wait("tensor", c.sV, vec_done[tp - 2])
            p.wait("tensor", c.sC, c.sC.n - 4)
        for t, banks in ((ta, bA), (tb_, bB)):
            for j in range(ND):
                bnk = banks[0] if j < 4 else banks[1]

                def tr(e, bnk=bnk, j=j, t=t):
                    return e.transpose(bnk[:, 128 * (j % 4):128 * (j % 4 + 1)],
                                       c.xres[j][:, 128 * t:128 * (t + 1)], c.ident[:])
                p.add("tensor", tr, c.sT if j == ND - 1 else None, 1)
        grp = c.sT.n
        p.wait("scalar", c.sT, grp)
        p.add("scalar", lambda e, bA=bA: e.activation(
            c.ln_scr[:, 0:512], bA[0][:], ACT.Square,
            accum_out=c.gsc[:, 8:9]))
        p.add("scalar", lambda e, bA=bA: e.activation(
            c.ln_scr[:, 0:512], bA[1][:], ACT.Square,
            accum_out=c.gsc[:, 9:10]), c.sC, 1)
        p.add("scalar", lambda e, bB=bB: e.activation(
            c.ln_scr[:, 0:512], bB[0][:], ACT.Square,
            accum_out=c.gsc2[:, 8:9]))
        p.add("scalar", lambda e, bB=bB: e.activation(
            c.ln_scr[:, 0:512], bB[1][:], ACT.Square,
            accum_out=c.gsc2[:, 9:10]), c.sC, 1)
        sq_done = c.sC.n
        p.wait("vector", c.sT, grp)
        p.wait("vector", c.sC, sq_done)
        p.add("vector", lambda e, bA=bA: e.reduce_sum(c.gsc[:, 0:1], bA[0][:], axis=AXX))
        p.add("vector", lambda e, bB=bB: e.reduce_sum(c.gsc2[:, 0:1], bB[0][:], axis=AXX))
        p.add("vector", lambda e, bA=bA: e.reduce_sum(c.gsc[:, 1:2], bA[1][:], axis=AXX))
        p.add("vector", lambda e, bB=bB: e.reduce_sum(c.gsc2[:, 1:2], bB[1][:], axis=AXX))

        def stats_ab(e):
            cp = c.cpool
            A, B = c.gsc, c.gsc2
            for g in (A, B):
                e.tensor_scalar(g[:, 2:3], g[:, 0:1], g[:, 1:2], None, ALU.add)
            for g in (A, B):
                e.tensor_scalar(g[:, 3:4], g[:, 8:9], g[:, 9:10], None, ALU.add)
            for g in (A, B):
                e.tensor_scalar(g[:, 4:5], g[:, 2:3], cp[:, C_INV1024:C_INV1024 + 1],
                                None, ALU.mult)
            for g in (A, B):
                e.tensor_scalar(g[:, 5:6], g[:, 3:4], cp[:, C_INV1024:C_INV1024 + 1],
                                cp[:, C_EPS:C_EPS + 1], ALU.mult, ALU.add)
            for g in (A, B):
                e.tensor_scalar(g[:, 6:7], g[:, 4:5], g[:, 4:5], None, ALU.mult)
            for g in (A, B):
                return_ins = e.tensor_scalar(g[:, 7:8], g[:, 5:6], g[:, 6:7],
                                             None, ALU.subtract)
            return return_ins
        p.add("vector", stats_ab, c.sV, 1)
        v_ready = c.sV.n
        p.wait("scalar", c.sV, v_ready)
        p.add("scalar", lambda e: e.activation(c.gsc[:, 10:11], c.gsc[:, 7:8], ACT.Sqrt))
        p.add("scalar", lambda e: e.activation(c.gsc2[:, 10:11], c.gsc2[:, 7:8],
                                               ACT.Sqrt), c.sC, 1)
        sqrt_done = c.sC.n
        p.wait("vector", c.sC, sqrt_done)

        def newton_ab(e, ta=ta, tb_=tb_):
            cp = c.cpool
            A, B = c.gsc, c.gsc2
            e.reciprocal(A[:, 15:16], A[:, 10:11])
            e.reciprocal(B[:, 15:16], B[:, 10:11])
            for _ in range(2):
                for g in (A, B):
                    e.tensor_scalar(g[:, 11:12], g[:, 15:16], g[:, 15:16], None, ALU.mult)
                for g in (A, B):
                    e.tensor_scalar(g[:, 12:13], g[:, 11:12], g[:, 7:8], None, ALU.mult)
                for g in (A, B):
                    e.tensor_scalar(g[:, 13:14], g[:, 12:13], cp[:, C_NHALF:C_NHALF + 1],
                                    cp[:, C_3HALF:C_3HALF + 1], ALU.mult, ALU.add)
                for g in (A, B):
                    e.tensor_scalar(g[:, 15:16], g[:, 15:16], g[:, 13:14], None, ALU.mult)
            e.tensor_copy(c.stat_tm[:, 2 * ta:2 * ta + 1], A[:, 15:16])
            e.tensor_copy(c.stat_tm[:, 2 * tb_:2 * tb_ + 1], B[:, 15:16])
            e.tensor_scalar(c.stat_tm[:, 2 * ta + 1:2 * ta + 2], A[:, 15:16], A[:, 4:5],
                            None, ALU.mult)
            return e.tensor_scalar(c.stat_tm[:, 2 * tb_ + 1:2 * tb_ + 2], B[:, 15:16],
                                   B[:, 4:5], None, ALU.mult)
        p.add("vector", newton_ab, c.sV, 1)
        vec_done[tp] = c.sV.n
    c.stats_done = c.sV.n


def ln_block(c, tb):
    p = c.p
    p.wait("tensor", c.sV, c.sV.n)
    p.wait("tensor", c.sC, c.sC.n)
    for t in range(4):
        tile = 4 * tb + t

        def tr(e, t=t, tile=tile):
            return e.transpose(c.pb[5][0:2, 128 * t:128 * (t + 1)],
                               c.stat_tm[:, 2 * tile:2 * tile + 2], c.ident[:])
        p.add("tensor", tr, c.sT if t == 3 else None, 1)
    p.wait("vector", c.sT, c.sT.n)
    p.add("vector", lambda e: e.tensor_copy(c.rowbuf[:], c.pb[5][0:2, 0:TKB]), c.sV, 1)
    p.wait("tensor", c.sV, c.sV.n)
    p.add("tensor", lambda e: e.matmul(c.pb[4][:], c.self32[:, 0:128], c.rowbuf[:],
                                       start=True, stop=True))
    p.add("tensor", lambda e: e.matmul(c.pb[6][:], c.self32[:, 128:256], c.rowbuf[:],
                                       start=True, stop=True), c.sT, 1)
    p.wait("vector", c.sT, c.sT.n)
    for j in range(ND):
        def apply(e, j=j, tb=tb):
            xs = c.xres[j][:, TKB * tb:TKB * (tb + 1)]
            e.tensor_mul(c.ln_t1[:], xs, c.pb[4][:])
            e.tensor_sub(c.ln_t2[:], c.ln_t1[:], c.pb[6][:])
            e.tensor_copy(c.xln_h[:, TKB * j:TKB * (j + 1)], c.ln_t2[:])
            return e.tensor_sub(c.xln_l[:, TKB * j:TKB * (j + 1)], c.ln_t2[:],
                                c.xln_h[:, TKB * j:TKB * (j + 1)])
        p.add("vector", apply, c.sV if j == ND - 1 else None, 1)
    return c.sV.n


def emit_ffn(c, st, call):
    nc, p = c.nc, c.p
    sb = lambda name, shape, dt=F32: st.enter_context(nc.sbuf_tensor(name, shape, dt))
    s_h = sb(f"s_h{call}", [128, NER * TKB], BF16)
    s_l = sb(f"s_l{call}", [128, NER * TKB], BF16)
    wuvc = [sb(f"wuvc{call}_{i}", [128, 4096], BF16) for i in range(2)]
    woc = [sb(f"woc{call}_{i}", [128, 4096], BF16) for i in range(2)]
    fresh = (call == 0)
    if fresh:
        wst = [sb(f"wst{i}", [128, 2048]) for i in range(2)]
        wost = wst

    uv_consumed = {}
    y_consumed = {}
    schain_done = {}
    yadd_done = {}
    split_w = {}
    split_wo = {}
    whl_written = {}
    wohl_written = {}
    wa_ld = {}
    wb_ld = {}

    def gate_mm(e):
        first = True
        ins = None
        for cc in range(ND):
            gh = c.gw_h[:, 8 * cc:8 * (cc + 1)]
            gl = c.gw_l[:, 8 * cc:8 * (cc + 1)]
            xh = c.xln_h[:, TKB * cc:TKB * (cc + 1)]
            xl = c.xln_l[:, TKB * cc:TKB * (cc + 1)]
            e.matmul(c.pb[7][0:8, :], gh, xh, start=first, stop=False); first = False
            e.matmul(c.pb[7][0:8, :], gh, xl, start=False, stop=False)
            ins = e.matmul(c.pb[7][0:8, :], gl, xh, start=False, stop=(cc == ND - 1))
        return ins

    def gate_math1(e):
        cp = c.cpool
        ins = None
        for tp in range(2):
            tA, tB = 2 * tp, 2 * tp + 1
            lgA = c.g_tm[:, 8 * tA:8 * (tA + 1)]
            lgB = c.g_tm[:, 8 * tB:8 * (tB + 1)]
            A, B = c.gsc, c.gsc2
            gA, gB = c.gt, c.gt2
            e.reduce_max(A[:, 0:1], lgA, axis=AXX)
            e.reduce_max(B[:, 0:1], lgB, axis=AXX)
            e.tensor_scalar(gA[0][:], lgA, A[:, 0:1], None, ALU.subtract)
            e.tensor_scalar(gB[0][:], lgB, B[:, 0:1], None, ALU.subtract)
            e.tensor_scalar(gA[1][:], gA[0][:], cp[:, C_ZERO:C_ZERO + 1], None, ALU.is_equal)
            e.tensor_scalar(gB[1][:], gB[0][:], cp[:, C_ZERO:C_ZERO + 1], None, ALU.is_equal)
            e.scalar_tensor_tensor(gA[2][:], gA[1][:], cp[:, C_BIGNEG:C_BIGNEG + 1],
                                   gA[0][:], ALU.mult, ALU.add)
            e.scalar_tensor_tensor(gB[2][:], gB[1][:], cp[:, C_BIGNEG:C_BIGNEG + 1],
                                   gB[0][:], ALU.mult, ALU.add)
            e.reduce_max(c.dbuf[:, tA:tA + 1], gA[2][:], axis=AXX)
            e.reduce_max(c.dbuf[:, tB:tB + 1], gB[2][:], axis=AXX)
            e.tensor_scalar(gA[3][:], gA[2][:], c.dbuf[:, tA:tA + 1], None, ALU.is_equal)
            e.tensor_scalar(gB[3][:], gB[2][:], c.dbuf[:, tB:tB + 1], None, ALU.is_equal)
            e.tensor_copy(c.ln_t1[:, 16 * tA:16 * tA + 8], gA[1][:])
            e.tensor_copy(c.ln_t1[:, 16 * tB:16 * tB + 8], gB[1][:])
            e.tensor_copy(c.ln_t1[:, 16 * tA + 8:16 * tA + 16], gA[3][:])
            ins = e.tensor_copy(c.ln_t1[:, 16 * tB + 8:16 * tB + 16], gB[3][:])
        return ins

    def gate_math2(e):
        cp = c.cpool
        ins = None
        for tp in range(2):
            tA, tB = 2 * tp, 2 * tp + 1
            A, B = c.gsc, c.gsc2
            gA, gB = c.gt, c.gt2
            w2A = c.wbuf2[:, tA:tA + 1]
            w2B = c.wbuf2[:, tB:tB + 1]
            e.tensor_scalar(A[:, 0:1], w2A, cp[:, C_NONE:C_NONE + 1],
                            cp[:, C_ONE:C_ONE + 1], ALU.mult, ALU.add)
            e.tensor_scalar(B[:, 0:1], w2B, cp[:, C_NONE:C_NONE + 1],
                            cp[:, C_ONE:C_ONE + 1], ALU.mult, ALU.add)
            e.tensor_scalar(gA[0][:], c.ln_t1[:, 16 * tA:16 * tA + 8], A[:, 0:1],
                            None, ALU.mult)
            e.tensor_scalar(gB[0][:], c.ln_t1[:, 16 * tB:16 * tB + 8], B[:, 0:1],
                            None, ALU.mult)
            e.scalar_tensor_tensor(c.g_tm[:, 8 * tA:8 * (tA + 1)],
                                   c.ln_t1[:, 16 * tA + 8:16 * tA + 16], w2A,
                                   gA[0][:], ALU.mult, ALU.add)
            ins = e.scalar_tensor_tensor(c.g_tm[:, 8 * tB:8 * (tB + 1)],
                                         c.ln_t1[:, 16 * tB + 8:16 * tB + 16], w2B,
                                         gB[0][:], ALU.mult, ALU.add)
        return ins

    def pstage(ctx, s, tb):
        if s == 0:
            for t in range(4):
                tile = 4 * tb + t

                def tr(e, t=t, tile=tile):
                    return e.transpose(c.pb[5][0:2, 128 * t:128 * (t + 1)],
                                       c.stat_tm[:, 2 * tile:2 * tile + 2], c.ident[:])
                p.add("tensor", tr, c.sT if t == 3 else None, 1)
            ctx[0] = c.sT.n
        elif s == 1:
            p.wait("vector", c.sT, ctx[0])
            p.add("vector", lambda e: e.tensor_copy(c.rowbuf[:], c.pb[5][0:2, 0:TKB]),
                  c.sV, 1)
            ctx[1] = c.sV.n
        elif s == 2:
            p.wait("tensor", c.sV, ctx[1])
            p.add("tensor", lambda e: e.matmul(c.pb[4][:], c.self32[:, 0:128],
                                               c.rowbuf[:], start=True, stop=True))
            p.add("tensor", lambda e: e.matmul(c.pb[6][:], c.self32[:, 128:256],
                                               c.rowbuf[:], start=True, stop=True),
                  c.sT, 1)
            ctx[2] = c.sT.n
        elif s == 3:
            p.wait("vector", c.sT, ctx[2])
            for j in range(ND):
                def apply(e, j=j, tb=tb):
                    xs = c.xres[j][:, TKB * tb:TKB * (tb + 1)]
                    e.tensor_mul(c.ln_t1[:], xs, c.pb[4][:])
                    e.tensor_sub(c.ln_t2[:], c.ln_t1[:], c.pb[6][:])
                    e.tensor_copy(c.xln_h[:, TKB * j:TKB * (j + 1)], c.ln_t2[:])
                    return e.tensor_sub(c.xln_l[:, TKB * j:TKB * (j + 1)], c.ln_t2[:],
                                        c.xln_h[:, TKB * j:TKB * (j + 1)])
                p.add("vector", apply, c.sV if j == ND - 1 else None, 1)
            ctx["xln"] = c.sV.n
        elif s == 4:
            p.wait("tensor", c.sV, ctx["xln"])
            p.add("tensor", gate_mm, c.sMM, 1)
            ctx[4] = c.sMM.n
        elif s == 5:
            p.wait("vector", c.sMM, ctx[4])
            p.add("vector", lambda e: e.tensor_copy(c.gbuf[:], c.pb[7][0:8, :]), c.sV, 1)
            ctx[5] = c.sV.n
        elif s == 6:
            p.wait("tensor", c.sV, ctx[5])
            for t in range(4):
                def tr(e, t=t):
                    return e.transpose(c.pb[5][:, 8 * t:8 * (t + 1)],
                                       c.gbuf[0:8, 128 * t:128 * (t + 1)],
                                       c.ident[0:8, 0:8])
                p.add("tensor", tr, c.sT if t == 3 else None, 1)
            ctx[6] = c.sT.n
        elif s == 7:
            p.wait("vector", c.sT, ctx[6])
            p.add("vector", lambda e: e.tensor_copy(c.g_tm[:, 0:16], c.pb[5][:, 0:16]))
            p.add("vector", lambda e: e.tensor_copy(c.g_tm[:, 16:32], c.pb[5][:, 16:32]))
            p.add("vector", gate_math1, c.sV, 1)
            ctx[7] = c.sV.n
        elif s == 8:
            p.wait("scalar", c.sV, ctx[7])
            p.add("scalar", lambda e: e.activation(c.wbuf2[:], c.dbuf[:], ACT.Sigmoid),
                  c.sC, 1)
            ctx[8] = c.sC.n
        elif s == 9:
            p.wait("vector", c.sC, ctx[8])
            p.add("vector", gate_math2, c.sV, 1)
            ctx[9] = c.sV.n
        elif s == 10:
            p.wait("tensor", c.sV, ctx[9])
            for t in range(4):
                def tr(e, t=t):
                    return e.transpose(c.pb[5][0:8, 128 * t:128 * (t + 1)],
                                       c.g_tm[:, 8 * t:8 * (t + 1)], c.ident[:])
                p.add("tensor", tr, c.sT if t == 3 else None, 1)
            ctx[10] = c.sT.n
        elif s == 11:
            p.wait("vector", c.sT, ctx[10])
            p.add("vector", lambda e: e.tensor_copy(c.g_rows[:], c.pb[5][0:8, 0:TKB]))
            p.add("vector", lambda e: e.tensor_copy(c.grh[:], c.g_rows[:]))
            p.add("vector", lambda e: e.tensor_sub(c.grl[:], c.g_rows[:], c.grh[:]),
                  c.sV, 1)
            ctx["grows"] = c.sV.n

    TE_AFTER = {0: [0], 1: [2], 2: [4], 3: [6], 5: [10]}
    VEC_AFTER = {0: [1], 1: [3], 2: [5], 3: [7, 8], 4: [9], 5: [11]}

    nctx = None
    for tb in range(NTB):
        if tb == 0:
            ctx = {}
            p.wait("tensor", c.sV, c.sV.n)
            p.wait("tensor", c.sC, c.sC.n)
            for s in range(12):
                pstage(ctx, s, tb)
        else:
            ctx = nctx
        xln_ready = ctx["xln"]
        grows_done = ctx["grows"]

        for k in range(NER):
            gch = tb * NER + k
            slot = gch % 2
            e_idx = k // 2
            qeng = "sync"
            qsem = c.sWA if k % 2 == 0 else c.sWAg
            if fresh and tb == 0:
                if gch >= 2:
                    p.wait(qeng, c.sV, split_w[gch - 2])
                p.add(qeng, lambda e, k=k, slot=slot: e.dma_start(
                    out=wst[slot][:].rearrange("p (u cc e) -> p u cc e", u=2, cc=8),
                    in_=c.wuv_in[k].rearrange("u (cc p) e -> p u cc e", p=128)), qsem, 16)
                wa_ld[gch] = (qsem, qsem.n)
                p.wait("vector", *wa_ld[gch])
                if gch >= 2:
                    p.wait("vector", c.sMM, uv_consumed[gch - 2])
                    p.wait("vector", c.sWHL, whl_written[gch - 2])

                def spl(e, slot=slot):
                    wv = wst[slot][:].rearrange("p (u cc e) -> p u cc e", u=2, cc=8)
                    comb = wuvc[slot][:].rearrange("p (u cc x) -> p u cc x", u=2, cc=8)
                    e.tensor_copy(comb[:, :, :, 0:128], wv)
                    return e.tensor_sub(comb[:, :, :, 128:256], wv, comb[:, :, :, 0:128])
                p.add("vector", spl, c.sV, 1)
                split_w[gch] = c.sV.n
                p.wait("scalar", c.sV, split_w[gch])
                p.add("scalar", lambda e, k=k, slot=slot: e.dma_start(
                    out=c.wuv_hl[k].rearrange("u (cc p) x -> p u cc x", p=128),
                    in_=wuvc[slot][:].rearrange("p (u cc x) -> p u cc x", u=2, cc=8)),
                    c.sWHL, 16)
                whl_written[gch] = c.sWHL.n
            else:
                if fresh and tb == 1 and k < 2:
                    p.wait(qeng, c.sWHL, whl_written[NER - 1])
                if gch >= 2 and (gch - 2) in uv_consumed:
                    p.wait(qeng, c.sMM, uv_consumed[gch - 2])
                p.add(qeng, lambda e, k=k, slot=slot: e.dma_start(
                    out=wuvc[slot][:].rearrange("p (u cc x) -> p u cc x", u=2, cc=8),
                    in_=c.wuv_hl[k].rearrange("u (cc p) x -> p u cc x", p=128)), qsem, 16)
                wa_ld[gch] = (qsem, qsem.n)

            gb = c.pb[4] if e_idx % 2 == 0 else c.pb[6]
            bu = c.pb[(k % 2) * 2]
            bv = c.pb[(k % 2) * 2 + 1]
            p.wait("tensor", *wa_ld[gch])
            if fresh and tb == 0:
                p.wait("tensor", c.sV, split_w[gch])
            if gch >= 2 and (gch - 2) in schain_done:
                p.wait("tensor", c.sV, schain_done[gch - 2])

            def uvmm(e, slot=slot, bu=bu, bv=bv):
                ins = None
                for uv, bank in ((0, bu), (1, bv)):
                    first = True
                    for cc in range(ND):
                        off = (uv * ND + cc) * 256
                        wh = wuvc[slot][:, off:off + 128]
                        wl = wuvc[slot][:, off + 128:off + 256]
                        xh = c.xln_h[:, TKB * cc:TKB * (cc + 1)]
                        xl = c.xln_l[:, TKB * cc:TKB * (cc + 1)]
                        e.matmul(bank[:], wh, xh, start=first, stop=False); first = False
                        e.matmul(bank[:], wh, xl, start=False, stop=False)
                        ins = e.matmul(bank[:], wl, xh, start=False, stop=(cc == ND - 1))
                return ins
            p.add("tensor", uvmm, c.sMM, 1)
            uv_consumed[gch] = c.sMM.n

            if k % 2 == 0:
                p.wait("tensor", c.sV, grows_done)
                if gch - 3 in schain_done:
                    p.wait("tensor", c.sV, schain_done[gch - 3])

                def gbc(e, gb=gb, e_idx=e_idx):
                    sel = c.selb[:, 128 * e_idx:128 * (e_idx + 1)]
                    e.matmul(gb[:], sel, c.grh[:], start=True, stop=False)
                    return e.matmul(gb[:], sel, c.grl[:], start=False, stop=True)
                p.add("tensor", gbc, c.sT, 1)
            gbc_done = c.sT.n

            p.wait("scalar", c.sMM, uv_consumed[gch])
            p.add("scalar", lambda e, k=k, bu=bu: e.activation(
                c.sil[k % 2][:], bu[:], ACT.Silu), c.sC, 1)
            sil_done = c.sC.n

            p.wait("vector", c.sC, sil_done)
            p.wait("vector", c.sT, gbc_done)

            def schain(e, k=k, bv=bv, gb=gb):
                e.tensor_mul(c.s_t1[:], c.sil[k % 2][:], bv[:])
                e.tensor_mul(c.s_t2[:], c.s_t1[:], gb[:])
                e.tensor_copy(s_h[:, TKB * k:TKB * (k + 1)], c.s_t2[:])
                return e.tensor_sub(s_l[:, TKB * k:TKB * (k + 1)], c.s_t2[:],
                                    s_h[:, TKB * k:TKB * (k + 1)])
            p.add("vector", schain, c.sV, 1)
            schain_done[gch] = c.sV.n

        nctx = {}
        all_s = schain_done[tb * NER + NER - 1]
        for j in range(ND):
            gj = tb * ND + j
            slot = gj % 2
            qeng = "sync"
            qsem = c.sWB if j % 2 == 0 else c.sWBg
            if fresh and tb == 0:
                if gj < 2:
                    p.wait(qeng, c.sV, split_w[14 + gj])
                else:
                    p.wait(qeng, c.sV, split_wo[gj - 2])
                p.add(qeng, lambda e, j=j, slot=slot: e.dma_start(
                    out=wost[slot][:].rearrange("p (cc e) -> p cc e", cc=16),
                    in_=c.wo_in[j].rearrange("(cc p) e -> p cc e", p=128)), qsem, 16)
                wb_ld[gj] = (qsem, qsem.n)
                p.wait("vector", *wb_ld[gj])
                if gj >= 2:
                    p.wait("vector", c.sMM, y_consumed[gj - 2])
                    p.wait("vector", c.sWHL, wohl_written[gj - 2])

                def splo(e, slot=slot):
                    wv = wost[slot][:].rearrange("p (cc e) -> p cc e", cc=16)
                    comb = woc[slot][:].rearrange("p (cc x) -> p cc x", cc=16)
                    e.tensor_copy(comb[:, :, 0:128], wv)
                    return e.tensor_sub(comb[:, :, 128:256], wv, comb[:, :, 0:128])
                p.add("vector", splo, c.sV, 1)
                split_wo[gj] = c.sV.n
                p.wait("scalar", c.sV, split_wo[gj])
                p.add("scalar", lambda e, j=j, slot=slot: e.dma_start(
                    out=c.wo_hl[j].rearrange("(cc p) x -> p cc x", p=128),
                    in_=woc[slot][:].rearrange("p (cc x) -> p cc x", cc=16)), c.sWHL, 16)
                wohl_written[gj] = c.sWHL.n
            else:
                if fresh and tb == 1 and j < 2:
                    p.wait(qeng, c.sWHL, wohl_written[ND - 1])
                if gj >= 2 and (gj - 2) in y_consumed:
                    p.wait(qeng, c.sMM, y_consumed[gj - 2])
                p.add(qeng, lambda e, j=j, slot=slot: e.dma_start(
                    out=woc[slot][:].rearrange("p (cc x) -> p cc x", cc=16),
                    in_=c.wo_hl[j].rearrange("(cc p) x -> p cc x", p=128)), qsem, 16)
                wb_ld[gj] = (qsem, qsem.n)

            by = c.pb[j % 2]
            p.wait("tensor", c.sV, all_s)
            p.wait("tensor", *wb_ld[gj])
            if fresh and tb == 0:
                p.wait("tensor", c.sV, split_wo[gj])
            if gj >= 2 and (gj - 2) in yadd_done:
                p.wait("tensor", c.sV, yadd_done[gj - 2])

            def ymm(e, slot=slot, by=by):
                first = True
                ins = None
                for k in range(NER):
                    off = k * 256
                    wh = woc[slot][:, off:off + 128]
                    wl = woc[slot][:, off + 128:off + 256]
                    sh_ = s_h[:, TKB * k:TKB * (k + 1)]
                    sl_ = s_l[:, TKB * k:TKB * (k + 1)]
                    e.matmul(by[:], wh, sh_, start=first, stop=False); first = False
                    e.matmul(by[:], wh, sl_, start=False, stop=False)
                    ins = e.matmul(by[:], wl, sh_, start=False, stop=(k == NER - 1))
                return ins
            p.add("tensor", ymm, c.sMM, 1)
            y_consumed[gj] = c.sMM.n
            if tb + 1 < NTB:
                for s in TE_AFTER.get(j, []):
                    pstage(nctx, s, tb + 1)

            p.wait("vector", c.sMM, y_consumed[gj])

            def yadd(e, j=j, tb=tb, by=by):
                xs = c.xres[j][:, TKB * tb:TKB * (tb + 1)]
                return e.scalar_tensor_tensor(xs, by[:], c.cpool[:, C_HALF:C_HALF + 1],
                                              xs, ALU.mult, ALU.add)
            p.add("vector", yadd, c.sV, 1)
            yadd_done[gj] = c.sV.n
            if tb + 1 < NTB:
                for s in VEC_AFTER.get(j, []):
                    pstage(nctx, s, tb + 1)


def emit_p2(c, st):
    nc, p = c.nc, c.p
    sb = lambda name, shape, dt=F32: st.enter_context(nc.sbuf_tensor(name, shape, dt))
    c.qk_h = sb("qk_h", [128, NT], BF16)
    c.qk_l = sb("qk_l", [128, NT], BF16)
    xvstage = sb("xvstage", [128, 4 * 128], BF16)
    kv_base = c.sKV.n
    c.cc_done = {}

    for tb in range(NTB):
        xln_ready = ln_block(c, tb)
        p.wait("tensor", c.sV, xln_ready)

        def qkmm(e):
            first = True
            ins = None
            for cc in range(ND):
                xh = c.xln_h[:, TKB * cc:TKB * (cc + 1)]
                xl = c.xln_l[:, TKB * cc:TKB * (cc + 1)]
                qh = c.wq_h[:, 64 * cc:64 * (cc + 1)]
                ql = c.wq_l[:, 64 * cc:64 * (cc + 1)]
                kh = c.wkv_h[:, 128 * cc:128 * cc + 64]
                kl = c.wkv_l[:, 128 * cc:128 * cc + 64]
                e.matmul(c.pb[7][0:64, :], qh, xh, start=first, stop=False)
                e.matmul(c.pb[6][0:64, :], kh, xh, start=first, stop=False)
                first = False
                e.matmul(c.pb[7][0:64, :], qh, xl, start=False, stop=False)
                e.matmul(c.pb[6][0:64, :], kh, xl, start=False, stop=False)
                e.matmul(c.pb[7][0:64, :], ql, xh, start=False, stop=(cc == ND - 1))
                ins = e.matmul(c.pb[6][0:64, :], kl, xh, start=False, stop=(cc == ND - 1))
            return ins
        p.add("tensor", qkmm, c.sMM, 1)
        p.wait("vector", c.sMM, c.sMM.n)

        def qk_split(e, tb=tb):
            cols = slice(TKB * tb, TKB * (tb + 1))
            e.tensor_copy(c.qk_h[0:64, cols], c.pb[7][0:64, :])
            e.tensor_copy(c.qk_h[64:128, cols], c.pb[6][0:64, :])
            e.tensor_sub(c.qk_l[0:64, cols], c.pb[7][0:64, :], c.qk_h[0:64, cols])
            return e.tensor_sub(c.qk_l[64:128, cols], c.pb[6][0:64, :], c.qk_h[64:128, cols])
        p.add("vector", qk_split, c.sV, 1)
        qk_done = c.sV.n

        xv_done = {}
        for t in range(4):
            bxv = c.pb[t % 2]

            def xvmm(e, t=t, bxv=bxv):
                first = True
                ins = None
                for cc in range(ND):
                    xh = c.xln_h[:, TKB * cc + 128 * t:TKB * cc + 128 * (t + 1)]
                    xl = c.xln_l[:, TKB * cc + 128 * t:TKB * cc + 128 * (t + 1)]
                    wh = c.wkv_h[:, 128 * cc + 64:128 * cc + 128]
                    wl = c.wkv_l[:, 128 * cc + 64:128 * cc + 128]
                    e.matmul(bxv[:, 0:64], xh, wh, start=first, stop=False); first = False
                    e.matmul(bxv[:, 0:64], xh, wl, start=False, stop=False)
                    ins = e.matmul(bxv[:, 0:64], xl, wh, start=False, stop=(cc == ND - 1))
                return ins
            if t >= 2:
                p.wait("tensor", c.sV, xv_done[t - 2])
            p.add("tensor", xvmm, c.sMM, 1)
            p.wait("vector", c.sMM, c.sMM.n)
            if t == 0 and tb > 0:
                p.wait("vector", c.sKV, kv_base + 16 * 4 * tb)

            def xv_split(e, t=t, bxv=bxv):
                base = 128 * t
                e.tensor_copy(xvstage[:, base:base + 64], bxv[:, 0:64])
                return e.tensor_sub(xvstage[:, base + 64:base + 128], bxv[:, 0:64],
                                    xvstage[:, base:base + 64])
            p.add("vector", xv_split, c.sV, 1)
            xv_done[t] = c.sV.n

        if True:
            p.wait("gpsimd", c.sV, qk_done)
            p.add("gpsimd", lambda e, tb=tb: e.dma_start(
                out=c.cc_in[512 * tb:512 * tb + 256, 0:128]
                    .rearrange("(t p) f -> p t f", p=64),
                in_=c.qk_h[64:128, TKB * tb:TKB * (tb + 1)]
                    .rearrange("p (t f) -> p t f", t=4)), c.sKV, 16)
            p.add("gpsimd", lambda e, tb=tb: e.dma_start(
                out=c.cc_in[512 * tb:512 * tb + 256, 128:256]
                    .rearrange("(t p) f -> p t f", p=64),
                in_=c.qk_l[64:128, TKB * tb:TKB * (tb + 1)]
                    .rearrange("p (t f) -> p t f", t=4)), c.sKV, 16)
            p.wait("gpsimd", c.sV, xv_done[3])
            for half in range(2):
                def xvdma(e, half=half, tb=tb):
                    rows = slice(512 * tb + 256 + 128 * half,
                                 512 * tb + 256 + 128 * (half + 1))
                    return e.dma_start(
                        out=c.cc_in[rows, :].rearrange("p (t f) -> p t f", t=2),
                        in_=xvstage[:, 256 * half:256 * (half + 1)]
                            .rearrange("p (t f) -> p t f", t=2))
                p.add("gpsimd", xvdma, c.sKV, 16)

        p.wait("gpsimd", c.sKV, kv_base + 16 * 4 * (tb + 1))
        p.add("gpsimd", lambda e, tb=tb: e.collective_compute(
            "AllGather", ALU.bypass, replica_groups=[list(range(8))],
            ins=[c.cc_in[512 * tb:512 * (tb + 1), :].opt()],
            outs=[c.cc_out[4096 * tb:4096 * (tb + 1), :].opt()]), c.sCC, 1)
        c.cc_done[tb] = c.sCC.n


def emit_p3(c, st):
    nc, p = c.nc, c.p
    sb = lambda name, shape, dt=F32: st.enter_context(nc.sbuf_tensor(name, shape, dt))
    XVS = 72
    k_h = sb("k_h", [64, 64 * 128], BF16)
    k_l = sb("k_l", [64, 64 * 128], BF16)
    xvh = sb("xvh", [128, 64 * XVS], BF16)
    xvl = sb("xvl", [128, 64 * XVS], BF16)
    ebuf = [sb(f"ebuf{i}", [128, 1024]) for i in range(2)]
    eh = [sb(f"eh{i}", [128, 1024], BF16) for i in range(2)]
    el = [sb(f"el{i}", [128, 1024], BF16) for i in range(2)]
    aacc = sb("aacc", [65, 1024])
    aev = aacc
    c.masks = sb("masks", [128, 16 * 128], BF16)
    anh = sb("anh", [64, 1024], BF16)
    anl = sb("anl", [64, 1024], BF16)

    p.add("sync", lambda e: e.dma_start(
        out=c.masks[:].rearrange("p (k f) -> p k f", k=16),
        in_=c.masks_in.rearrange("k p f -> p k f")), c.sX, 16)
    masks_ld = c.sX.n

    def ones_cols(e):
        e.memset(xvh[:].rearrange("p (j x) -> p j x", j=64)[:, :, 64:72], 0.0)
        e.memset(xvl[:].rearrange("p (j x) -> p j x", j=64)[:, :, 64:72], 0.0)
        e.memset(xvh[:].rearrange("p (j x) -> p j x", j=64)[:, :, 64:65], 1.0)
        return e.memset(xvl[:].rearrange("p (j x) -> p j x", j=64)[:, :, 64:65], 0.0)
    p.add("gpsimd", ones_cols, c.sG, 1)
    ones_done = c.sG.n

    for h in range(2):
        if h == 1:
            p.wait("sync", c.sMM, c.sMM.n)
        owner_ready = {}
        for t2 in range(2):
            tbp = 2 * h + t2
            p.wait("sync", c.sCC, c.cc_done[tbp])
            for r in range(8):
                c0 = r if h == 0 else 7 - r
                for lo in (False, True):
                    def kload(e, r=r, c0=c0, lo=lo, t2=t2, tbp=tbp):
                        dst = k_l if lo else k_h
                        cols = slice(128, 256) if lo else slice(0, 128)
                        src = c.cc_out[4096 * tbp + 512 * r:4096 * tbp + 512 * r + 256,
                                       cols]
                        src3 = src.rearrange("(i p) f -> p i f", p=64)
                        j0 = c0 + 32 * t2
                        dst3 = dst[0:64, :].rearrange("p (j f) -> p j f", f=128)[
                            :, j0:j0 + 25:8, :]
                        return e.dma_start(out=dst3, in_=src3)
                    p.add("sync", kload, c.sX, 16)
                for par in (0, 1):
                    for lo in (False, True):
                        def xvload(e, r=r, c0=c0, par=par, lo=lo, t2=t2, tbp=tbp):
                            dst = xvl if lo else xvh
                            rows0 = 4096 * tbp + 512 * r + 256
                            colh = 128 * par + (64 if lo else 0)
                            src = c.cc_out[rows0:rows0 + 256, colh:colh + 64]
                            src3 = src.rearrange("(b p) f -> p b f", p=128)
                            j0 = 8 * (4 * t2 + par) + c0
                            dst3 = dst[:, XVS * j0:].rearrange("p (j x) -> p j x",
                                                               x=XVS)[:, 0:17:16, 0:64]
                            return e.dma_start(out=dst3, in_=src3)
                        p.add("sync", xvload, c.sX, 16)
                owner_ready[(t2, r)] = c.sX.n

        e_split = {}
        e_done = {}
        a_after = {}
        for j in range(64):
            qlo = 128 * (j // 8)
            fd = 1024 - qlo
            zb = [(c.pb[4], c.pb[5]), (c.pb[6], c.pb[7]), (c.pb[2], c.pb[3])][j % 3]
            segs = segments(qlo, fd)
            r = (j % 8) if h == 0 else 7 - (j % 8)
            p.wait("tensor", c.sX, owner_ready[(j // 32, r)])
            if j == 0:
                p.wait("tensor", c.sG, ones_done)
            if j >= 3:
                p.wait("tensor", c.sV, e_split[j - 3])

            def zmm(e, j=j, qlo=qlo, zb=zb, h=h, segs=segs):
                ins = None
                for pi, (lh, rh) in enumerate((("h", "h"), ("h", "l"), ("l", "h"))):
                    kv = k_h if lh == "h" else k_l
                    qv = c.qk_h if rh == "h" else c.qk_l
                    lhs = kv[0:64, 128 * j:128 * (j + 1)]
                    for (bh, bo, off, seg) in segs:
                        rhs = qv[0:64, 1024 * h + qlo + off:1024 * h + qlo + off + seg]
                        ins = e.matmul(zb[bh][:, bo:bo + seg], lhs, rhs,
                                       start=(pi == 0), stop=(pi == 2))
                return ins
            p.add("tensor", zmm, c.sMM, 1)
            z_done = c.sMM.n

            p.wait("vector", c.sMM, z_done)
            if h == 0 and j == 0:
                p.wait("vector", c.sX, masks_ld)

            def maskadd(e, j=j, qlo=qlo, zb=zb, h=h):
                bh = 0 if qlo < 512 else 1
                bo = qlo % 512
                mk = c.masks[:, 128 * (h * 8 + j % 8):128 * (h * 8 + j % 8 + 1)]
                t = zb[bh][:, bo:bo + 128]
                return e.tensor_add(t, t, mk)
            p.add("vector", maskadd, c.sV, 1)
            mask_done = c.sV.n

            p.wait("scalar", c.sV, mask_done)
            if j >= 2:
                p.wait("scalar", c.sV, e_split[j - 2])

            def eact(e, j=j, zb=zb, segs=segs):
                ins = None
                for (bh, bo, off, seg) in segs:
                    ins = e.activation(ebuf[j % 2][:, off:off + seg],
                                       zb[bh][:, bo:bo + seg], ACT.Exp,
                                       bias=c.cpool[:, C_NCSH:C_NCSH + 1],
                                       scale=c.cpool[:, C_SC:C_SC + 1])
                return ins
            p.add("scalar", eact, c.sC, 1)
            e_done[j] = c.sC.n
            p.wait("vector", c.sC, e_done[j])
            if j >= 2:
                p.wait("vector", c.sMM, a_after[j - 2])

            def esplit(e, j=j, fd=fd):
                hf = fd // 2
                e.tensor_copy(eh[j % 2][:, 0:hf], ebuf[j % 2][:, 0:hf])
                e.tensor_copy(eh[j % 2][:, hf:fd], ebuf[j % 2][:, hf:fd])
                e.tensor_sub(el[j % 2][:, 0:hf], ebuf[j % 2][:, 0:hf],
                             eh[j % 2][:, 0:hf])
                return e.tensor_sub(el[j % 2][:, hf:fd], ebuf[j % 2][:, hf:fd],
                                    eh[j % 2][:, hf:fd])
            p.add("vector", esplit, c.sV, 1)
            e_split[j] = c.sV.n

            p.wait("tensor", c.sV, e_split[j])
            blk_start = (j % 8 == 0)
            blk_end = (j % 8 == 7)
            if blk_start and j > 0:
                p.wait("tensor", c.sV, c.sV.n)

            def amm(e, j=j, qlo=qlo, segs=segs, blk_start=blk_start, blk_end=blk_end):
                ins = None
                for pi, (lh, rh) in enumerate((("h", "h"), ("h", "l"), ("l", "h"))):
                    xv = xvh if lh == "h" else xvl
                    ev = eh[j % 2] if rh == "h" else el[j % 2]
                    lhs = xv[:, XVS * j:XVS * j + 65]
                    for si, (bh, bo, off, seg) in enumerate(segs):
                        ins = e.matmul(c.pb[bh][0:65, bo:bo + seg], lhs,
                                       ev[:, off:off + seg],
                                       start=(blk_start and pi == 0),
                                       stop=(blk_end and pi == 2))
                return ins
            p.add("tensor", amm, c.sMM, 1)
            a_after[j] = c.sMM.n

            if blk_end:
                p.wait("vector", c.sMM, a_after[j])
                first_blk = (j < 8)
                qlo_b = 128 * ((j - 7) // 8)

                def aev_add(e, first_blk=first_blk, qlo_b=qlo_b):
                    if first_blk:
                        e.tensor_copy(aacc[:, 0:512], c.pb[0][0:65, :])
                        return e.tensor_copy(aacc[:, 512:1024], c.pb[1][0:65, :])
                    ins = None
                    for (bh, bo, off, seg) in segments(qlo_b, 1024 - qlo_b):
                        ins = e.tensor_add(aacc[:, qlo_b + off:qlo_b + off + seg],
                                           aacc[:, qlo_b + off:qlo_b + off + seg],
                                           c.pb[bh][0:65, bo:bo + seg])
                    return ins
                p.add("vector", aev_add, c.sV, 1)

        p.add("vector", lambda e: e.reciprocal(c.ln_t1[0:1, :], aev[64:65, 0:512]))
        p.add("vector", lambda e: e.reciprocal(c.ln_t2[0:1, :], aev[64:65, 512:1024]), c.sV, 1)
        p.wait("tensor", c.sV, c.sV.n)
        p.wait("tensor", c.sC, c.sC.n)
        p.add("tensor", lambda e: e.matmul(c.pb[2][0:64, :], c.onesf[:, 0:64],
                                           c.ln_t1[0:1, :], start=True, stop=True))
        p.add("tensor", lambda e: e.matmul(c.pb[3][0:64, :], c.onesf[:, 0:64],
                                           c.ln_t2[0:1, :], start=True, stop=True), c.sT, 1)
        p.wait("vector", c.sT, c.sT.n)

        def an_make(e):
            e.tensor_mul(aev[0:64, 0:512], aev[0:64, 0:512], c.pb[2][0:64, :])
            e.tensor_mul(aev[0:64, 512:1024], aev[0:64, 512:1024], c.pb[3][0:64, :])
            e.tensor_copy(anh[:], aev[0:64, :])
            return e.tensor_sub(anl[:], aev[0:64, :], anh[:])
        p.add("vector", an_make, c.sV, 1)
        an_done = c.sV.n
        p.wait("tensor", c.sV, an_done)

        out_add = {}
        for j in range(ND):
            ob = (c.pb[4], c.pb[5]) if j % 2 == 0 else (c.pb[6], c.pb[7])
            if j >= 2:
                p.wait("tensor", c.sV, out_add[j - 2])

            def omm(e, j=j, ob=ob):
                ins = None
                for pi, (lh, rh) in enumerate((("h", "h"), ("h", "l"), ("l", "h"))):
                    wv = c.wv2_h if lh == "h" else c.wv2_l
                    av = anh if rh == "h" else anl
                    lhs = wv[:, 128 * j:128 * (j + 1)]
                    for seg in range(2):
                        ins = e.matmul(ob[seg][:], lhs, av[:, 512 * seg:512 * (seg + 1)],
                                       start=(pi == 0), stop=(pi == 2))
                return ins
            p.add("tensor", omm, c.sMM, 1)
            og = c.sMM.n
            p.wait("vector", c.sMM, og)

            def oadd(e, j=j, ob=ob, h=h):
                xs0 = c.xres[j][:, 1024 * h:1024 * h + 512]
                xs1 = c.xres[j][:, 1024 * h + 512:1024 * h + 1024]
                e.tensor_add(xs0, xs0, ob[0][:])
                return e.tensor_add(xs1, xs1, ob[1][:])
            p.add("vector", oadd, c.sV, 1)
            out_add[j] = c.sV.n


def emit_p5(c, st, nbuf=2):
    nc, p = c.nc, c.p
    sb = lambda name, shape, dt=F32: st.enter_context(nc.sbuf_tensor(name, shape, dt))
    outstg = [sb(f"outstg{i}", [128, 1024]) for i in range(nbuf)]
    out_base = c.sOUT.n
    p.wait("tensor", c.sV, c.sV.n)
    p.wait("tensor", c.sC, c.sC.n)
    evac = {}
    for t in range(NTILE):
        b0 = c.pb[6] if t % 2 == 0 else c.pb[4]
        b1 = c.pb[7] if t % 2 == 0 else c.pb[5]
        if t >= 2:
            p.wait("tensor", c.sV, evac[t - 2])
        for j in range(ND):
            bnk = b0 if j < 4 else b1

            def tr(e, bnk=bnk, j=j, t=t):
                return e.transpose(bnk[:, 128 * (j % 4):128 * (j % 4 + 1)],
                                   c.xres[j][:, 128 * t:128 * (t + 1)], c.ident[:])
            p.add("tensor", tr, c.sT if j == ND - 1 else None, 1)
        grp = c.sT.n
        p.wait("vector", c.sT, grp)
        if t >= nbuf:
            p.wait("vector", c.sOUT, out_base + 16 * (t - nbuf + 1))

        def ev(e, t=t, b0=b0, b1=b1):
            e.tensor_copy(outstg[t % nbuf][:, 0:512], b0[:])
            return e.tensor_copy(outstg[t % nbuf][:, 512:1024], b1[:])
        p.add("vector", ev, c.sV, 1)
        evac[t] = c.sV.n
        p.wait("scalar", c.sV, evac[t])
        p.add("scalar", lambda e, t=t: e.dma_start(
            out=c.out[128 * t:128 * (t + 1), :], in_=outstg[t % nbuf][:]), c.sOUT, 16)
    p.wait("gpsimd", c.sOUT, out_base + 16 * NTILE)
    for s in (c.sX, c.sWA, c.sWB, c.sWHL, c.sKV, c.sWAg, c.sWBg):
        p.wait("gpsimd", s, s.n)


def host_prep(inputs):
    x = np.asarray(inputs["L0_attn_x"], dtype=np.float32)
    wq = np.ascontiguousarray(np.asarray(inputs["L0_attn_w_q"], dtype=np.float32))
    wk = np.asarray(inputs["L0_attn_w_k"], dtype=np.float32)
    wv1 = np.asarray(inputs["L0_attn_w_v1"], dtype=np.float32)
    wv2 = np.ascontiguousarray(np.asarray(inputs["L0_attn_w_v2"], dtype=np.float32))
    gw = np.ascontiguousarray(np.asarray(inputs["L0_ffn_gate_w"], dtype=np.float32))
    Wu = np.asarray(inputs["L0_ffn_Wu"], dtype=np.float32)
    Wv = np.asarray(inputs["L0_ffn_Wv"], dtype=np.float32)
    Wo = np.asarray(inputs["L0_ffn_Wo"], dtype=np.float32)

    Wu2 = Wu.transpose(1, 0, 2).reshape(1024, 2048)
    Wv2 = Wv.transpose(1, 0, 2).reshape(1024, 2048)
    wuv = np.empty((NER, 2, 1024, 128), dtype=np.float32)
    for k in range(NER):
        wuv[k, 0] = Wu2[:, 128 * k:128 * (k + 1)]
        wuv[k, 1] = Wv2[:, 128 * k:128 * (k + 1)]
    Wo2 = Wo.reshape(2048, 1024)
    wo = np.empty((ND, 2048, 128), dtype=np.float32)
    for j in range(ND):
        wo[j] = Wo2[:, 128 * j:128 * (j + 1)]
    wkv1 = np.ascontiguousarray(np.concatenate([wk, wv1], axis=1))

    tri = np.where(np.arange(128)[None, :] >= np.arange(128)[:, None], 0.0, -1e30
                   ).astype(np.float32)
    in_maps = []
    for core in range(8):
        t0 = [8 * i + core for i in range(8)]
        t1 = [8 * i + (7 - core) for i in range(8)]
        xs = np.concatenate([x[0, 128 * t:128 * (t + 1)] for t in t0]
                            + [x[1, 128 * t:128 * (t + 1)] for t in t1], axis=0)
        masks = np.zeros((16, 128, 128), dtype=np.float32)
        for half in range(2):
            pos = core if half == 0 else 7 - core
            for kk in range(8):
                if kk == pos:
                    masks[half * 8 + kk] = tri
                elif kk > pos:
                    masks[half * 8 + kk] = -1e30
        import ml_dtypes
        in_maps.append({
            "x_in": np.ascontiguousarray(xs),
            "wuv_in": wuv, "wo_in": wo, "wkv1_in": wkv1, "wq_in": wq,
            "wv2_in": wv2, "gw_in": gw,
            "masks_in": masks.astype(ml_dtypes.bfloat16),
        })
    return in_maps


def host_gather(results):
    out = np.empty((2, 8192, 1024), dtype=np.float32)
    for core in range(8):
        r = np.asarray(results[core]["out"])
        for i in range(8):
            t = 8 * i + core
            out[0, 128 * t:128 * (t + 1)] = r[128 * i:128 * (i + 1)]
            t = 8 * i + (7 - core)
            out[1, 128 * t:128 * (t + 1)] = r[1024 + 128 * i:1024 + 128 * (i + 1)]
    return out


def kernel(**inputs):
    in_maps = host_prep(inputs)
    nc = build()
    res = run_bass_kernel_spmd(nc, in_maps, core_ids=list(range(8)))
    return host_gather(res.results)
